# version 08
# baseline (speedup 1.0000x reference)
import contextlib

import numpy as np
import ml_dtypes

import concourse.bass as bass
import concourse.mybir as mybir
from concourse.bass_utils import run_bass_kernel_spmd

F32 = mybir.dt.float32
BF16 = mybir.dt.bfloat16
U8 = mybir.dt.uint8
AF = mybir.ActivationFunctionType
ALU = mybir.AluOpType
AX = mybir.AxisListType
BF = ml_dtypes.bfloat16

CONSTS = (1e-5, -0.5, -1.0, 1.0, 2.0, 0.5, 3.0, 4.0)
DBG = False
CUT = 0

NCORES = 8
T = 2048
TB = 8192
D = 1024
NH = 16
NE = 8
HID = 2816
DI = 2048
DIC = 512
DS = 16

GROUPS = [[0, 1, 2, 3], [4, 5, 6, 7]]

ENGINES = ("sync", "tensor", "vector", "scalar", "gpsimd")


class KB:

    def __init__(self):
        self.nc = bass.Bass("TRN2", target_bir_lowering=False, debug=False)
        self.stack = contextlib.ExitStack()
        self.plan = []
        self.semv = {}
        self._pend = {e: [] for e in ENGINES}
        self._phase = "P0"

    def phase(self, name):
        self._phase = name

    def sem(self, name):
        h = self.stack.enter_context(self.nc.semaphore(name))
        self.semv[h] = 0
        return h

    def sbuf(self, name, shape, dt):
        return self.stack.enter_context(self.nc.sbuf_tensor(name, shape, dt))

    def psum(self, name, shape, dt=F32):
        return self.stack.enter_context(self.nc.psum_tensor(name, shape, dt))

    def op(self, eng, fn, waits=(), inc=None, n=1, selfwait=None):
        w = list(self._pend[eng]) + list(waits)
        self._pend[eng] = []
        if selfwait is None:
            selfwait = eng in ("vector", "scalar", "gpsimd")
        if selfwait and inc is not None and self.semv.get(inc, 0) > 0:
            w.append((inc, self.semv[inc]))
        incs = []
        if inc is not None:
            self.semv[inc] += n
            incs = [(inc, n)]
        self.plan.append((eng, fn, w, incs, self._phase))

    def wait(self, eng, sem, val=None):
        self._pend[eng].append((sem, self.semv[sem] if val is None else val))

    def val(self, s):
        return self.semv[s]

    def fence(self):
        snap = [(s, v) for s, v in self.semv.items() if v > 0]
        for e in ENGINES:
            self._pend[e] = self._pend[e] + list(snap)

    def dma(self, q, out, in_, sem, waits=(), **kw):
        self.op(q, lambda e, o=out, i=in_, kw=kw: e.dma_start(out=o, in_=i, **kw),
                waits, inc=sem, n=16, selfwait=False)

    def cc(self, kind, op_, ins, outs, sem, waits=(), groups=None):
        g = GROUPS if groups is None else groups
        self.op("gpsimd",
                lambda e, k=kind, o=op_, i=ins, u=outs, g=g:
                e.collective_compute(k, o, replica_groups=g,
                                     ins=[i.opt()], outs=[u.opt()]),
                waits, inc=sem, n=1, selfwait=False)

    def op_raw(self, eng, fn, waits=(), incs=None):
        w = list(self._pend[eng]) + list(waits)
        self._pend[eng] = []
        for s, n in (incs or {}).items():
            self.semv[s] += n
        self.plan.append((eng, fn, w, [], self._phase))

    def mm(self, out, lhsT, rhs, start, stop, waits=(), inc=None):
        self.op("tensor",
                lambda e, o=out, l=lhsT, r=rhs, st=start, sp=stop:
                e.matmul(o, l, r, start=st, stop=sp),
                waits, inc=inc)

    def trans(self, out, in_, ident, waits=(), inc=None):
        self.op("tensor",
                lambda e, o=out, i=in_, d=ident: e.transpose(o, i, d),
                waits, inc=inc)

    def act(self, out, in_, func, bias=0.0, scale=1.0, waits=(), inc=None,
            accum_out=None):
        self.op("scalar",
                lambda e, o=out, i=in_, f=func, b=bias, s=scale, a=accum_out:
                e.activation(o, i, f, bias=b, scale=s, accum_out=a),
                waits, inc=inc)

    def v(self, fname, *args, waits=(), inc=None):
        self.op("vector",
                lambda e, fn=fname, a=args: getattr(e, fn)(*a),
                waits, inc=inc)

    def g(self, fname, *args, waits=(), inc=None):
        self.op("gpsimd",
                lambda e, fn=fname, a=args: getattr(e, fn)(*a),
                waits, inc=inc)

    def stt(self, out, in0, scalar, in1, op0, op1, waits=(), inc=None):
        self.op("vector",
                lambda e, o=out, a=in0, s=scalar, b=in1, o0=op0, o1=op1:
                e.scalar_tensor_tensor(o, a, s, b, o0, o1),
                waits, inc=inc)

    def gstt(self, out, in0, scalar, in1, op0, op1, waits=(), inc=None):
        self.op("gpsimd",
                lambda e, o=out, a=in0, s=scalar, b=in1, o0=op0, o1=op1:
                e.scalar_tensor_tensor(o, a, s, b, o0, o1),
                waits, inc=inc)

    def emit(self):
        plan = self.plan
        with self.nc.Block() as block:
            for engname in ENGINES:
                def body(eng, engname=engname):
                    for (en, fn, waits, incs, ph) in plan:
                        if en != engname:
                            continue
                        with self.nc.named_scope(ph):
                            for (s, vv) in waits:
                                eng.wait_ge(s, vv)
                            inst = fn(eng)
                            for (s, a) in incs:
                                assert inst is not None, "inc on op, no inst"
                                inst.then_inc(s, a)
                getattr(block, engname)(body)
        return self.nc


def lay(total, **sizes):
    off, out = 0, {}
    for k, n in sizes.items():
        out[k] = off
        off += n
    assert off <= total, f"layout overflow: {off} > {total}"
    return out


FA_COLS = 26752
BA_COLS = 46080



def emit_layernorm_tm(kb, s_tiles, sq_scr, cols, n_tiles, DW, sems,
                      gain_col=None, bias_col=None):
    sDVE, sACT = sems
    inv = 1.0 / DW
    kb.wait("scalar", sDVE)
    kb.wait("vector", sACT)
    if hasattr(kb, "kcon_ld"):
        kb.wait("scalar", kb.sLd_h, kb.kcon_ld)
    for i in range(n_tiles):
        s = s_tiles(i)
        kb.act(sq_scr, s, AF.Square, accum_out=cols[:, 1:2], inc=sACT)
        kb.v("reduce_sum", cols[:, 0:1], s, AX.X, inc=sDVE)
        kb.wait("vector", sACT)
        kb.v("tensor_scalar_mul", cols[:, 2:3], cols[:, 0:1], inv, inc=sDVE)
        kb.v("tensor_scalar_mul", cols[:, 3:4], cols[:, 1:2], inv, inc=sDVE)
        kb.v("tensor_mul", cols[:, 6:7], cols[:, 2:3], cols[:, 2:3], inc=sDVE)
        kb.v("tensor_sub", cols[:, 3:4], cols[:, 3:4], cols[:, 6:7], inc=sDVE)
        kb.wait("scalar", sDVE)
        kb.act(cols[:, 7:8], cols[:, 3:4], AF.Ln, bias=1e-5, inc=sACT)
        kb.act(cols[:, 4:5], cols[:, 7:8], AF.Exp, scale=-0.5, inc=sACT)
        kb.wait("vector", sACT)
        kb.v("tensor_mul", cols[:, 5:6], cols[:, 2:3], cols[:, 4:5], inc=sDVE)
        kb.v("tensor_scalar_mul", cols[:, 5:6], cols[:, 5:6], -1.0, inc=sDVE)
        kb.wait("scalar", sDVE)
        kb.act(s, s, AF.Identity, scale=cols[:, 4:5], bias=cols[:, 5:6], inc=sACT)
        if gain_col is not None:
            kb.act(s, s, AF.Identity, scale=gain_col, bias=bias_col, inc=sACT)
        kb.wait("vector", sACT)


def emit_tm2cm(kb, src_tile, ba, stage_off, dst_fn, n_tiles, idb, sems):
    sPE, sDVE, sACT = sems
    for i in range(n_tiles):
        kb.wait("vector", sACT)
        kb.v("tensor_copy", ba(stage_off, D), src_tile(i), inc=sDVE)
        kb.wait("tensor", sDVE)
        for ci in range(8):
            kb.trans(kb.pT[:, 0:128], ba(stage_off + ci * 128, 128), idb[:],
                     inc=sPE)
            kb.wait("vector", sPE)
            kb.v("tensor_copy", dst_fn(ci, i), kb.pT[:, 0:128], inc=sDVE)
            kb.wait("tensor", sDVE)



def build_fused():
    kb = KB()
    dram = lambda n, s, dt, k="ExternalInput": kb.nc.dram_tensor(n, s, dt, kind=k).ap()

    xk = dram("xk", [T, D], F32)
    wq = dram("wq", [D, D], F32)
    wk = dram("wk", [D, D], F32)
    wv = dram("wv", [D, D], F32)
    wp = dram("wp", [D, D], F32)
    bq_c = dram("bq_c", [128, 8], F32)
    idf_d = dram("idf", [128, 128], F32)
    ckon_d = dram("ckon", [128, 8], F32)
    bk_r = dram("bk_r", [1, D], F32)
    bv_r = dram("bv_r", [1, D], F32)
    bp_r = dram("bp_r", [1, D], F32)
    id128b = dram("id128b", [128, 128], BF16)
    ones_d = dram("ones_d", [1, 128], F32)
    gwT = dram("gwT", [D, NE], F32)
    eb_r = dram("eb_r", [1, NE], F32)
    w1own = dram("w1own", [D, HID], F32)
    w2own = dram("w2own", [HID, D], F32)
    w3own = dram("w3own", [D, HID], F32)
    ltri2 = dram("ltri2", [8, 16], F32)
    iotas = dram("iotas", [8, 56], F32)
    sel4_d = dram("sel4_d", [8, 512], F32)
    winx = dram("winx", [D, DIC], F32)
    winz = dram("winz", [D, DIC], F32)
    cw = dram("cw", [128, 16], F32)
    cb = dram("cb", [128, 4], F32)
    wxT = dram("wxT", [DI, 544], F32)
    wdtT = dram("wdtT", [DI, DIC], F32)
    bdt_c = dram("bdt_c", [128, 4], F32)
    alog_c = dram("alog_c", [128, 64], F32)
    dcol = dram("dcol", [128, 4], F32)
    woT = dram("woT", [DIC, D], F32)
    lng = dram("lng", [1, D], F32)
    lnb = dram("lnb", [1, D], F32)
    k0t = dram("k0t", [D, 4096], U8)
    k0b = dram("k0b", [D, 4096], U8)
    k1 = dram("k1", [4096, 4096], U8)
    k2 = dram("k2", [4096, D], U8)
    scol = dram("scol", [128, 3], F32)

    WX = 8 * HID
    w13loc = dram("w13loc", [128, 2 * WX], BF16, "Internal")
    w2loc = dram("w2loc", [128, WX], BF16, "Internal")
    w13br_t = kb.nc.dram_tensor("w13br", [NE * 128, 2 * WX], BF16,
                                kind="Internal")
    w2br_t = kb.nc.dram_tensor("w2br", [NE * 128, WX], BF16, kind="Internal")
    w13br, w2br = w13br_t.ap(), w2br_t.ap()
    xsort = dram("xsort", [24 * 256, D], BF16, "Internal")
    osort = dram("osort", [24 * 256, D], F32, "Internal")
    x2f = dram("x2f", [T, D], F32, "Internal")
    zf = dram("zf", [DIC, TB], F32, "Internal")
    uf32 = dram("uf32", [DIC, TB], F32, "Internal")
    bcf = dram("bcf", [32, TB], F32, "Internal")
    x3fh = dram("x3fh", [T, D], F32, "Internal")
    x3bh = dram("x3bh", [D, T], BF16, "Internal")
    w0dh = dram("w0dh", [D, 4096], BF16, "Internal")
    w1dh = dram("w1dh", [4096, 4096], BF16, "Internal")
    w2dh = dram("w2dh", [4096, D], BF16, "Internal")
    kvcc_i = dram("kvcc_i", [128, 512], F32, "Internal")
    kvcc_o = dram("kvcc_o", [128, 512], F32, "Internal")
    x2cc_i = dram("x2cc_i", [D, T], BF16, "Internal")
    x2g = dram("x2g", [4 * D, T], BF16, "Internal")
    ucc_i = dram("ucc_i", [DIC, TB], BF16, "Internal")
    ug = dram("ug", [DI, TB], BF16, "Internal")
    dincc_i = dram("dincc_i", [DIC, TB], BF16, "Internal")
    ding = dram("ding", [DI, TB], BF16, "Internal")
    pocc_i = dram("pocc_i", [TB, D], F32, "Internal")
    pors = dram("pors", [T, D], F32, "Internal")

    xo = dram("xo", [T, D], F32, "ExternalOutput")

    sLd = kb.sem("sLd")
    kb.sLd_h = sLd
    sCast = kb.sem("sCast")
    sPE = kb.sem("sPE")
    sDVE = kb.sem("sDVE")
    sACT = kb.sem("sACT")
    sSt = kb.sem("sSt")
    sC = [kb.sem(f"sC{i}") for i in range(8)]
    sems = (sDVE, sACT)

    bq_cols = kb.sbuf("bq_cols", [128, 8], F32)
    ones1 = kb.sbuf("ones1", [1, 128], F32)
    idb = kb.sbuf("idb", [128, 128], BF16)
    idf = kb.sbuf("idf_sb", [128, 128], F32)
    cols = kb.sbuf("cols", [128, 8], F32)
    kcon = kb.sbuf("kcon", [128, 8], F32)
    cw_sb = kb.sbuf("cw_sb", [128, 16], F32)
    cb_sb = kb.sbuf("cb_sb", [128, 4], F32)
    bdt = kb.sbuf("bdt", [128, 4], F32)
    alog = kb.sbuf("alog", [128, 64], F32)
    dco = kb.sbuf("dco", [128, 4], F32)
    ac = kb.sbuf("ac", [128, 64], F32)
    stc = kb.sbuf("stc", [128, 64], F32)
    brows = kb.sbuf("brows", [1, 2 * 1024], F32)
    sc_sb = kb.sbuf("sc_sb", [128, 4], F32)
    ltri = kb.sbuf("ltri", [8, 16], F32)
    iot = kb.sbuf("iot", [8, 56], F32)
    sidxu = kb.sbuf("sidxu", [128, 32], mybir.dt.uint32)
    rsel = kb.sbuf("rsel", [1, 48], mybir.dt.int32)
    gsc = kb.sbuf("gsc", [128, 32], F32)
    sel4 = kb.sbuf("sel4", [8, 512], F32)

    FA = kb.sbuf("FA", [128, FA_COLS], F32)
    BA = kb.sbuf("BA", [128, BA_COLS], BF16)
    KVB = 41984
    FAB = FA.bitcast(BF16)
    fa = lambda o, n: FA[:, o:o + n]
    ba = lambda o, n: BA[:, o:o + n]
    fab = lambda o, n: FAB[:, o:o + n]

    pA = kb.psum("pA", [128, 512])
    pB = kb.psum("pB", [128, 512])
    pC = kb.psum("pC", [128, 1024])
    pKV = kb.psum("pKV", [128, 512])
    pR = kb.psum("pR", [128, 512])
    pTF = kb.psum("pTF", [128, 128])
    pT = kb.psum("pT", [128, 256], BF16)
    kb.pT = pT

    kb.phase("P0_wconv")
    G8 = [list(range(8))]
    kb.dma("gpsimd", BA[:, 0:WX],
           w1own.rearrange("(c p) j -> p c j", p=128), sCast)
    kb.dma("gpsimd", BA[:, WX:2 * WX],
           w3own.rearrange("(c p) j -> p c j", p=128), sCast)
    kb.wait("gpsimd", sCast)
    kb.dma("gpsimd", w13loc, BA[:, 0:2 * WX], sCast)
    kb.wait("gpsimd", sCast)
    kb.cc("AllGather", ALU.bypass, w13loc, w13br, sC[5], groups=G8)
    kb.dma("gpsimd", BA[:, 0:WX],
           w2own.rearrange("(c p) j -> p c j", p=128), sCast)
    kb.wait("gpsimd", sCast)
    kb.dma("gpsimd", w2loc, BA[:, 0:WX], sCast)
    kb.wait("gpsimd", sCast)
    kb.cc("AllGather", ALU.bypass, w2loc, w2br, sC[6], groups=G8)
    conv_done = kb.val(sCast)

    kb.phase("P1_attn")
    LF = lay(FA_COLS, xr=16 * D, bbc=3 * D, brow=3 * D, sq=D,
             s1=512, s2=512, gb=T, gsm=128)
    LF["gem"] = LF["brow"]
    LF["x1cf"] = LF["s1"]

    kb.dma("sync", kcon[:], ckon_d, sLd)
    for j, cv in enumerate(CONSTS):
        kb.nc.const_aps.register(F32, cv, kcon[:, j:j + 1])
    kb.dma("sync", bq_cols[:], bq_c, sLd)
    kb.dma("sync", FA[0:1, LF["brow"]:LF["brow"] + D], bk_r, sLd)
    kb.dma("sync", FA[0:1, LF["brow"] + D:LF["brow"] + 2 * D], bv_r, sLd)
    kb.dma("sync", FA[0:1, LF["brow"] + 2 * D:LF["brow"] + 3 * D], bp_r, sLd)
    kb.dma("sync", idb[:], id128b, sLd)
    kb.dma("sync", idf[:], idf_d, sLd)
    kb.dma("sync", ones1[:], ones_d, sLd)
    kb.kcon_ld = kb.val(sLd)

    xk_t = xk.rearrange("(n p) d -> n p d", p=128)
    for i in range(16):
        kb.dma("sync", fa(LF["xr"] + i * D, D), xk_t[i], sLd)

    kb.wait("tensor", sLd, kb.kcon_ld)
    for j in range(6):
        kb.mm(pR[:], ones1[:, :],
              FA[0:1, LF["brow"] + j * 512:LF["brow"] + (j + 1) * 512],
              True, True, inc=sPE)
        kb.wait("vector", sPE)
        kb.v("tensor_copy", fa(LF["bbc"] + j * 512, 512), pR[:], inc=sDVE)
        kb.wait("tensor", sDVE)

    LB = lay(BA_COLS, xq=8 * T, wk=8 * D, wv=8 * D, k1=4 * D, vb=4 * D,
             stage=D)
    LQ = {"xq": LB["xq"], "wq": LB["wk"], "q1": LB["wv"]}
    kb.wait("vector", sLd)
    kb.wait("vector", sCast, conv_done)
    emit_tm2cm(kb, lambda i: fa(LF["xr"] + i * D, D), ba, LB["stage"],
               lambda ci, i: ba(LB["xq"] + ci * T + i * 128, 128),
               16, idb, (sPE, sDVE, sACT))

    for wsrc, off in ((wk, LB["wk"]), (wv, LB["wv"])):
        for ci in range(8):
            kb.dma("gpsimd", ba(off + ci * D, D),
                   wsrc[ci * 128:(ci + 1) * 128, :], sCast)
    kb.wait("tensor", sCast)
    kb.wait("tensor", sDVE)
    kb.v("memset", fa(LF["gb"], 512), 0.0, inc=sDVE)

    for g in range(16):
        for coh in range(2):
            for ci in range(8):
                kb.mm(pA[:], ba(LB["xq"] + ci * T + g * 128, 128),
                      ba(LB["wk"] + ci * D + coh * 512, 512),
                      ci == 0, ci == 7, inc=sPE if ci == 7 else None)
            for ci in range(8):
                kb.mm(pB[:], ba(LB["xq"] + ci * T + g * 128, 128),
                      ba(LB["wv"] + ci * D + coh * 512, 512),
                      ci == 0, ci == 7, inc=sPE if ci == 7 else None)
            kb.wait("vector", sPE)
            s1 = fa(LF["s1"], 512)
            s2 = fa(LF["s2"], 512)
            kb.stt(s1, pA[:], 1.0, fa(LF["bbc"] + coh * 512, 512),
                   ALU.mult, ALU.add, inc=sDVE)
            kb.stt(s2, pB[:], 1.0, fa(LF["bbc"] + D + coh * 512, 512),
                   ALU.mult, ALU.add, inc=sDVE)
            kb.wait("tensor", sDVE)
            kb.v("tensor_copy",
                 ba(LB["vb"] + (g % 4) * D + coh * 512, 512), s2, inc=sDVE)
            kb.wait("scalar", sDVE)
            kb.act(fa(LF["sq"], 512), s1, AF.Relu, inc=sACT)
            kb.wait("vector", sACT)
            kb.v("tensor_scalar_min", s1, s1, 0.0, inc=sDVE)
            kb.wait("scalar", sDVE)
            kb.act(s1, s1, AF.Exp, inc=sACT)
            kb.wait("vector", sACT)
            kb.v("tensor_add", s1, s1, fa(LF["sq"], 512), inc=sDVE)
            kb.v("tensor_copy",
                 ba(LB["k1"] + (g % 4) * D + coh * 512, 512), s1, inc=sDVE)
        if g % 4 == 3:
            kb.wait("tensor", sDVE)
            for hp in range(8):
                for hh in range(2):
                    for t4 in range(4):
                        h = hp * 2 + hh
                        kb.mm(pKV[hh * 64:hh * 64 + 64, 0:64],
                              ba(LB["k1"] + t4 * D + h * 64, 64),
                              ba(LB["vb"] + t4 * D + h * 64, 64),
                              t4 == 0, t4 == 3,
                              inc=sPE if (t4 == 3 and hh == 1) else None)
                kb.wait("vector", sPE)
                kb.op("vector", lambda e, o=FA[:, LF["gb"] + hp * 64:LF["gb"] + hp * 64 + 64],
                      a=FA[:, LF["gb"] + hp * 64:LF["gb"] + hp * 64 + 64],
                      b=pKV[:, 0:64]: e.tensor_add(o, a, b),
                      waits=[(sDVE, kb.val(sDVE))], inc=sDVE)
                kb.wait("tensor", sDVE)

    kb.wait("gpsimd", sDVE)
    kb.dma("gpsimd", kvcc_i, fa(LF["gb"], 512), sSt)
    kb.wait("gpsimd", sSt)
    kb.cc("AllReduce", ALU.add, kvcc_i, kvcc_o, sC[0])
    kb.wait("sync", sC[0], 1)
    kb.dma("sync", fa(LF["gb"], 512), kvcc_o, sLd)
    kb.wait("vector", sLd)
    kb.v("memset", BA[:, KVB:KVB + 1024], 0.0, inc=sDVE)
    for h in range(NH):
        prow = (h % 2) * 64
        hp = h // 2
        dro = (h % 2) * 64
        kb.v("tensor_copy",
             BA[dro:dro + 64, KVB + hp * 128 + dro:KVB + hp * 128 + dro + 64],
             FA[prow:prow + 64, LF["gb"] + hp * 64:LF["gb"] + hp * 64 + 64],
             inc=sDVE)

    kv_done = kb.val(sPE)
    for ci in range(8):
        w = [(sPE, kv_done)] if ci == 0 else []
        kb.dma("gpsimd", ba(LQ["wq"] + ci * D, D),
               wq[ci * 128:(ci + 1) * 128, :], sCast, waits=w)
    kb.wait("tensor", sCast)
    for co in range(8):
        for tc in range(4):
            for ci in range(8):
                kb.mm(pA[:], ba(LQ["wq"] + ci * D + co * 128, 128),
                      ba(LQ["xq"] + ci * T + tc * 512, 512),
                      ci == 0, ci == 7, inc=sPE if ci == 7 else None)
            kb.wait("scalar", sPE)
            kb.wait("scalar", sDVE)
            kb.act(fa(LF["s1"], 512), pA[:], AF.Identity,
                   bias=bq_cols[:, co:co + 1], inc=sACT)
            kb.act(fa(LF["s2"], 512), pA[:], AF.Relu,
                   bias=bq_cols[:, co:co + 1], inc=sACT)
            kb.wait("vector", sACT)
            kb.v("tensor_scalar_min", fa(LF["s1"], 512), fa(LF["s1"], 512),
                 0.0, inc=sDVE)
            kb.wait("scalar", sDVE)
            kb.act(fa(LF["s1"], 512), fa(LF["s1"], 512), AF.Exp, inc=sACT)
            kb.wait("vector", sACT)
            kb.v("tensor_add", fa(LF["s1"], 512), fa(LF["s1"], 512),
                 fa(LF["s2"], 512), inc=sDVE)
            kb.v("tensor_copy", ba(LQ["q1"] + co * T + tc * 512, 512),
                 fa(LF["s1"], 512), inc=sDVE)
            kb.wait("tensor", sDVE)

    for ci in range(8):
        w = [(sPE, kb.val(sPE))] if ci == 0 else []
        kb.dma("gpsimd", ba(LQ["wq"] + ci * D, D),
               wp[ci * 128:(ci + 1) * 128, :], sCast, waits=w)
    for hp in range(8):
        for tc in range(4):
            kb.mm(pA[:], BA[:, KVB + hp * 128:KVB + (hp + 1) * 128],
                  ba(LQ["q1"] + hp * T + tc * 512, 512), True, True, inc=sPE)
            kb.wait("vector", sPE)
            kb.v("tensor_copy", ba(LQ["q1"] + hp * T + tc * 512, 512), pA[:],
                 inc=sDVE)
            kb.wait("tensor", sDVE)
    kb.wait("tensor", sCast)
    for tt in range(16):
        for coh in range(2):
            for ci in range(8):
                kb.mm(pC[:, coh * 512:(coh + 1) * 512],
                      ba(LQ["q1"] + ci * T + tt * 128, 128),
                      ba(LQ["wq"] + ci * D + coh * 512, 512),
                      ci == 0, ci == 7, inc=sPE if ci == 7 else None)
        kb.wait("vector", sPE)
        for coh in range(2):
            xo_ = LF["xr"] + tt * D + coh * 512
            kb.stt(fa(LF["s1"], 512), pC[:, coh * 512:(coh + 1) * 512], 1.0,
                   fa(LF["bbc"] + 2 * D + coh * 512, 512), ALU.mult, ALU.add,
                   inc=sDVE)
            kb.v("tensor_add", fa(xo_, 512), fa(xo_, 512), fa(LF["s1"], 512),
                 inc=sDVE)
        kb.wait("tensor", sDVE)
    emit_layernorm_tm(kb, lambda i: fa(LF["xr"] + i * D, D),
                      fa(LF["sq"], D), cols, 16, D, sems)

    kb.phase("P2_moe")
    kb.wait("sync", sLd)
    kb.dma("sync", ltri[:], ltri2, sLd)
    kb.dma("sync", iot[:], iotas, sLd)
    kb.dma("sync", sel4[:], sel4_d, sLd)
    kb.wait("sync", sLd)
    build_moe_g(kb, (gwT, eb_r, w13br_t, w2br_t, xsort, osort, x2f),
                (FA, BA, FAB, LF), (pA, pB, pC, pKV, pR, pTF, pT),
                (sLd, sCast, sPE, sDVE, sACT, sSt, sC),
                (ones1, idf, idb, cols, ltri, iot, sidxu, rsel, gsc,
                 kcon, sel4))
    kb.fence()
    emit_layernorm_tm(kb, lambda i: fa(LF["xr"] + i * D, D),
                      fa(LF["sq"], D), cols, 16, D, sems)

    kb.phase("P3_x2gather")
    L3x = lay(BA_COLS, x2cb=8 * T, stage=D)
    emit_tm2cm(kb, lambda i: fa(LF["xr"] + i * D, D), ba, L3x["stage"],
               lambda ci, i: ba(L3x["x2cb"] + ci * T + i * 128, 128),
               16, idb, (sPE, sDVE, sACT))
    for i in range(16):
        kb.dma("gpsimd", x2f[i * 128:(i + 1) * 128, :],
               fa(LF["xr"] + i * D, D), sSt)
    for ci in range(8):
        kb.dma("gpsimd", x2cc_i[ci * 128:(ci + 1) * 128, :],
               ba(L3x["x2cb"] + ci * T, T), sSt)
    kb.wait("gpsimd", sSt)
    kb.wait("gpsimd", sDVE)
    kb.cc("AllGather", ALU.bypass, x2cc_i, x2g, sC[1])
    kb.fence()

    kb.phase("P4_win_conv")
    LFa = lay(FA_COLS, xs=4 * (T + 3), ua=T, us=T, zs=T)
    LBa = lay(BA_COLS, wx=8 * 512, wz=8 * 512, xc=2 * 8 * T, ub=T)
    kb.dma("sync", cw_sb[:], cw, sLd)
    kb.dma("sync", cb_sb[:], cb, sLd)
    for ci in range(8):
        kb.dma("gpsimd", ba(LBa["wx"] + ci * 512, 512),
               winx[ci * 128:(ci + 1) * 128, :], sCast)
        kb.dma("gpsimd", ba(LBa["wz"] + ci * 512, 512),
               winz[ci * 128:(ci + 1) * 128, :], sCast)
    kb.wait("gpsimd", sCast)
    for dt in range(4):
        kb.v("memset", fa(LFa["xs"] + dt * (T + 3), 3), 0.0, inc=sDVE)

    for ch in range(4):
        par = ch % 2
        xcb = lambda ci: ba(LBa["xc"] + par * 8 * T + ci * T, T)
        w = [(sPE, kb.val(sPE))] if ch >= 2 else [(sC[1], 1)]
        kb.wait("sync", sLd)
        for ci in range(8):
            kb.dma("sync", xcb(ci),
                   x2g[ch * D + ci * 128:ch * D + (ci + 1) * 128, :], sLd,
                   waits=w if ci == 0 else ())
        kb.wait("sync", sLd)
        kb.wait("tensor", sLd)
        if ch == 0:
            kb.wait("tensor", sCast, kb.val(sCast))
        for dt in range(4):
            xso = LFa["xs"] + dt * (T + 3)
            for t4 in range(4):
                for ci in range(8):
                    kb.mm(pA[:], ba(LBa["wx"] + ci * 512 + dt * 128, 128),
                          xcb(ci)[:, t4 * 512:(t4 + 1) * 512],
                          ci == 0, ci == 7, inc=sPE if ci == 7 else None)
                for ci in range(8):
                    kb.mm(pB[:], ba(LBa["wz"] + ci * 512 + dt * 128, 128),
                          xcb(ci)[:, t4 * 512:(t4 + 1) * 512],
                          ci == 0, ci == 7, inc=sPE if ci == 7 else None)
                kb.wait("vector", sPE)
                kb.v("tensor_copy", FA[:, xso + 3 + t4 * 512:xso + 3 + (t4 + 1) * 512],
                     pA[:], inc=sDVE)
                kb.v("tensor_copy", fa(LFa["zs"] + t4 * 512, 512), pB[:],
                     inc=sDVE)
                kb.wait("tensor", sDVE)
            kb.wait("gpsimd", sDVE)
            kb.wait("gpsimd", sSt)
            kb.dma("gpsimd", zf[dt * 128:(dt + 1) * 128, ch * T:(ch + 1) * T],
                   fa(LFa["zs"], T), sSt)
            kb.op("vector", lambda e, o=fa(LFa["ua"], T), i=FA[:, xso:xso + T],
                  s=cw_sb[:, dt * 4:dt * 4 + 1]:
                  e.tensor_scalar_mul(o, i, s), inc=sDVE)
            for j in range(1, 4):
                kb.stt(fa(LFa["ua"], T), FA[:, xso + j:xso + j + T],
                       cw_sb[:, dt * 4 + j:dt * 4 + j + 1], fa(LFa["ua"], T),
                       ALU.mult, ALU.add, inc=sDVE)
            kb.v("tensor_copy", FA[:, xso:xso + 3],
                 FA[:, xso + T:xso + T + 3], inc=sDVE)
            kb.wait("scalar", sDVE)
            kb.wait("scalar", sSt, kb.val(sSt))
            kb.act(fa(LFa["us"], T), fa(LFa["ua"], T), AF.Silu,
                   bias=cb_sb[:, dt:dt + 1], inc=sACT)
            kb.wait("vector", sACT)
            kb.v("tensor_copy", ba(LBa["ub"], T), fa(LFa["us"], T), inc=sDVE)
            kb.wait("gpsimd", sDVE)
            kb.wait("gpsimd", sACT)
            kb.wait("gpsimd", sSt)
            kb.dma("gpsimd", uf32[dt * 128:(dt + 1) * 128, ch * T:(ch + 1) * T],
                   fa(LFa["us"], T), sSt)
            kb.dma("gpsimd", ucc_i[dt * 128:(dt + 1) * 128, ch * T:(ch + 1) * T],
                   ba(LBa["ub"], T), sSt)
            kb.wait("vector", sSt)
    kb.wait("gpsimd", sSt)
    kb.cc("AllGather", ALU.bypass, ucc_i, ug, sC[2])
    kb.fence()

    kb.phase("P5_wx")
    TC2 = 1024
    LBb = lay(BA_COLS, wx=16 * 544, uc=2 * 16 * TC2, st1=512)
    LFb = lay(FA_COLS, st2=512)
    for ci in range(16):
        kb.dma("gpsimd", ba(LBb["wx"] + ci * 544, 544),
               wxT[ci * 128:(ci + 1) * 128, :], sCast)
    kb.wait("gpsimd", sCast)
    for ch in range(8):
        par = ch % 2
        ucb = lambda ci: ba(LBb["uc"] + par * 16 * TC2 + ci * TC2, TC2)
        w = [(sPE, kb.val(sPE))] if ch >= 2 else [(sC[2], 1)]
        kb.wait("sync", sLd)
        for ci in range(16):
            kb.dma("sync", ucb(ci), ug[ci * 128:(ci + 1) * 128,
                                       ch * TC2:(ch + 1) * TC2], sLd,
                   waits=w if ci == 0 else ())
        kb.wait("sync", sLd)
        kb.wait("tensor", sLd)
        if ch == 0:
            kb.wait("tensor", sCast, kb.val(sCast))
        for t2 in range(2):
            for co in range(4):
                for ci in range(16):
                    kb.mm(pA[:], ba(LBb["wx"] + ci * 544 + co * 128, 128),
                          ucb(ci)[:, t2 * 512:(t2 + 1) * 512],
                          ci == 0, ci == 15, inc=sPE if ci == 15 else None)
                kb.wait("vector", sPE)
                kb.wait("vector", sSt, kb.val(sSt))
                kb.v("tensor_copy", ba(LBb["st1"], 512), pA[:], inc=sDVE)
                kb.wait("tensor", sDVE)
                kb.wait("gpsimd", sDVE)
                kb.wait("gpsimd", sSt)
                kb.dma("gpsimd",
                       dincc_i[co * 128:(co + 1) * 128,
                               ch * TC2 + t2 * 512:ch * TC2 + (t2 + 1) * 512],
                       ba(LBb["st1"], 512), sSt)
            for ci in range(16):
                kb.mm(pR[0:32, :], ba(LBb["wx"] + ci * 544 + 512, 32),
                      ucb(ci)[:, t2 * 512:(t2 + 1) * 512],
                      ci == 0, ci == 15, inc=sPE if ci == 15 else None)
            kb.wait("vector", sPE)
            kb.wait("vector", sSt, kb.val(sSt))
            kb.v("tensor_copy", FA[0:32, LFb["st2"]:LFb["st2"] + 512],
                 pR[0:32, :], inc=sDVE)
            kb.wait("tensor", sDVE)
            kb.wait("gpsimd", sDVE)
            kb.wait("gpsimd", sSt)
            kb.dma("gpsimd",
                   bcf[:, ch * TC2 + t2 * 512:ch * TC2 + (t2 + 1) * 512],
                   FA[0:32, LFb["st2"]:LFb["st2"] + 512], sSt)
    kb.wait("gpsimd", sSt)
    kb.cc("AllGather", ALU.bypass, dincc_i, ding, sC[3])
    kb.fence()

    kb.phase("P6_scan")
    TC = 1024
    LFc = lay(FA_COLS, delta=4 * TC, du=4 * TC, yac=4 * TC, uc=4 * TC,
              bb=TC, cbr=TC, dA=2 * TC, dBu=4 * TC, sseq=TC, pst=TC)
    LFc["zc"] = LFc["delta"]
    LFc["s1"] = LFc["pst"]
    LFc["s2"] = LFc["pst"] + 512
    LBc = lay(BA_COLS, dinc=16 * TC, wdt=16 * 512, wo=4 * D, y2b=4 * TC)
    kb.dma("sync", bdt[:], bdt_c, sLd)
    kb.dma("sync", alog[:], alog_c, sLd)
    kb.dma("sync", dco[:], dcol, sLd)
    kb.wait("sync", sLd)
    for ci in range(16):
        kb.dma("gpsimd", ba(LBc["wdt"] + ci * 512, 512),
               wdtT[ci * 128:(ci + 1) * 128, :], sCast)
    for ci in range(4):
        kb.dma("gpsimd", ba(LBc["wo"] + ci * D, D),
               woT[ci * 128:(ci + 1) * 128, :], sCast)
    kb.wait("gpsimd", sCast)
    wld = kb.val(sCast)
    kb.wait("scalar", sLd)
    kb.act(ac[:], alog[:], AF.Exp, inc=sACT)
    kb.wait("vector", sACT)
    kb.v("tensor_scalar_mul", ac[:], ac[:], -1.0, inc=sDVE)
    kb.v("memset", stc[:], 0.0, inc=sDVE)

    scan_done = []
    pool_done = {}
    dve_done = {}
    bcast_mm_done = {}

    for ch in range(8):
        dcb = lambda ci: ba(LBc["dinc"] + ci * TC, TC)
        ucb = lambda dt: fa(LFc["uc"] + dt * TC, TC)
        w = [(sPE, kb.val(sPE)), (sDVE, kb.val(sDVE))] if ch >= 1 else [(sC[3], 1)]
        kb.wait("sync", sLd)
        for ci in range(16):
            kb.dma("sync", dcb(ci), ding[ci * 128:(ci + 1) * 128,
                                         ch * TC:(ch + 1) * TC], sLd,
                   waits=w if ci == 0 else ())
        for dt in range(4):
            kb.dma("sync", ucb(dt), uf32[dt * 128:(dt + 1) * 128,
                                         ch * TC:(ch + 1) * TC], sLd)
        kb.wait("sync", sLd)
        ld_ch = kb.val(sLd)
        kb.wait("tensor", sLd, ld_ch)
        if ch == 0:
            kb.wait("tensor", sCast, wld)
        kb.wait("scalar", sSt, kb.val(sSt))
        for dt in range(4):
            for t2 in range(2):
                pp = (pA, pB)[t2]
                for ci in range(16):
                    kb.mm(pp[:], ba(LBc["wdt"] + ci * 512 + dt * 128, 128),
                          dcb(ci)[:, t2 * 512:(t2 + 1) * 512],
                          ci == 0, ci == 15, inc=sPE if ci == 15 else None)
                kb.wait("scalar", sPE)
                kb.wait("scalar", sDVE)
                do = LFc["delta"] + dt * TC + t2 * 512
                kb.act(fa(LFc["s1"], 512), pp[:], AF.Abs,
                       bias=bdt[:, dt:dt + 1], inc=sACT)
                kb.act(fa(LFc["s1"], 512), fa(LFc["s1"], 512), AF.Exp,
                       scale=-1.0, inc=sACT)
                kb.act(fa(LFc["s1"], 512), fa(LFc["s1"], 512), AF.Ln,
                       bias=1.0, inc=sACT)
                kb.act(fa(LFc["s2"], 512), pp[:], AF.Relu,
                       bias=bdt[:, dt:dt + 1], inc=sACT)
                kb.wait("vector", sACT)
                kb.v("tensor_add", fa(do, 512), fa(LFc["s1"], 512),
                     fa(LFc["s2"], 512), inc=sDVE)
                kb.wait("tensor", sDVE)
            kb.wait("vector", sLd, ld_ch)
            kb.v("tensor_mul", fa(LFc["du"] + dt * TC, TC),
                 fa(LFc["delta"] + dt * TC, TC), ucb(dt), inc=sDVE)
        kb.v("memset", fa(LFc["yac"], 4 * TC), 0.0, inc=sDVE)
        du_done = kb.val(sDVE)
        for n in range(DS):
            np_ = n % 2
            pn = n - 1
            prev = (ch, pn) if pn >= 0 else ((ch - 1, DS + pn) if ch >= 1
                                             else None)
            wbr = []
            if prev in pool_done:
                wbr.append((sCast, pool_done[prev]))
            if prev in dve_done:
                wbr.append((sDVE, dve_done[prev]))
            kb.wait("sync", sLd)
            kb.dma("sync", fa(LFc["bb"], TC),
                   bcf[n:n + 1, ch * TC:(ch + 1) * TC].broadcast_to((128, TC)),
                   sLd, waits=wbr)
            kb.dma("sync", fa(LFc["cbr"], TC),
                   bcf[16 + n:17 + n, ch * TC:(ch + 1) * TC].broadcast_to((128, TC)),
                   sLd)
            kb.wait("sync", sLd)
            bcast_done_v = kb.val(sLd)
            for dtp in (0, 2):
                kbase = len(scan_done)
                for q in (0, 1):
                    dt = dtp + q
                    k = kbase + q
                    p = k % 4
                    p2 = k % 2
                    gva = max(scan_done[k - 2] if k >= 2 else 0, du_done)
                    gvb = max(scan_done[k - 4] if k >= 4 else 0, du_done)
                    kb.wait("scalar", sDVE, gva)
                    kb.wait("gpsimd", sDVE, gvb)
                    if dtp == 0 and q == 0:
                        kb.wait("gpsimd", sLd, bcast_done_v)
                        kb.wait("vector", sLd, bcast_done_v)
                    kb.act(fa(LFc["dA"] + p2 * TC, TC),
                           fa(LFc["delta"] + dt * TC, TC), AF.Exp,
                           scale=ac[:, dt * 16 + n:dt * 16 + n + 1], inc=sACT)
                    kb.g("tensor_mul", fa(LFc["dBu"] + p * TC, TC),
                         fa(LFc["du"] + dt * TC, TC),
                         fa(LFc["bb"], TC), inc=sCast)
                act_pair = kb.val(sACT)
                pool_pair = kb.val(sCast)
                for q in (0, 1):
                    dt = dtp + q
                    k = kbase + q
                    p = k % 4
                    kb.wait("vector", sACT, act_pair - 1 + q)
                    kb.wait("vector", sCast, pool_pair - 1 + q)
                    p2 = k % 2
                    kb.op("vector", lambda e, o=fa(LFc["sseq"], TC),
                          a=fa(LFc["dA"] + p2 * TC, TC),
                          b=fa(LFc["dBu"] + p * TC, TC),
                          i=stc[:, dt * 16 + n:dt * 16 + n + 1]:
                          e.tensor_tensor_scan(o, a, b, i, ALU.mult, ALU.add),
                          inc=sDVE)
                    scan_v = kb.val(sDVE)
                    kb.v("tensor_copy", stc[:, dt * 16 + n:dt * 16 + n + 1],
                         fa(LFc["sseq"] + TC - 1, 1), inc=sDVE)
                    kb.wait("gpsimd", sDVE, scan_v)
                    kb.g("tensor_mul", fa(LFc["dBu"] + p * TC, TC),
                         fa(LFc["sseq"], TC), fa(LFc["cbr"], TC),
                         inc=sCast)
                    mul_v = kb.val(sCast)
                    kb.wait("vector", sCast, mul_v)
                    kb.v("tensor_add", fa(LFc["yac"] + dt * TC, TC),
                         fa(LFc["yac"] + dt * TC, TC),
                         fa(LFc["dBu"] + p * TC, TC), inc=sDVE)
                    scan_done.append(kb.val(sDVE))
            pool_done[(ch, n)] = kb.val(sCast)
            dve_done[(ch, n)] = kb.val(sDVE)
        kb.wait("sync", sACT)
        kb.wait("sync", sDVE)
        for dt in range(4):
            kb.dma("sync", fa(LFc["zc"] + dt * TC, TC),
                   zf[dt * 128:(dt + 1) * 128, ch * TC:(ch + 1) * TC], sLd)
        kb.wait("sync", sLd)
        kb.wait("scalar", sLd)
        kb.wait("scalar", sDVE)
        kb.wait("vector", sPE)
        for dt in range(4):
            ysl = fa(LFc["yac"] + dt * TC, TC)
            zsl = fa(LFc["zc"] + dt * TC, TC)
            kb.stt(ysl, ucb(dt), dco[:, dt:dt + 1], ysl,
                   ALU.mult, ALU.add, inc=sDVE)
            kb.wait("scalar", sDVE)
            kb.act(zsl, zsl, AF.Silu, inc=sACT)
            kb.wait("vector", sACT)
            kb.v("tensor_mul", ysl, ysl, zsl, inc=sDVE)
            kb.v("tensor_copy", ba(LBc["y2b"] + dt * TC, TC), ysl, inc=sDVE)
        kb.wait("tensor", sDVE)
        kb.wait("tensor", sACT)
        for tt in range(8):
            for coh in range(2):
                pp = pC[:, coh * 512:coh * 512 + 512]
                for dt in range(4):
                    kb.mm(pp, ba(LBc["y2b"] + dt * TC + tt * 128, 128),
                          ba(LBc["wo"] + dt * D + coh * 512, 512),
                          dt == 0, dt == 3,
                          inc=sPE if dt == 3 else None)
            kb.wait("vector", sPE)
            kb.wait("vector", sSt, kb.val(sSt))
            kb.v("tensor_copy", fa(LFc["pst"], 512), pC[:, 0:512], inc=sDVE)
            kb.v("tensor_copy", fa(LFc["pst"] + 512, 512), pC[:, 512:1024],
                 inc=sDVE)
            kb.wait("tensor", sDVE)
            kb.wait("scalar", sDVE)
            kb.dma("scalar",
                   pocc_i[ch * TC + tt * 128:ch * TC + (tt + 1) * 128, :],
                   fa(LFc["pst"], TC), sSt)
    kb.wait("gpsimd", sSt)
    kb.cc("ReduceScatter", ALU.add, pocc_i, pors, sC[4])
    kb.fence()

    kb.phase("P7_lns")
    LF7 = lay(FA_COLS, xr=16 * D, gbc=2 * D, pl=2 * D, sq=D, s1=512, s2=512,
              xt=D)
    LB7 = lay(BA_COLS, x3h=8 * 1024, stage=D)
    lr = FA[0:1, LF7["pl"]:LF7["pl"] + 2 * D]
    kb.dma("sync", lr[:, 0:D], lng, sLd)
    kb.dma("sync", lr[:, D:2 * D], lnb, sLd)
    kb.dma("sync", sc_sb[:, 0:3], scol, sLd)
    kb.wait("sync", sLd)
    kb.wait("vector", sLd)
    kb.v("tensor_scalar_mul", sc_sb[:, 0:2], sc_sb[:, 0:2], 2.0, inc=sDVE)
    kb.wait("tensor", sLd)
    kb.wait("tensor", sDVE)
    for j in range(4):
        kb.mm(pR[:], ones1[:, :], lr[:, j * 512:(j + 1) * 512],
              True, True, inc=sPE)
        kb.wait("vector", sPE)
        kb.v("tensor_copy", fa(LF7["gbc"] + j * 512, 512), pR[:], inc=sDVE)
        kb.wait("tensor", sDVE)

    for i in range(16):
        kb.wait("sync", sLd)
        kb.wait("sync", sDVE)
        kb.dma("sync", fa(LF7["pl"], D), pors[i * 128:(i + 1) * 128, :], sLd,
               waits=[(sC[4], 1)] if i == 0 else ())
        kb.dma("sync", fa(LF7["xr"] + i * D, D), x2f[i * 128:(i + 1) * 128, :],
               sLd)
        kb.wait("sync", sLd)
        kb.wait("vector", sLd)
        m = fa(LF7["pl"], D)
        kb.v("tensor_add", m, m, fa(LF7["xr"] + i * D, D), inc=sDVE)
        emit_layernorm_tm(kb, lambda _i: m, fa(LF7["sq"], D), cols, 1, D, sems)
        kb.wait("vector", sACT)
        kb.v("tensor_mul", m, m, fa(LF7["gbc"], D), inc=sDVE)
        kb.v("tensor_add", m, m, fa(LF7["gbc"] + D, D), inc=sDVE)
        kb.v("tensor_add", m, fa(LF7["xr"] + i * D, D), m, inc=sDVE)
        emit_layernorm_tm(kb, lambda _i: m, fa(LF7["sq"], D), cols, 1, D, sems)
        kb.wait("gpsimd", sDVE)
        kb.wait("gpsimd", sACT)
        kb.wait("gpsimd", sSt)
        kb.dma("gpsimd", x3fh[i * 128:(i + 1) * 128, :], m, sSt)
        kb.wait("vector", sACT)
        kb.v("tensor_copy", fa(LF7["xr"] + i * D, D), m, inc=sDVE)
        kb.v("tensor_copy", ba(LB7["stage"], D), m, inc=sDVE)
        kb.wait("tensor", sDVE)
        for ci in range(8):
            kb.trans(pT[:, 0:128], ba(LB7["stage"] + ci * 128, 128), idb[:],
                     inc=sPE)
            kb.wait("vector", sPE)
            kb.v("tensor_copy",
                 ba(LB7["x3h"] + ci * 1024 + (i % 8) * 128, 128), pT[:, 0:128],
                 inc=sDVE)
            kb.wait("tensor", sDVE)
        if i % 8 == 7:
            half = i // 8
            kb.wait("gpsimd", sDVE)
            kb.wait("gpsimd", sSt)
            for ci in range(8):
                kb.dma("gpsimd", x3bh[ci * 128:(ci + 1) * 128,
                                      half * 1024:(half + 1) * 1024],
                       ba(LB7["x3h"] + ci * 1024, 1024), sSt)
            kb.wait("gpsimd", sSt)
            kb.wait("vector", sSt)
    kb.fence()

    kb.phase("P8_ffn")
    LB8 = lay(BA_COLS, h2=32 * 1024, wst=13312)
    F8_h1 = 8192
    F8_x3 = 40960
    LF8 = lay(4096, sq=1024, s1=512, s2=512, xt=1024)
    h1 = lambda o, n: FAB[:, F8_h1 + o:F8_h1 + o + n]
    x3c = lambda o, n: FAB[:, F8_x3 + o:F8_x3 + o + n]

    def build_w(src_u8, dst, rows, ncols, second=None):
        for r in range(rows // 128):
            kb.wait("gpsimd", sSt)
            kb.wait("gpsimd", sCast)
            kb.dma("gpsimd", ba(LB8["wst"], ncols),
                   src_u8[r * 128:(r + 1) * 128, :], sSt)
            if second is not None:
                kb.dma("gpsimd", ba(LB8["h2"], ncols),
                       second[r * 128:(r + 1) * 128, :], sSt)
            kb.wait("gpsimd", sSt)
            if second is None:
                kb.g("tensor_scalar", ba(LB8["wst"], ncols),
                     ba(LB8["wst"], ncols), 2.0, -1.0, ALU.mult, ALU.add,
                     inc=sCast)
            else:
                kb.g("tensor_sub", ba(LB8["wst"], ncols), ba(LB8["wst"], ncols),
                     ba(LB8["h2"], ncols), inc=sCast)
                kb.g("tensor_scalar_mul", ba(LB8["wst"], ncols),
                     ba(LB8["wst"], ncols), 2.0, inc=sCast)
            kb.wait("gpsimd", sCast)
            kb.dma("gpsimd", dst[r * 128:(r + 1) * 128, :],
                   ba(LB8["wst"], ncols), sSt)

    kb.phase("P8a_buildw")
    build_w(k0t, w0dh, D, 4096, second=k0b)
    build_w(k1, w1dh, 4096, 4096)
    build_w(k2, w2dh, 4096, D)
    kb.phase("P8_ffn")
    kb.fence()

    HT = 1024
    psl = [pA[:], pB[:], pC[:, 0:512], pC[:, 512:1024], pKV[:], pR[:]]

    for half in range(2):
        h0 = half * HT
        kb.wait("sync", sSt, kb.val(sSt))
        kb.wait("sync", sPE)
        for ci in range(8):
            kb.dma("sync", x3c(ci * HT, HT),
                   x3bh[ci * 128:(ci + 1) * 128, h0:h0 + HT], sLd)
        kb.wait("sync", sLd)
        kb.wait("tensor", sLd)

        co = 0
        while co < 32:
            ng = min(3, 32 - co)
            kb.wait("sync", sPE)
            kb.wait("sync", sLd)
            for ci in range(8):
                kb.dma("sync", ba(LB8["wst"] + ci * ng * 128, ng * 128),
                       w0dh[ci * 128:(ci + 1) * 128, co * 128:(co + ng) * 128],
                       sLd)
            kb.wait("sync", sLd)
            kb.wait("tensor", sLd)
            for ci in range(8):
                for g in range(ng):
                    for t2 in range(2):
                        kb.mm(psl[g * 2 + t2],
                              ba(LB8["wst"] + ci * ng * 128 + g * 128, 128),
                              x3c(ci * HT + t2 * 512, 512),
                              ci == 0, ci == 7,
                              inc=sPE if (ci == 7 and g == ng - 1 and t2 == 1)
                              else None)
            kb.wait("scalar", sPE)
            kb.wait("scalar", sDVE)
            for g in range(ng):
                for t2 in range(2):
                    kb.wait("scalar", sDVE)
                    kb.act(FA[:, LF8["s1"]:LF8["s1"] + 512],
                           psl[g * 2 + t2],
                           AF.Relu, scale=sc_sb[:, 0:1], inc=sACT)
                    kb.wait("vector", sACT)
                    kb.v("tensor_copy",
                         h1((co + g) * HT + t2 * 512, 512),
                         FA[:, LF8["s1"]:LF8["s1"] + 512], inc=sDVE)
            kb.wait("tensor", sDVE)
            co += ng

        co = 0
        while co < 32:
            ng = min(3, 32 - co)
            kb.wait("sync", sPE)
            kb.wait("sync", sLd)
            for ci in range(32):
                kb.dma("sync",
                       ba(LB8["wst"] + ci * ng * 128, ng * 128),
                       w1dh[ci * 128:(ci + 1) * 128, co * 128:(co + ng) * 128],
                       sLd)
            kb.wait("sync", sLd)
            kb.wait("tensor", sLd)
            for ci in range(32):
                for g in range(ng):
                    for t2 in range(2):
                        kb.mm(psl[g * 2 + t2],
                              ba(LB8["wst"] + ci * ng * 128 + g * 128, 128),
                              h1(ci * HT + t2 * 512, 512),
                              ci == 0, ci == 31,
                              inc=sPE if (ci == 31 and g == ng - 1 and t2 == 1)
                              else None)
            kb.wait("scalar", sPE)
            kb.wait("scalar", sDVE)
            for g in range(ng):
                for t2 in range(2):
                    kb.wait("scalar", sDVE)
                    kb.act(FA[:, LF8["s1"]:LF8["s1"] + 512],
                           psl[g * 2 + t2],
                           AF.Relu, scale=sc_sb[:, 1:2], inc=sACT)
                    kb.wait("vector", sACT)
                    kb.v("tensor_copy",
                         ba(LB8["h2"] + (co + g) * HT + t2 * 512, 512),
                         FA[:, LF8["s1"]:LF8["s1"] + 512], inc=sDVE)
            kb.wait("tensor", sDVE)
            co += ng

        for ttg in range(0, 8, 3):
            ntt = min(3, 8 - ttg)
            for cig in range(0, 32, 11):
                nci = min(11, 32 - cig)
                kb.wait("sync", sPE)
                kb.wait("sync", sLd)
                for cl in range(nci):
                    kb.dma("sync", ba(LB8["wst"] + cl * D, D),
                           w2dh[(cig + cl) * 128:(cig + cl + 1) * 128, :], sLd)
                kb.wait("sync", sLd)
                kb.wait("tensor", sLd)
                for cl in range(nci):
                    ci = cig + cl
                    for tt in range(ntt):
                        for coh in range(2):
                            kb.mm(psl[tt * 2 + coh],
                                  ba(LB8["h2"] + ci * HT + (ttg + tt) * 128, 128),
                                  ba(LB8["wst"] + cl * D + coh * 512, 512),
                                  ci == 0, ci == 31,
                                  inc=sPE if (cl == nci - 1 and tt == ntt - 1
                                              and coh == 1) else None)
            kb.wait("scalar", sPE)
            for tt in range(ntt):
                gi = half * 8 + ttg + tt
                kb.wait("sync", sDVE)
                kb.wait("sync", sSt, kb.val(sSt))
                kb.dma("sync", FA[:, LF8["xt"]:LF8["xt"] + D],
                       x3fh[gi * 128:(gi + 1) * 128, :], sLd)
                kb.wait("sync", sLd)
                kb.wait("scalar", sDVE)
                for coh in range(2):
                    kb.act(FA[:, (LF8["s1"] if coh == 0 else LF8["s2"]):
                              (LF8["s1"] if coh == 0 else LF8["s2"]) + 512],
                           psl[tt * 2 + coh],
                           AF.Identity, scale=sc_sb[:, 2:3], inc=sACT)
                kb.wait("vector", sACT)
                kb.wait("vector", sLd)
                kb.v("tensor_add", FA[:, LF8["xt"]:LF8["xt"] + 512],
                     FA[:, LF8["xt"]:LF8["xt"] + 512],
                     FA[:, LF8["s1"]:LF8["s1"] + 512], inc=sDVE)
                kb.op("vector", lambda e, o=FA[:, LF8["xt"] + 512:LF8["xt"] + 1024],
                      a=FA[:, LF8["xt"] + 512:LF8["xt"] + 1024],
                      b=FA[:, LF8["s2"]:LF8["s2"] + 512]:
                      e.tensor_add(o, a, b), inc=sDVE)
                emit_layernorm_tm(kb, lambda _i: FA[:, LF8["xt"]:LF8["xt"] + D],
                                  FA[:, LF8["sq"]:LF8["sq"] + D], cols, 1, D,
                                  sems)
                kb.wait("gpsimd", sACT)
                kb.wait("gpsimd", sSt)
                kb.dma("gpsimd", xo[gi * 128:(gi + 1) * 128, :],
                       FA[:, LF8["xt"]:LF8["xt"] + D], sSt)
            kb.wait("tensor", sDVE)
            kb.wait("tensor", sACT)
    kb.op("gpsimd", lambda e: e.memset(cols[0:1, 0:1], 0.0),
          waits=[(sSt, kb.val(sSt))], selfwait=False)

    kb.emit()
    return kb


def build_moe_g(kb, drams, mem, psums, sems, consts):
    (gwT, eb_r, w13br_t, w2br_t, xsort, osort, x2f) = drams
    WX = 8 * HID
    FA, BA, FAB, LF = mem
    pA, pB, pC, pKV, pR, pTF, pT = psums
    sLd, sCast, sPE, sDVE, sACT, sSt, sC = sems
    (ones1, idf, idb, cols, ltri, iot, sidxu, rsel, gsc, kcon,
     sel4) = consts
    fa = lambda o, n: FA[:, o:o + n]

    GS = LF["gsm"]
    GB, BBC, SQ2, GEM, XR = (LF["gb"], LF["bbc"], LF["sq"], LF["gem"],
                             LF["xr"])
    gcol = FA[:, GS:GS + 8]
    key = FA[:, GS + 8:GS + 16]
    msk = FA[:, GS + 16:GS + 24]
    ebb = FA[:, GS + 24:GS + 32]
    neg = FA[:, GS + 32:GS + 40]
    gtm = FA[:, GS + 40:GS + 48]
    ebr = FA[0:1, GS + 48:GS + 56]

    gw_f = FA[:, GS + 64:GS + 64 + 64]
    kb.wait("sync", sLd)
    kb.dma("sync", ebr, eb_r, sLd)
    for ci in range(8):
        kb.dma("sync", FA[:, GS + 64 + ci * NE:GS + 64 + (ci + 1) * NE],
               gwT[ci * 128:(ci + 1) * 128, :], sLd)
    kb.wait("sync", sLd)
    kb.v("memset", neg[:], -1e30, inc=sDVE)
    kb.wait("tensor", sLd)
    kb.wait("tensor", sDVE)
    kb.mm(pR[:, 0:8], ones1[:, :], ebr, True, True, inc=sPE)
    kb.wait("vector", sPE)
    kb.v("tensor_copy", ebb[:], pR[:, 0:8], inc=sDVE)
    kb.wait("tensor", sDVE)
    for tt in range(16):
        for ci in range(8):
            kb.trans(pTF[:], FA[:, XR + tt * D + ci * 128:XR + tt * D + ci * 128 + 128],
                     idf[:], inc=sPE)
            kb.wait("vector", sPE)
            kb.op("vector", lambda e, o=FA[:, LF["x1cf"] + ci * 128:LF["x1cf"] + (ci + 1) * 128],
                  p=pTF[:]: e.tensor_copy(o, p), inc=sDVE)
            kb.wait("tensor", sDVE)
        for ci in range(8):
            kb.mm(pR[:, 0:8],
                  FA[:, LF["x1cf"] + ci * 128:LF["x1cf"] + (ci + 1) * 128],
                  gw_f[:, ci * NE:(ci + 1) * NE],
                  ci == 0, ci == 7, inc=sPE if ci == 7 else None)
        kb.wait("vector", sPE)
        kb.op("vector", lambda e, o=gcol[:, 0:1], p=pR[:, 0:8]:
              e.reduce_max(o, p, AX.X), inc=sDVE)
        kb.v("tensor_scalar_mul", gcol[:, 1:2], gcol[:, 0:1], -1.0, inc=sDVE)
        kb.wait("scalar", sDVE)
        kb.op("scalar", lambda e, o=gtm[:], p=pR[:, 0:8], b=gcol[:, 1:2]:
              e.activation(o, p, AF.Exp, bias=b), inc=sACT)
        kb.wait("vector", sACT)
        kb.wait("tensor", sACT)
        kb.op("vector", lambda e, o=gcol[:, 2:3], i=gtm[:]:
              e.reduce_sum(o, i, AX.X), inc=sDVE)
        kb.v("reciprocal", gcol[:, 3:4], gcol[:, 2:3], inc=sDVE)
        kb.op("vector", lambda e, o=gtm[:], i=gtm[:], m=gcol[:, 3:4]:
              e.tensor_scalar_mul(o, i, m), inc=sDVE)
        kb.v("tensor_add", key[:], gtm[:], ebb[:], inc=sDVE)
        kb.op("vector", lambda e, o=gcol[:, 4:5], i=key[:]:
              e.reduce_max(o, i, AX.X), inc=sDVE)
        kb.op("vector", lambda e, o=msk[:], i=key[:], m=gcol[:, 4:5]:
              e.tensor_single_scalar(o, i, m, ALU.is_equal), inc=sDVE)
        kb.v("tensor_scalar_mul", msk[:], msk[:], -1e30, inc=sDVE)
        kb.v("tensor_add", msk[:], msk[:], key[:], inc=sDVE)
        kb.op("vector", lambda e, o=gcol[:, 5:6], i=msk[:]:
              e.reduce_max(o, i, AX.X), inc=sDVE)
        kb.op("vector", lambda e, o=key[:], i=key[:], m=gcol[:, 5:6]:
              e.tensor_single_scalar(o, i, m, ALU.is_ge), inc=sDVE)
        kb.v("tensor_mul", gtm[:], gtm[:], key[:], inc=sDVE)
        kb.wait("tensor", sDVE)
        kb.trans(pTF[0:8, :], gtm[:], idf[:], inc=sPE)
        kb.wait("vector", sPE)
        kb.op("vector",
              lambda e, o=FA[0:8, GEM + tt * 128:GEM + (tt + 1) * 128],
              p=pTF[0:8, :]: e.tensor_copy(o, p), inc=sDVE)
        kb.wait("tensor", sDVE)

    if CUT == 1:
        return
    kb.wait("vector", sACT)
    kb.v("memset", BA[:, 0:1024], 0.0, inc=sDVE)
    kb.wait("scalar", sDVE)
    for j in range(48):
        kb.dma("scalar", xsort[j * 128:(j + 1) * 128, :], BA[:, 0:1024], sSt)
    kb.wait("scalar", sACT)
    for i in range(16):
        kb.dma("scalar", x2f[i * 128:(i + 1) * 128, :],
               fa(XR + i * D, D), sSt)
    spill_done = kb.val(sSt)

    HT2 = 1024
    cnt = FA[0:8, GS:GS + 1]
    pc_ = FA[0:8, GS + 1:GS + 2]
    off = FA[0:8, GS + 2:GS + 3]
    om1 = FA[0:8, GS + 3:GS + 4]
    o81 = FA[0:8, GS + 4:GS + 5]
    carry = FA[0:8, GS + 5:GS + 6]
    kb.v("memset", o81, 1.0, inc=sDVE)
    kb.v("memset", FA[0:8, GB + HT2:GB + 2 * HT2], 1.0, inc=sDVE)
    for h in range(2):
        gemH = FA[0:8, GEM + h * HT2:GEM + (h + 1) * HT2]
        posH = FA[0:8, BBC + h * HT2:BBC + (h + 1) * HT2]
        mH = FA[0:8, GB:GB + HT2]
        kb.v("tensor_single_scalar", mH, gemH, 0.0, ALU.is_gt, inc=sDVE)
        init = 0.0 if h == 0 else carry
        kb.op("vector", lambda e, o=posH, a=FA[0:8, GB + HT2:GB + 2 * HT2],
              b=mH, i=init:
              e.tensor_tensor_scan(o, a, b, i, ALU.mult, ALU.add), inc=sDVE)
        if h == 0:
            kb.v("tensor_copy", carry, FA[0:8, BBC + HT2 - 1:BBC + HT2],
                 inc=sDVE)
    kb.v("tensor_copy", cnt, FA[0:8, BBC + 2 * HT2 - 1:BBC + 2 * HT2],
         inc=sDVE)
    kb.v("tensor_single_scalar", FA[0:8, GS + 8:GS + 16], iot[0:8, 0:8],
         cnt, ALU.is_lt, inc=sDVE)
    kb.v("reduce_sum", pc_, FA[0:8, GS + 8:GS + 16], AX.X, inc=sDVE)
    kb.v("tensor_scalar_mul", pc_, pc_, 256.0, inc=sDVE)
    kb.wait("tensor", sDVE)
    kb.wait("tensor", sACT)
    kb.mm(pR[0:8, 0:1], ltri[0:8, 0:8], pc_, True, True, inc=sPE)
    kb.wait("vector", sPE)
    kb.v("tensor_copy", off, pR[0:8, 0:1], inc=sDVE)
    kb.v("tensor_scalar_sub", om1, off, 1.0, inc=sDVE)
    kb.v("tensor_single_scalar", FA[0:8, BBC:BBC + 2 * HT2],
         FA[0:8, BBC:BBC + 2 * HT2], om1, ALU.add, inc=sDVE)
    kb.v("tensor_single_scalar", FA[0:8, GS + 8:GS + 32], iot[0:8, 16:40],
         off, ALU.is_ge, inc=sDVE)
    kb.wait("tensor", sDVE)
    kb.mm(pR[0:1, 0:24], o81, FA[0:8, GS + 8:GS + 32], True, True, inc=sPE)
    kb.wait("vector", sPE)
    kb.v("tensor_copy", FA[0:1, GS + 48:GS + 72], pR[0:1, 0:24], inc=sDVE)
    kb.v("tensor_scalar_sub", FA[0:1, GS + 48:GS + 72],
         FA[0:1, GS + 48:GS + 72], 1.0, inc=sDVE)
    kb.v("memset", FA[0:1, GS + 88:GS + 89], -1.0, inc=sDVE)
    kb.v("tensor_copy", FA[0:1, GS + 89:GS + 112], FA[0:1, GS + 48:GS + 71],
         inc=sDVE)
    kb.op("vector", lambda e, o=FA[0:1, GS + 88:GS + 112],
          a=FA[0:1, GS + 48:GS + 72], b=FA[0:1, GS + 88:GS + 112]:
          e.tensor_tensor(o, a, b, ALU.is_equal), inc=sDVE)
    kb.v("tensor_scalar_mul", FA[0:1, GS + 8:GS + 32],
         FA[0:1, GS + 48:GS + 72], float(128 * 2 * WX), inc=sDVE)
    kb.stt(FA[0:1, GS + 8:GS + 32], FA[0:1, GS + 88:GS + 112], 67108864.0,
           FA[0:1, GS + 8:GS + 32], ALU.mult, ALU.add, inc=sDVE)
    kb.v("tensor_copy", rsel[0:1, 0:24], FA[0:1, GS + 8:GS + 32], inc=sDVE)
    kb.v("tensor_scalar_mul", FA[0:1, GS + 8:GS + 32],
         FA[0:1, GS + 48:GS + 72], float(128 * WX), inc=sDVE)
    kb.stt(FA[0:1, GS + 8:GS + 32], FA[0:1, GS + 88:GS + 112], 33554432.0,
           FA[0:1, GS + 8:GS + 32], ALU.mult, ALU.add, inc=sDVE)
    kb.v("tensor_copy", rsel[0:1, 24:48], FA[0:1, GS + 8:GS + 32],
         inc=sDVE)

    for h in range(2):
        gemH = FA[0:8, GEM + h * HT2:GEM + (h + 1) * HT2]
        pposH = FA[0:8, BBC + h * HT2:BBC + (h + 1) * HT2]
        mH = FA[0:8, GB:GB + HT2]
        cumH = FA[0:8, GB + HT2:GB + 2 * HT2]
        t1H = FA[0:8, SQ2:SQ2 + HT2]
        kb.wait("vector", sPE)
        kb.v("tensor_single_scalar", mH, gemH, 0.0, ALU.is_gt, inc=sDVE)
        for q in range(2):
            kb.wait("tensor", sDVE)
            kb.mm(pR[0:8, 0:512], ltri[0:8, 8:16],
                  FA[0:8, GB + q * 512:GB + (q + 1) * 512], True, True,
                  inc=sPE)
            kb.wait("vector", sPE)
            kb.v("tensor_copy",
                 FA[0:8, GB + HT2 + q * 512:GB + HT2 + (q + 1) * 512],
                 pR[0:8, 0:512], inc=sDVE)
        kb.v("tensor_single_scalar", t1H, cumH, kcon[0:8, 4:5],
             ALU.is_equal, inc=sDVE)
        kb.v("tensor_mul", t1H, t1H, mH, inc=sDVE)
        kb.v("tensor_single_scalar", cumH, cumH, kcon[0:8, 3:4],
             ALU.is_equal, inc=sDVE)
        kb.v("tensor_mul", cumH, cumH, mH, inc=sDVE)
        for r, (a_, b_) in enumerate(((cumH, pposH), (t1H, pposH),
                                      (cumH, gemH), (t1H, gemH))):
            kb.wait("vector", sPE)
            kb.op("vector", lambda e, o=mH, a=a_, b=b_:
                  e.tensor_mul(o, a, b), inc=sDVE)
            kb.wait("tensor", sDVE)
            for q in range(2):
                kb.mm(pC[:, q * 512:(q + 1) * 512],
                      sel4[0:8, r * 128:(r + 1) * 128],
                      FA[0:8, GB + q * 512:GB + (q + 1) * 512],
                      r == 0, r == 3, inc=sPE if q == 1 else None)
        kb.wait("vector", sPE)
        kb.v("tensor_copy", FA[:, GEM + 2 * HT2:GEM + 3 * HT2], pC[:, 0:1024],
             inc=sDVE)
        for kk in range(8):
            kcoli = h * 8 + kk
            kb.wait("tensor", sDVE)
            kb.trans(pTF[:], FA[:, GEM + 2 * HT2 + kk * 128:GEM + 2 * HT2 + (kk + 1) * 128],
                     idf[:], inc=sPE)
            kb.wait("vector", sPE)
            kb.v("tensor_copy", sidxu[:, kcoli:kcoli + 1], pTF[:, 0:1],
                 inc=sDVE)
            kb.v("tensor_copy", sidxu[:, 16 + kcoli:17 + kcoli],
                 pTF[:, 32:33], inc=sDVE)
            kb.v("tensor_copy", gsc[:, kcoli:kcoli + 1], pTF[:, 64:65],
                 inc=sDVE)
            kb.v("tensor_copy", gsc[:, 16 + kcoli:17 + kcoli], pTF[:, 96:97],
                 inc=sDVE)
        kb.wait("tensor", sDVE)

    kb.wait("gpsimd", sSt, spill_done)
    kb.wait("gpsimd", sDVE)
    wreg0 = {}
    for k in range(16):
        kb.wait("scalar", sSt, kb.val(sSt))
        kb.wait("scalar", sDVE)
        kb.act(BA[:, 0:1024], fa(XR + k * D, D), AF.Identity,
               scale=gsc[:, k:k + 1], inc=sACT)
        kb.act(BA[:, 1024:2048], fa(XR + k * D, D), AF.Identity,
               scale=gsc[:, 16 + k:17 + k], inc=sACT)
        kb.wait("gpsimd", sACT)
        for (six0, src) in ((k, BA[:, 0:1024]), (16 + k, BA[:, 1024:2048])):
            def scat_x(e, c=six0, s=src):
                if "x" not in wreg0:
                    wreg0["x"] = e.register("xbnd").__enter__()
                    e.reg_mov(wreg0["x"], 24 * 256 - 1)
                e.indirect_dma_start(
                    out=xsort,
                    out_offset=bass.IndirectOffsetOnAxis(
                        ap=sidxu[:, c:c + 1], axis=0),
                    in_=s, in_offset=None, bounds_check=wreg0["x"],
                    oob_is_err=False).then_inc(sSt, 16)
            kb.op_raw("gpsimd", scat_x, incs={sSt: 16})

    if CUT == 2:
        return
    if DBG:
        dbg_sidxu = kb.nc.dram_tensor("dbg_sidxu", [128, 32],
                                      mybir.dt.uint32,
                                      kind="ExternalOutput").ap()
        dbg_gsc = kb.nc.dram_tensor("dbg_gsc", [128, 32], F32,
                                    kind="ExternalOutput").ap()
        dbg_rsel = kb.nc.dram_tensor("dbg_rsel", [1, 48], mybir.dt.int32,
                                     kind="ExternalOutput").ap()
        dbg_cols = kb.nc.dram_tensor("dbg_cols", [8, 8], F32,
                                     kind="ExternalOutput").ap()
        dbg_gem = kb.nc.dram_tensor("dbg_gem", [8, T], F32,
                                    kind="ExternalOutput").ap()
        dbg_xs = kb.nc.dram_tensor("dbg_xs", [24 * 256, D], BF16,
                                   kind="ExternalOutput").ap()
        dbg_x1 = kb.nc.dram_tensor("dbg_x1", [T, D], F32,
                                   kind="ExternalOutput").ap()
        dbg_os = kb.nc.dram_tensor("dbg_os", [24 * 256, D], F32,
                                   kind="ExternalOutput").ap()
        dbg_x2 = kb.nc.dram_tensor("dbg_x2", [T, D], F32,
                                   kind="ExternalOutput").ap()
        kb.wait("scalar", sDVE)
        kb.wait("scalar", sSt, kb.val(sSt))
        kb.dma("scalar", dbg_sidxu[:, 0:32], sidxu[:, 0:32], sSt)
        kb.dma("scalar", dbg_gsc, gsc[:], sSt)
        kb.dma("scalar", dbg_rsel, rsel[0:1, :], sSt)
        kb.dma("scalar", dbg_cols, FA[0:8, GS:GS + 8], sSt)
        kb.dma("scalar", dbg_gem, FA[0:8, GEM:GEM + T], sSt)
        kb.wait("scalar", sSt)

    FB_xt, FB_xs, FB_h, FB_hT = 22528, 23552, 25600, 28416
    OST = 16384
    S1F = LF["s1"]
    HCC = ((0, 512), (512, 1024), (1024, 1536), (1536, 2048), (2048, 2560),
           (2560, 2816))
    mm_prev = kb.val(sPE)
    if DBG:
        kb.wait("scalar", sSt, kb.val(sSt))
        for jj in range(48):
            kb.dma("scalar", dbg_xs[jj * 128:(jj + 1) * 128, :],
                   xsort[jj * 128:(jj + 1) * 128, :], sSt)
        kb.dma("scalar", dbg_x1, x2f, sSt)
        kb.wait("scalar", sSt)
    silu_done = {}
    xtr_done = [0, 0]
    wreg = {}
    pat13 = [[2 * WX, 128], [WX, 2], [1, WX]]
    pat2 = [[WX, 128], [1, WX]]
    for j in range(24):
        par = j % 2
        if j == 0:
            kb.wait("sync", sSt, kb.val(sSt))
            kb.wait("sync", sACT)
            kb.wait("sync", sDVE)
            kb.wait("sync", sC[5], 1)
            kb.wait("sync", sC[6], 1)
        kb.wait("sync", sPE, mm_prev)

        def wload13(e, jj=j):
            if "s" not in wreg:
                wreg["s"] = e.register("wsel_s").__enter__()
            r = wreg["s"]
            e.reg_load(r, rsel[0:1, jj:jj + 1])
            e.dma_start(out=BA[:, 0:2 * WX], in_=bass.AP(w13br_t, r, pat13),
                        bounds_check="skip_entire_dma").then_inc(sLd, 16)
        if CUT != 3:
            kb.op_raw("sync", wload13, incs={sLd: 16})
        kb.wait("scalar", sPE, mm_prev)
        kb.wait("scalar", sDVE)

        def wload2(e, jj=j):
            if "a" not in wreg:
                wreg["a"] = e.register("wsel_a").__enter__()
            r = wreg["a"]
            e.reg_load(r, rsel[0:1, 24 + jj:25 + jj])
            e.dma_start(out=FAB[:, 0:22528], in_=bass.AP(w2br_t, r, pat2),
                        bounds_check="skip_entire_dma").then_inc(sLd, 16)
        if CUT != 3:
            kb.op_raw("scalar", wload2, incs={sLd: 16})
        wld_j = kb.val(sLd)

        for st_ in range(2):
            sp = (2 * j + st_) % 2
            kb.wait("sync", sPE, xtr_done[sp])
            kb.dma("sync", FAB[:, FB_xs + sp * 1024:FB_xs + (sp + 1) * 1024],
                   xsort[j * 256 + st_ * 128:j * 256 + (st_ + 1) * 128, :],
                   sLd)
            kb.wait("sync", sLd)
            ldx = kb.val(sLd)
            dvev = []
            kb.wait("tensor", sLd, ldx)
            tslots = [pA.bitcast(BF16), pB.bitcast(BF16),
                      pKV.bitcast(BF16), pR.bitcast(BF16)]
            kb.wait("tensor", sDVE)
            kb.wait("tensor", sACT)
            for ci in range(8):
                ph = ci % 4
                if ci >= 4:
                    kb.wait("tensor", sDVE, dvev[ci - 4])
                kb.trans(tslots[ph][:, 0:128],
                         FAB[:, FB_xs + sp * 1024 + ci * 128:FB_xs + sp * 1024 + (ci + 1) * 128],
                         idb[:], inc=sPE)
                kb.wait("vector", sPE)
                if ci == 0:
                    kb.wait("vector", sPE, mm_prev)
                kb.v("tensor_copy",
                     FAB[:, FB_xt + ci * 128:FB_xt + (ci + 1) * 128],
                     tslots[ph][:, 0:128], inc=sDVE)
                dvev.append(kb.val(sDVE))
            xtr_done[sp] = kb.val(sPE)

            kb.wait("tensor", sLd, wld_j)
            kb.wait("tensor", sDVE)
            for hci, (h0, h1) in enumerate(HCC):
                wdt = h1 - h0
                pw1 = (pA, pKV)[hci % 2]
                pw3 = (pB, pR)[hci % 2]
                if (j, st_, hci - 2) in silu_done:
                    kb.wait("tensor", sDVE, silu_done[(j, st_, hci - 2)])
                for ci in range(8):
                    kb.mm(pw1[:, 0:wdt],
                          FAB[:, FB_xt + ci * 128:FB_xt + (ci + 1) * 128],
                          BA[:, ci * HID + h0:ci * HID + h1],
                          ci == 0, ci == 7, inc=sPE if ci == 7 else None)
                for ci in range(8):
                    kb.mm(pw3[:, 0:wdt],
                          FAB[:, FB_xt + ci * 128:FB_xt + (ci + 1) * 128],
                          BA[:, WX + ci * HID + h0:WX + ci * HID + h1],
                          ci == 0, ci == 7, inc=sPE if ci == 7 else None)
                kb.wait("scalar", sPE)
                kb.wait("scalar", sDVE)
                kb.act(FA[:, S1F:S1F + wdt], pw1[:, 0:wdt], AF.Silu, inc=sACT)
                kb.wait("vector", sACT)
                kb.op("vector", lambda e, o=FAB[:, FB_h + h0:FB_h + h1],
                      a=FA[:, S1F:S1F + wdt], b=pw3[:, 0:wdt]:
                      e.tensor_mul(o, a, b), inc=sDVE)
                silu_done[(j, st_, hci)] = kb.val(sDVE)
            actv = []
            kb.wait("tensor", sDVE)
            kb.wait("tensor", sACT)
            for kkk in range(22):
                ph = kkk % 4
                if kkk >= 4:
                    kb.wait("tensor", sACT, actv[kkk - 4])
                kb.trans(tslots[ph][:, 0:128],
                         FAB[:, FB_h + kkk * 128:FB_h + (kkk + 1) * 128],
                         idb[:], inc=sPE)
                kb.wait("scalar", sPE)
                kb.act(FAB[:, FB_hT + kkk * 128:FB_hT + (kkk + 1) * 128],
                       tslots[ph][:, 0:128], AF.Identity, inc=sACT)
                actv.append(kb.val(sACT))
            kb.wait("tensor", sACT)
            kb.wait("tensor", sDVE)
            for co in range(2):
                for kkk in range(22):
                    kb.mm(pC[:, co * 512:(co + 1) * 512],
                          FAB[:, FB_hT + kkk * 128:FB_hT + (kkk + 1) * 128],
                          FAB[:, kkk * 1024 + co * 512:kkk * 1024 + co * 512 + 512],
                          kkk == 0, kkk == 21,
                          inc=sPE if kkk == 21 else None)
            kb.wait("vector", sPE)
            kb.wait("vector", sSt, kb.val(sSt))
            kb.v("tensor_copy", fa(OST, 512), pC[:, 0:512], inc=sDVE)
            kb.v("tensor_copy", fa(OST + 512, 512), pC[:, 512:1024], inc=sDVE)
            kb.wait("scalar", sDVE)
            kb.dma("scalar",
                   osort[j * 256 + st_ * 128:j * 256 + (st_ + 1) * 128, :],
                   fa(OST, 1024), sSt)
            mm_prev = kb.val(sPE)

    if DBG:
        kb.wait("scalar", sSt, kb.val(sSt))
        kb.dma("scalar", dbg_os, osort, sSt)
        kb.wait("scalar", sSt)
    kb.fence()
    for i in range(16):
        kb.wait("sync", sLd)
        kb.wait("sync", sDVE)
        kb.dma("sync", fa(XR + i * D, D), x2f[i * 128:(i + 1) * 128, :], sLd)
        kb.wait("gpsimd", sLd)
        kb.wait("gpsimd", sDVE)
        kb.op_raw("gpsimd",
                  lambda e, c=i:
                  e.indirect_dma_start(
                      out=FA[:, OST:OST + 1024], out_offset=None,
                      in_=osort,
                      in_offset=bass.IndirectOffsetOnAxis(
                          ap=sidxu[:, c:c + 1], axis=0)).then_inc(sLd, 16),
                  incs={sLd: 16})
        kb.op_raw("gpsimd",
                  lambda e, c=16 + i:
                  e.indirect_dma_start(
                      out=FA[:, OST + 1024:OST + 2048], out_offset=None,
                      in_=osort,
                      in_offset=bass.IndirectOffsetOnAxis(
                          ap=sidxu[:, c:c + 1], axis=0)).then_inc(sLd, 16),
                  incs={sLd: 16})
        kb.wait("vector", sLd)
        kb.v("tensor_add", fa(XR + i * D, D), fa(XR + i * D, D),
             fa(OST, 1024), inc=sDVE)
        kb.v("tensor_add", fa(XR + i * D, D), fa(XR + i * D, D),
             fa(OST + 1024, 1024), inc=sDVE)
        if DBG:
            kb.wait("gpsimd", sDVE)
            kb.dma("gpsimd", dbg_x2[i * 128:(i + 1) * 128, :],
                   fa(XR + i * D, D), sSt)
    kb.wait("vector", sDVE)



_CACHE = {}


def _get(name, builder):
    if name not in _CACHE:
        _CACHE[name] = builder().nc
    return _CACHE[name]


def _sel4():
    s = np.zeros((8, 512), np.float32)
    for r, rr in enumerate((0, 32, 64, 96)):
        s[:, r * 128 + rr] = 1.0
    return s


def _col4(v):
    return np.ascontiguousarray(np.asarray(v, np.float32).reshape(4, 128).T)


def _in_maps(inputs):
    x = np.asarray(inputs["L0_attn_x"])
    w_in = np.asarray(inputs["L1_attn_W_in"], np.float32)
    cw = np.asarray(inputs["L1_attn_conv_w"], np.float32)
    cb = np.asarray(inputs["L1_attn_conv_b"], np.float32)
    w_x = np.asarray(inputs["L1_attn_W_x"], np.float32)
    w_dt = np.asarray(inputs["L1_attn_W_dt"], np.float32)
    b_dt = np.asarray(inputs["L1_attn_b_dt"], np.float32)
    alog = np.asarray(inputs["L1_attn_A_log"], np.float32)
    dd = np.asarray(inputs["L1_attn_D"], np.float32)
    w_out = np.asarray(inputs["L1_attn_W_out"], np.float32)
    k0 = np.asarray(inputs["L1_ffn_k0"]).view(np.uint8)
    k1 = np.ascontiguousarray(np.asarray(inputs["L1_ffn_k1"]).view(np.uint8))
    k2 = np.ascontiguousarray(np.asarray(inputs["L1_ffn_k2"]).view(np.uint8))
    s0 = np.asarray(inputs["L1_ffn_s0"], np.float32).reshape(())
    s1 = np.asarray(inputs["L1_ffn_s1"], np.float32).reshape(())
    s2 = np.asarray(inputs["L1_ffn_s2"], np.float32).reshape(())
    scol = np.tile(np.stack([s0, s1, s2]).reshape(1, 3), (128, 1))
    base = {
        "wq": np.ascontiguousarray(inputs["L0_attn_Wq"], np.float32),
        "wk": np.ascontiguousarray(inputs["L0_attn_Wk"], np.float32),
        "wv": np.ascontiguousarray(inputs["L0_attn_Wv"], np.float32),
        "wp": np.ascontiguousarray(inputs["L0_attn_Wp"], np.float32),
        "bq_c": np.ascontiguousarray(
            np.asarray(inputs["L0_attn_bq"], np.float32).reshape(8, 128).T),
        "bk_r": np.asarray(inputs["L0_attn_bk"], np.float32).reshape(1, D),
        "bv_r": np.asarray(inputs["L0_attn_bv"], np.float32).reshape(1, D),
        "bp_r": np.asarray(inputs["L0_attn_bp"], np.float32).reshape(1, D),
        "id128b": np.eye(128, dtype=np.float32).astype(BF),
        "idf": np.eye(128, dtype=np.float32),
        "ckon": np.tile(np.array([list(CONSTS)], np.float32), (128, 1)),
        "ones_d": np.ones((1, 128), np.float32),
        "gwT": np.ascontiguousarray(
            np.asarray(inputs["L0_ffn_gate_w"], np.float32).T),
        "eb_r": np.asarray(
            inputs["L0_ffn_expert_bias"], np.float32).reshape(1, NE),
        "ltri2": np.concatenate(
            [np.triu(np.ones((8, 8), np.float32), 1),
             np.triu(np.ones((8, 8), np.float32), 0)], axis=1),
        "iotas": np.concatenate(
            [np.tile(np.arange(16, dtype=np.float32) * 256, (8, 1)),
             np.tile(np.arange(40, dtype=np.float32) * 256, (8, 1))], axis=1),

        "sel4_d": _sel4(),
        "lng": np.asarray(inputs["L1_attn_ln_g"], np.float32).reshape(1, D),
        "lnb": np.asarray(inputs["L1_attn_ln_b"], np.float32).reshape(1, D),
        "k0t": np.ascontiguousarray(k0[:D]),
        "k0b": np.ascontiguousarray(k0[D:]),
        "k1": k1, "k2": k2,
        "scol": np.ascontiguousarray(scol),
    }
    w1f = np.asarray(inputs["L0_ffn_w1"], np.float32)
    w2f = np.asarray(inputs["L0_ffn_w2"], np.float32)
    w3f = np.asarray(inputs["L0_ffn_w3"], np.float32)
    maps = []
    for c in range(NCORES):
        b, s = c // 4, c % 4
        sl = slice(s * 512, (s + 1) * 512)
        m = dict(base)
        m["w1own"] = np.ascontiguousarray(w1f[c])
        m["w2own"] = np.ascontiguousarray(w2f[c])
        m["w3own"] = np.ascontiguousarray(w3f[c])
        m["xk"] = np.ascontiguousarray(
            np.asarray(x[b, s * T:(s + 1) * T], dtype=np.float32))
        m["winx"] = np.ascontiguousarray(w_in[sl, :].T)
        m["winz"] = np.ascontiguousarray(
            w_in[2048 + s * 512:2048 + (s + 1) * 512, :].T)
        m["cw"] = np.ascontiguousarray(
            cw[sl, 0, :].reshape(4, 128, 4).transpose(1, 0, 2).reshape(128, 16))
        m["cb"] = _col4(cb[sl])
        wsl = np.concatenate([w_x[sl], w_x[2048:2080]], axis=0)
        m["wxT"] = np.ascontiguousarray(wsl.T)
        m["wdtT"] = np.ascontiguousarray(w_dt[sl, :].T)
        m["bdt_c"] = _col4(b_dt[sl])
        m["alog_c"] = np.ascontiguousarray(
            alog[sl, :].reshape(4, 128, 16).transpose(1, 0, 2).reshape(128, 64))
        m["dcol"] = _col4(dd[sl])
        m["woT"] = np.ascontiguousarray(w_out[:, sl].T)
        maps.append(m)
    return maps


def kernel(**inputs):
    nc = _get("fused", build_fused)
    res = run_bass_kernel_spmd(nc, _in_maps(inputs),
                               core_ids=list(range(NCORES))).results
    out = np.empty((2, TB, D), np.float32)
    for c in range(NCORES):
        b, s = c // 4, c % 4
        out[b, s * T:(s + 1) * T] = res[c]["xo"]
    return out


# version 09
# speedup vs baseline: 1.0103x; 1.0103x over previous
import contextlib

import numpy as np
import ml_dtypes

import concourse.bass as bass
import concourse.mybir as mybir
from concourse.bass_utils import run_bass_kernel_spmd

F32 = mybir.dt.float32
BF16 = mybir.dt.bfloat16
U8 = mybir.dt.uint8
AF = mybir.ActivationFunctionType
ALU = mybir.AluOpType
AX = mybir.AxisListType
BF = ml_dtypes.bfloat16

CONSTS = (1e-5, -0.5, -1.0, 1.0, 2.0, 0.5, 3.0, 4.0)
DBG = False
CUT = 0

NCORES = 8
T = 2048
TB = 8192
D = 1024
NH = 16
NE = 8
HID = 2816
DI = 2048
DIC = 512
DS = 16

GROUPS = [[0, 1, 2, 3], [4, 5, 6, 7]]

ENGINES = ("sync", "tensor", "vector", "scalar", "gpsimd")


class KB:

    def __init__(self):
        self.nc = bass.Bass("TRN2", target_bir_lowering=False, debug=False)
        self.stack = contextlib.ExitStack()
        self.plan = []
        self.semv = {}
        self._pend = {e: [] for e in ENGINES}
        self._phase = "P0"

    def phase(self, name):
        self._phase = name

    def sem(self, name):
        h = self.stack.enter_context(self.nc.semaphore(name))
        self.semv[h] = 0
        return h

    def sbuf(self, name, shape, dt):
        return self.stack.enter_context(self.nc.sbuf_tensor(name, shape, dt))

    def psum(self, name, shape, dt=F32):
        return self.stack.enter_context(self.nc.psum_tensor(name, shape, dt))

    def op(self, eng, fn, waits=(), inc=None, n=1, selfwait=None):
        w = list(self._pend[eng]) + list(waits)
        self._pend[eng] = []
        if selfwait is None:
            selfwait = eng in ("vector", "scalar", "gpsimd")
        if selfwait and inc is not None and self.semv.get(inc, 0) > 0:
            w.append((inc, self.semv[inc]))
        incs = []
        if inc is not None:
            self.semv[inc] += n
            incs = [(inc, n)]
        self.plan.append((eng, fn, w, incs, self._phase))

    def wait(self, eng, sem, val=None):
        self._pend[eng].append((sem, self.semv[sem] if val is None else val))

    def val(self, s):
        return self.semv[s]

    def fence(self):
        snap = [(s, v) for s, v in self.semv.items() if v > 0]
        for e in ENGINES:
            self._pend[e] = self._pend[e] + list(snap)

    def dma(self, q, out, in_, sem, waits=(), **kw):
        self.op(q, lambda e, o=out, i=in_, kw=kw: e.dma_start(out=o, in_=i, **kw),
                waits, inc=sem, n=16, selfwait=False)

    def cc(self, kind, op_, ins, outs, sem, waits=(), groups=None):
        g = GROUPS if groups is None else groups
        self.op("gpsimd",
                lambda e, k=kind, o=op_, i=ins, u=outs, g=g:
                e.collective_compute(k, o, replica_groups=g,
                                     ins=[i.opt()], outs=[u.opt()]),
                waits, inc=sem, n=1, selfwait=False)

    def op_raw(self, eng, fn, waits=(), incs=None):
        w = list(self._pend[eng]) + list(waits)
        self._pend[eng] = []
        for s, n in (incs or {}).items():
            self.semv[s] += n
        self.plan.append((eng, fn, w, [], self._phase))

    def mm(self, out, lhsT, rhs, start, stop, waits=(), inc=None):
        self.op("tensor",
                lambda e, o=out, l=lhsT, r=rhs, st=start, sp=stop:
                e.matmul(o, l, r, start=st, stop=sp),
                waits, inc=inc)

    def trans(self, out, in_, ident, waits=(), inc=None):
        self.op("tensor",
                lambda e, o=out, i=in_, d=ident: e.transpose(o, i, d),
                waits, inc=inc)

    def act(self, out, in_, func, bias=0.0, scale=1.0, waits=(), inc=None,
            accum_out=None):
        self.op("scalar",
                lambda e, o=out, i=in_, f=func, b=bias, s=scale, a=accum_out:
                e.activation(o, i, f, bias=b, scale=s, accum_out=a),
                waits, inc=inc)

    def v(self, fname, *args, waits=(), inc=None):
        self.op("vector",
                lambda e, fn=fname, a=args: getattr(e, fn)(*a),
                waits, inc=inc)

    def g(self, fname, *args, waits=(), inc=None):
        self.op("gpsimd",
                lambda e, fn=fname, a=args: getattr(e, fn)(*a),
                waits, inc=inc)

    def stt(self, out, in0, scalar, in1, op0, op1, waits=(), inc=None):
        self.op("vector",
                lambda e, o=out, a=in0, s=scalar, b=in1, o0=op0, o1=op1:
                e.scalar_tensor_tensor(o, a, s, b, o0, o1),
                waits, inc=inc)

    def gstt(self, out, in0, scalar, in1, op0, op1, waits=(), inc=None):
        self.op("gpsimd",
                lambda e, o=out, a=in0, s=scalar, b=in1, o0=op0, o1=op1:
                e.scalar_tensor_tensor(o, a, s, b, o0, o1),
                waits, inc=inc)

    def emit(self):
        plan = self.plan
        with self.nc.Block() as block:
            for engname in ENGINES:
                def body(eng, engname=engname):
                    for (en, fn, waits, incs, ph) in plan:
                        if en != engname:
                            continue
                        with self.nc.named_scope(ph):
                            for (s, vv) in waits:
                                eng.wait_ge(s, vv)
                            inst = fn(eng)
                            for (s, a) in incs:
                                assert inst is not None, "inc on op, no inst"
                                inst.then_inc(s, a)
                getattr(block, engname)(body)
        return self.nc


def lay(total, **sizes):
    off, out = 0, {}
    for k, n in sizes.items():
        out[k] = off
        off += n
    assert off <= total, f"layout overflow: {off} > {total}"
    return out


FA_COLS = 26752
BA_COLS = 46080



def emit_layernorm_tm(kb, s_tiles, sq_scr, cols, n_tiles, DW, sems,
                      gain_col=None, bias_col=None):
    sDVE, sACT = sems
    inv = 1.0 / DW
    kb.wait("scalar", sDVE)
    kb.wait("vector", sACT)
    if hasattr(kb, "kcon_ld"):
        kb.wait("scalar", kb.sLd_h, kb.kcon_ld)
    for i in range(n_tiles):
        s = s_tiles(i)
        kb.act(sq_scr, s, AF.Square, accum_out=cols[:, 1:2], inc=sACT)
        kb.v("reduce_sum", cols[:, 0:1], s, AX.X, inc=sDVE)
        kb.wait("vector", sACT)
        kb.v("tensor_scalar_mul", cols[:, 2:3], cols[:, 0:1], inv, inc=sDVE)
        kb.v("tensor_scalar_mul", cols[:, 3:4], cols[:, 1:2], inv, inc=sDVE)
        kb.v("tensor_mul", cols[:, 6:7], cols[:, 2:3], cols[:, 2:3], inc=sDVE)
        kb.v("tensor_sub", cols[:, 3:4], cols[:, 3:4], cols[:, 6:7], inc=sDVE)
        kb.wait("scalar", sDVE)
        kb.act(cols[:, 7:8], cols[:, 3:4], AF.Ln, bias=1e-5, inc=sACT)
        kb.act(cols[:, 4:5], cols[:, 7:8], AF.Exp, scale=-0.5, inc=sACT)
        kb.wait("vector", sACT)
        kb.v("tensor_mul", cols[:, 5:6], cols[:, 2:3], cols[:, 4:5], inc=sDVE)
        kb.v("tensor_scalar_mul", cols[:, 5:6], cols[:, 5:6], -1.0, inc=sDVE)
        kb.wait("scalar", sDVE)
        kb.act(s, s, AF.Identity, scale=cols[:, 4:5], bias=cols[:, 5:6], inc=sACT)
        if gain_col is not None:
            kb.act(s, s, AF.Identity, scale=gain_col, bias=bias_col, inc=sACT)
        kb.wait("vector", sACT)


def emit_tm2cm(kb, src_tile, ba, stage_off, dst_fn, n_tiles, idb, sems):
    sPE, sDVE, sACT = sems
    for i in range(n_tiles):
        kb.wait("vector", sACT)
        kb.v("tensor_copy", ba(stage_off, D), src_tile(i), inc=sDVE)
        kb.wait("tensor", sDVE)
        for ci in range(8):
            kb.trans(kb.pT[:, 0:128], ba(stage_off + ci * 128, 128), idb[:],
                     inc=sPE)
            kb.wait("vector", sPE)
            kb.v("tensor_copy", dst_fn(ci, i), kb.pT[:, 0:128], inc=sDVE)
            kb.wait("tensor", sDVE)



def build_fused():
    kb = KB()
    dram = lambda n, s, dt, k="ExternalInput": kb.nc.dram_tensor(n, s, dt, kind=k).ap()

    xk = dram("xk", [T, D], F32)
    wq = dram("wq", [D, D], F32)
    wk = dram("wk", [D, D], F32)
    wv = dram("wv", [D, D], F32)
    wp = dram("wp", [D, D], F32)
    bq_c = dram("bq_c", [128, 8], F32)
    idf_d = dram("idf", [128, 128], F32)
    ckon_d = dram("ckon", [128, 8], F32)
    bk_r = dram("bk_r", [1, D], F32)
    bv_r = dram("bv_r", [1, D], F32)
    bp_r = dram("bp_r", [1, D], F32)
    id128b = dram("id128b", [128, 128], BF16)
    ones_d = dram("ones_d", [1, 128], F32)
    gwT = dram("gwT", [D, NE], F32)
    eb_r = dram("eb_r", [1, NE], F32)
    w1own = dram("w1own", [D, HID], F32)
    w2own = dram("w2own", [HID, D], F32)
    w3own = dram("w3own", [D, HID], F32)
    ltri2 = dram("ltri2", [8, 16], F32)
    iotas = dram("iotas", [8, 56], F32)
    sel4_d = dram("sel4_d", [8, 512], F32)
    winx = dram("winx", [D, DIC], F32)
    winz = dram("winz", [D, DIC], F32)
    cw = dram("cw", [128, 16], F32)
    cb = dram("cb", [128, 4], F32)
    wxT = dram("wxT", [DI, 544], F32)
    wdtT = dram("wdtT", [DI, DIC], F32)
    bdt_c = dram("bdt_c", [128, 4], F32)
    alog_c = dram("alog_c", [128, 64], F32)
    dcol = dram("dcol", [128, 4], F32)
    woT = dram("woT", [DIC, D], F32)
    lng = dram("lng", [1, D], F32)
    lnb = dram("lnb", [1, D], F32)
    k0t = dram("k0t", [D, 4096], U8)
    k0b = dram("k0b", [D, 4096], U8)
    k1 = dram("k1", [4096, 4096], U8)
    k2 = dram("k2", [4096, D], U8)
    scol = dram("scol", [128, 3], F32)

    WX = 8 * HID
    w13loc = dram("w13loc", [128, 2 * WX], BF16, "Internal")
    w2loc = dram("w2loc", [128, WX], BF16, "Internal")
    w13br_t = kb.nc.dram_tensor("w13br", [NE * 128, 2 * WX], BF16,
                                kind="Internal")
    w2br_t = kb.nc.dram_tensor("w2br", [NE * 128, WX], BF16, kind="Internal")
    w13br, w2br = w13br_t.ap(), w2br_t.ap()
    xsort = dram("xsort", [24 * 256, D], BF16, "Internal")
    osort = dram("osort", [24 * 256, D], F32, "Internal")
    x2f = dram("x2f", [T, D], F32, "Internal")
    zf = dram("zf", [DIC, TB], F32, "Internal")
    uf32 = dram("uf32", [DIC, TB], F32, "Internal")
    bcf = dram("bcf", [32, TB], F32, "Internal")
    x3fh = dram("x3fh", [T, D], F32, "Internal")
    x3bh = dram("x3bh", [D, T], BF16, "Internal")
    w0dh = dram("w0dh", [D, 4096], BF16, "Internal")
    w1dh = dram("w1dh", [4096, 4096], BF16, "Internal")
    w2dh = dram("w2dh", [4096, D], BF16, "Internal")
    kvcc_i = dram("kvcc_i", [128, 512], F32, "Internal")
    kvcc_o = dram("kvcc_o", [128, 512], F32, "Internal")
    x2cc_i = dram("x2cc_i", [D, T], BF16, "Internal")
    x2g = dram("x2g", [4 * D, T], BF16, "Internal")
    ucc_i = dram("ucc_i", [DIC, TB], BF16, "Internal")
    ug = dram("ug", [DI, TB], BF16, "Internal")
    dincc_i = dram("dincc_i", [DIC, TB], BF16, "Internal")
    ding = dram("ding", [DI, TB], BF16, "Internal")
    pocc_i = dram("pocc_i", [TB, D], F32, "Internal")
    pors = dram("pors", [T, D], F32, "Internal")

    xo = dram("xo", [T, D], F32, "ExternalOutput")

    sLd = kb.sem("sLd")
    kb.sLd_h = sLd
    sCast = kb.sem("sCast")
    sPE = kb.sem("sPE")
    sDVE = kb.sem("sDVE")
    sACT = kb.sem("sACT")
    sSt = kb.sem("sSt")
    sC = [kb.sem(f"sC{i}") for i in range(8)]
    sems = (sDVE, sACT)

    bq_cols = kb.sbuf("bq_cols", [128, 8], F32)
    ones1 = kb.sbuf("ones1", [1, 128], F32)
    idb = kb.sbuf("idb", [128, 128], BF16)
    idf = kb.sbuf("idf_sb", [128, 128], F32)
    cols = kb.sbuf("cols", [128, 8], F32)
    kcon = kb.sbuf("kcon", [128, 8], F32)
    cw_sb = kb.sbuf("cw_sb", [128, 16], F32)
    cb_sb = kb.sbuf("cb_sb", [128, 4], F32)
    bdt = kb.sbuf("bdt", [128, 4], F32)
    alog = kb.sbuf("alog", [128, 64], F32)
    dco = kb.sbuf("dco", [128, 4], F32)
    ac = kb.sbuf("ac", [128, 64], F32)
    stc = kb.sbuf("stc", [128, 64], F32)
    brows = kb.sbuf("brows", [1, 2 * 1024], F32)
    sc_sb = kb.sbuf("sc_sb", [128, 4], F32)
    ltri = kb.sbuf("ltri", [8, 16], F32)
    iot = kb.sbuf("iot", [8, 56], F32)
    sidxu = kb.sbuf("sidxu", [128, 32], mybir.dt.uint32)
    rsel = kb.sbuf("rsel", [1, 48], mybir.dt.int32)
    gsc = kb.sbuf("gsc", [128, 32], F32)
    sel4 = kb.sbuf("sel4", [8, 512], F32)

    FA = kb.sbuf("FA", [128, FA_COLS], F32)
    BA = kb.sbuf("BA", [128, BA_COLS], BF16)
    KVB = 41984
    FAB = FA.bitcast(BF16)
    fa = lambda o, n: FA[:, o:o + n]
    ba = lambda o, n: BA[:, o:o + n]
    fab = lambda o, n: FAB[:, o:o + n]

    pA = kb.psum("pA", [128, 512])
    pB = kb.psum("pB", [128, 512])
    pC = kb.psum("pC", [128, 1024])
    pKV = kb.psum("pKV", [128, 512])
    pR = kb.psum("pR", [128, 512])
    pTF = kb.psum("pTF", [128, 128])
    pT = kb.psum("pT", [128, 256], BF16)
    kb.pT = pT

    kb.phase("P0_wconv")
    G8 = [list(range(8))]
    kb.dma("gpsimd", BA[:, 0:WX],
           w1own.rearrange("(c p) j -> p c j", p=128), sCast)
    kb.dma("gpsimd", BA[:, WX:2 * WX],
           w3own.rearrange("(c p) j -> p c j", p=128), sCast)
    kb.wait("gpsimd", sCast)
    kb.dma("gpsimd", w13loc, BA[:, 0:2 * WX], sCast)
    kb.wait("gpsimd", sCast)
    kb.cc("AllGather", ALU.bypass, w13loc, w13br, sC[5], groups=G8)
    kb.dma("gpsimd", BA[:, 0:WX],
           w2own.rearrange("(c p) j -> p c j", p=128), sCast)
    kb.wait("gpsimd", sCast)
    kb.dma("gpsimd", w2loc, BA[:, 0:WX], sCast)
    kb.wait("gpsimd", sCast)
    kb.cc("AllGather", ALU.bypass, w2loc, w2br, sC[6], groups=G8)
    conv_done = kb.val(sCast)

    kb.phase("P1_attn")
    LF = lay(FA_COLS, xr=16 * D, bbc=3 * D, brow=3 * D, sq=D,
             s1=512, s2=512, gb=T, gsm=128)
    LF["gem"] = LF["brow"]
    LF["x1cf"] = LF["s1"]

    kb.dma("sync", kcon[:], ckon_d, sLd)
    for j, cv in enumerate(CONSTS):
        kb.nc.const_aps.register(F32, cv, kcon[:, j:j + 1])
    kb.dma("sync", bq_cols[:], bq_c, sLd)
    kb.dma("sync", FA[0:1, LF["brow"]:LF["brow"] + D], bk_r, sLd)
    kb.dma("sync", FA[0:1, LF["brow"] + D:LF["brow"] + 2 * D], bv_r, sLd)
    kb.dma("sync", FA[0:1, LF["brow"] + 2 * D:LF["brow"] + 3 * D], bp_r, sLd)
    kb.dma("sync", idb[:], id128b, sLd)
    kb.dma("sync", idf[:], idf_d, sLd)
    kb.dma("sync", ones1[:], ones_d, sLd)
    kb.kcon_ld = kb.val(sLd)

    xk_t = xk.rearrange("(n p) d -> n p d", p=128)
    for i in range(16):
        kb.dma("sync", fa(LF["xr"] + i * D, D), xk_t[i], sLd)

    kb.wait("tensor", sLd, kb.kcon_ld)
    for j in range(6):
        kb.mm(pR[:], ones1[:, :],
              FA[0:1, LF["brow"] + j * 512:LF["brow"] + (j + 1) * 512],
              True, True, inc=sPE)
        kb.wait("vector", sPE)
        kb.v("tensor_copy", fa(LF["bbc"] + j * 512, 512), pR[:], inc=sDVE)
        kb.wait("tensor", sDVE)

    LB = lay(BA_COLS, xq=8 * T, wk=8 * D, wv=8 * D, k1=4 * D, vb=4 * D,
             stage=D)
    LQ = {"xq": LB["xq"], "wq": LB["wk"], "q1": LB["wv"]}
    kb.wait("vector", sLd)
    kb.wait("vector", sCast, conv_done)
    emit_tm2cm(kb, lambda i: fa(LF["xr"] + i * D, D), ba, LB["stage"],
               lambda ci, i: ba(LB["xq"] + ci * T + i * 128, 128),
               16, idb, (sPE, sDVE, sACT))

    for wsrc, off in ((wk, LB["wk"]), (wv, LB["wv"])):
        for ci in range(8):
            kb.dma("gpsimd", ba(off + ci * D, D),
                   wsrc[ci * 128:(ci + 1) * 128, :], sCast)
    kb.wait("tensor", sCast)
    kb.wait("tensor", sDVE)
    kb.v("memset", fa(LF["gb"], 512), 0.0, inc=sDVE)

    for g in range(16):
        for coh in range(2):
            for ci in range(8):
                kb.mm(pA[:], ba(LB["xq"] + ci * T + g * 128, 128),
                      ba(LB["wk"] + ci * D + coh * 512, 512),
                      ci == 0, ci == 7, inc=sPE if ci == 7 else None)
            for ci in range(8):
                kb.mm(pB[:], ba(LB["xq"] + ci * T + g * 128, 128),
                      ba(LB["wv"] + ci * D + coh * 512, 512),
                      ci == 0, ci == 7, inc=sPE if ci == 7 else None)
            kb.wait("vector", sPE)
            s1 = fa(LF["s1"], 512)
            s2 = fa(LF["s2"], 512)
            kb.stt(s1, pA[:], 1.0, fa(LF["bbc"] + coh * 512, 512),
                   ALU.mult, ALU.add, inc=sDVE)
            kb.stt(s2, pB[:], 1.0, fa(LF["bbc"] + D + coh * 512, 512),
                   ALU.mult, ALU.add, inc=sDVE)
            kb.wait("tensor", sDVE)
            kb.v("tensor_copy",
                 ba(LB["vb"] + (g % 4) * D + coh * 512, 512), s2, inc=sDVE)
            kb.wait("scalar", sDVE)
            kb.act(fa(LF["sq"], 512), s1, AF.Relu, inc=sACT)
            kb.wait("vector", sACT)
            kb.v("tensor_scalar_min", s1, s1, 0.0, inc=sDVE)
            kb.wait("scalar", sDVE)
            kb.act(s1, s1, AF.Exp, inc=sACT)
            kb.wait("vector", sACT)
            kb.v("tensor_add", s1, s1, fa(LF["sq"], 512), inc=sDVE)
            kb.v("tensor_copy",
                 ba(LB["k1"] + (g % 4) * D + coh * 512, 512), s1, inc=sDVE)
        if g % 4 == 3:
            kb.wait("tensor", sDVE)
            for hp in range(8):
                for hh in range(2):
                    for t4 in range(4):
                        h = hp * 2 + hh
                        kb.mm(pKV[hh * 64:hh * 64 + 64, 0:64],
                              ba(LB["k1"] + t4 * D + h * 64, 64),
                              ba(LB["vb"] + t4 * D + h * 64, 64),
                              t4 == 0, t4 == 3,
                              inc=sPE if (t4 == 3 and hh == 1) else None)
                kb.wait("vector", sPE)
                kb.op("vector", lambda e, o=FA[:, LF["gb"] + hp * 64:LF["gb"] + hp * 64 + 64],
                      a=FA[:, LF["gb"] + hp * 64:LF["gb"] + hp * 64 + 64],
                      b=pKV[:, 0:64]: e.tensor_add(o, a, b),
                      waits=[(sDVE, kb.val(sDVE))], inc=sDVE)
                kb.wait("tensor", sDVE)

    kb.wait("gpsimd", sDVE)
    kb.dma("gpsimd", kvcc_i, fa(LF["gb"], 512), sSt)
    kb.wait("gpsimd", sSt)
    kb.cc("AllReduce", ALU.add, kvcc_i, kvcc_o, sC[0])
    kb.wait("sync", sC[0], 1)
    kb.dma("sync", fa(LF["gb"], 512), kvcc_o, sLd)
    kb.wait("vector", sLd)
    kb.v("memset", BA[:, KVB:KVB + 1024], 0.0, inc=sDVE)
    for h in range(NH):
        prow = (h % 2) * 64
        hp = h // 2
        dro = (h % 2) * 64
        kb.v("tensor_copy",
             BA[dro:dro + 64, KVB + hp * 128 + dro:KVB + hp * 128 + dro + 64],
             FA[prow:prow + 64, LF["gb"] + hp * 64:LF["gb"] + hp * 64 + 64],
             inc=sDVE)

    kv_done = kb.val(sPE)
    for ci in range(8):
        w = [(sPE, kv_done)] if ci == 0 else []
        kb.dma("gpsimd", ba(LQ["wq"] + ci * D, D),
               wq[ci * 128:(ci + 1) * 128, :], sCast, waits=w)
    kb.wait("tensor", sCast)
    for co in range(8):
        for tc in range(4):
            for ci in range(8):
                kb.mm(pA[:], ba(LQ["wq"] + ci * D + co * 128, 128),
                      ba(LQ["xq"] + ci * T + tc * 512, 512),
                      ci == 0, ci == 7, inc=sPE if ci == 7 else None)
            kb.wait("scalar", sPE)
            kb.wait("scalar", sDVE)
            kb.act(fa(LF["s1"], 512), pA[:], AF.Identity,
                   bias=bq_cols[:, co:co + 1], inc=sACT)
            kb.act(fa(LF["s2"], 512), pA[:], AF.Relu,
                   bias=bq_cols[:, co:co + 1], inc=sACT)
            kb.wait("vector", sACT)
            kb.v("tensor_scalar_min", fa(LF["s1"], 512), fa(LF["s1"], 512),
                 0.0, inc=sDVE)
            kb.wait("scalar", sDVE)
            kb.act(fa(LF["s1"], 512), fa(LF["s1"], 512), AF.Exp, inc=sACT)
            kb.wait("vector", sACT)
            kb.v("tensor_add", fa(LF["s1"], 512), fa(LF["s1"], 512),
                 fa(LF["s2"], 512), inc=sDVE)
            kb.v("tensor_copy", ba(LQ["q1"] + co * T + tc * 512, 512),
                 fa(LF["s1"], 512), inc=sDVE)
            kb.wait("tensor", sDVE)

    for ci in range(8):
        w = [(sPE, kb.val(sPE))] if ci == 0 else []
        kb.dma("gpsimd", ba(LQ["wq"] + ci * D, D),
               wp[ci * 128:(ci + 1) * 128, :], sCast, waits=w)
    for hp in range(8):
        for tc in range(4):
            kb.mm(pA[:], BA[:, KVB + hp * 128:KVB + (hp + 1) * 128],
                  ba(LQ["q1"] + hp * T + tc * 512, 512), True, True, inc=sPE)
            kb.wait("vector", sPE)
            kb.v("tensor_copy", ba(LQ["q1"] + hp * T + tc * 512, 512), pA[:],
                 inc=sDVE)
            kb.wait("tensor", sDVE)
    kb.wait("tensor", sCast)
    for tt in range(16):
        for coh in range(2):
            for ci in range(8):
                kb.mm(pC[:, coh * 512:(coh + 1) * 512],
                      ba(LQ["q1"] + ci * T + tt * 128, 128),
                      ba(LQ["wq"] + ci * D + coh * 512, 512),
                      ci == 0, ci == 7, inc=sPE if ci == 7 else None)
        kb.wait("vector", sPE)
        for coh in range(2):
            xo_ = LF["xr"] + tt * D + coh * 512
            kb.stt(fa(LF["s1"], 512), pC[:, coh * 512:(coh + 1) * 512], 1.0,
                   fa(LF["bbc"] + 2 * D + coh * 512, 512), ALU.mult, ALU.add,
                   inc=sDVE)
            kb.v("tensor_add", fa(xo_, 512), fa(xo_, 512), fa(LF["s1"], 512),
                 inc=sDVE)
        kb.wait("tensor", sDVE)
    emit_layernorm_tm(kb, lambda i: fa(LF["xr"] + i * D, D),
                      fa(LF["sq"], D), cols, 16, D, sems)

    kb.phase("P2_moe")
    kb.wait("sync", sLd)
    kb.dma("sync", ltri[:], ltri2, sLd)
    kb.dma("sync", iot[:], iotas, sLd)
    kb.dma("sync", sel4[:], sel4_d, sLd)
    kb.wait("sync", sLd)
    build_moe_g(kb, (gwT, eb_r, w13br_t, w2br_t, xsort, osort, x2f),
                (FA, BA, FAB, LF), (pA, pB, pC, pKV, pR, pTF, pT),
                (sLd, sCast, sPE, sDVE, sACT, sSt, sC),
                (ones1, idf, idb, cols, ltri, iot, sidxu, rsel, gsc,
                 kcon, sel4))
    kb.fence()
    emit_layernorm_tm(kb, lambda i: fa(LF["xr"] + i * D, D),
                      fa(LF["sq"], D), cols, 16, D, sems)

    kb.phase("P3_x2gather")
    L3x = lay(BA_COLS, x2cb=8 * T, stage=D)
    emit_tm2cm(kb, lambda i: fa(LF["xr"] + i * D, D), ba, L3x["stage"],
               lambda ci, i: ba(L3x["x2cb"] + ci * T + i * 128, 128),
               16, idb, (sPE, sDVE, sACT))
    for i in range(16):
        kb.dma("gpsimd", x2f[i * 128:(i + 1) * 128, :],
               fa(LF["xr"] + i * D, D), sSt)
    for ci in range(8):
        kb.dma("gpsimd", x2cc_i[ci * 128:(ci + 1) * 128, :],
               ba(L3x["x2cb"] + ci * T, T), sSt)
    kb.wait("gpsimd", sSt)
    kb.wait("gpsimd", sDVE)
    kb.cc("AllGather", ALU.bypass, x2cc_i, x2g, sC[1])
    kb.fence()

    kb.phase("P8a_buildw")
    BW0, BW1 = 41984, 44032

    def build_w(src_u8, dst, rows, ncols, second=None):
        nchk = (ncols + 2047) // 2048
        for r in range(rows // 128):
            for cq in range(nchk):
                c0 = cq * 2048
                c1 = min(ncols, c0 + 2048)
                w_ = c1 - c0
                kb.wait("gpsimd", sSt)
                kb.wait("gpsimd", sCast)
                kb.dma("gpsimd", ba(BW0, w_),
                       src_u8[r * 128:(r + 1) * 128, c0:c1], sSt)
                if second is not None:
                    kb.dma("gpsimd", ba(BW1, w_),
                           second[r * 128:(r + 1) * 128, c0:c1], sSt)
                kb.wait("gpsimd", sSt)
                if second is None:
                    kb.g("tensor_scalar", ba(BW0, w_), ba(BW0, w_),
                         2.0, -1.0, ALU.mult, ALU.add, inc=sCast)
                else:
                    kb.g("tensor_sub", ba(BW0, w_), ba(BW0, w_), ba(BW1, w_),
                         inc=sCast)
                    kb.g("tensor_scalar_mul", ba(BW0, w_), ba(BW0, w_), 2.0,
                         inc=sCast)
                kb.wait("scalar", sCast)
                kb.dma("scalar", dst[r * 128:(r + 1) * 128, c0:c1],
                       ba(BW0, w_), sSt)

    build_w(k0t, w0dh, D, 4096, second=k0b)
    build_w(k1, w1dh, 4096, 4096)
    build_w(k2, w2dh, 4096, D)
    kb.phase("P4_win_conv")

    kb.phase("P4_win_conv")
    LFa = lay(FA_COLS, xs=4 * (T + 3), ua=T, us=T, zs=T)
    LBa = lay(BA_COLS, wx=8 * 512, wz=8 * 512, xc=2 * 8 * T, ub=T)
    kb.dma("sync", cw_sb[:], cw, sLd)
    kb.dma("sync", cb_sb[:], cb, sLd)
    for ci in range(8):
        kb.dma("gpsimd", ba(LBa["wx"] + ci * 512, 512),
               winx[ci * 128:(ci + 1) * 128, :], sCast)
        kb.dma("gpsimd", ba(LBa["wz"] + ci * 512, 512),
               winz[ci * 128:(ci + 1) * 128, :], sCast)
    kb.wait("gpsimd", sCast)
    for dt in range(4):
        kb.v("memset", fa(LFa["xs"] + dt * (T + 3), 3), 0.0, inc=sDVE)

    for ch in range(4):
        par = ch % 2
        xcb = lambda ci: ba(LBa["xc"] + par * 8 * T + ci * T, T)
        w = [(sPE, kb.val(sPE))] if ch >= 2 else [(sC[1], 1)]
        kb.wait("sync", sLd)
        for ci in range(8):
            kb.dma("sync", xcb(ci),
                   x2g[ch * D + ci * 128:ch * D + (ci + 1) * 128, :], sLd,
                   waits=w if ci == 0 else ())
        kb.wait("sync", sLd)
        kb.wait("tensor", sLd)
        if ch == 0:
            kb.wait("tensor", sCast, kb.val(sCast))
        for dt in range(4):
            xso = LFa["xs"] + dt * (T + 3)
            for t4 in range(4):
                for ci in range(8):
                    kb.mm(pA[:], ba(LBa["wx"] + ci * 512 + dt * 128, 128),
                          xcb(ci)[:, t4 * 512:(t4 + 1) * 512],
                          ci == 0, ci == 7, inc=sPE if ci == 7 else None)
                for ci in range(8):
                    kb.mm(pB[:], ba(LBa["wz"] + ci * 512 + dt * 128, 128),
                          xcb(ci)[:, t4 * 512:(t4 + 1) * 512],
                          ci == 0, ci == 7, inc=sPE if ci == 7 else None)
                kb.wait("vector", sPE)
                kb.v("tensor_copy", FA[:, xso + 3 + t4 * 512:xso + 3 + (t4 + 1) * 512],
                     pA[:], inc=sDVE)
                kb.v("tensor_copy", fa(LFa["zs"] + t4 * 512, 512), pB[:],
                     inc=sDVE)
                kb.wait("tensor", sDVE)
            kb.wait("gpsimd", sDVE)
            kb.wait("gpsimd", sSt)
            kb.dma("gpsimd", zf[dt * 128:(dt + 1) * 128, ch * T:(ch + 1) * T],
                   fa(LFa["zs"], T), sSt)
            kb.op("vector", lambda e, o=fa(LFa["ua"], T), i=FA[:, xso:xso + T],
                  s=cw_sb[:, dt * 4:dt * 4 + 1]:
                  e.tensor_scalar_mul(o, i, s), inc=sDVE)
            for j in range(1, 4):
                kb.stt(fa(LFa["ua"], T), FA[:, xso + j:xso + j + T],
                       cw_sb[:, dt * 4 + j:dt * 4 + j + 1], fa(LFa["ua"], T),
                       ALU.mult, ALU.add, inc=sDVE)
            kb.v("tensor_copy", FA[:, xso:xso + 3],
                 FA[:, xso + T:xso + T + 3], inc=sDVE)
            kb.wait("scalar", sDVE)
            kb.wait("scalar", sSt, kb.val(sSt))
            kb.act(fa(LFa["us"], T), fa(LFa["ua"], T), AF.Silu,
                   bias=cb_sb[:, dt:dt + 1], inc=sACT)
            kb.wait("vector", sACT)
            kb.v("tensor_copy", ba(LBa["ub"], T), fa(LFa["us"], T), inc=sDVE)
            kb.wait("gpsimd", sDVE)
            kb.wait("gpsimd", sACT)
            kb.wait("gpsimd", sSt)
            kb.dma("gpsimd", uf32[dt * 128:(dt + 1) * 128, ch * T:(ch + 1) * T],
                   fa(LFa["us"], T), sSt)
            kb.dma("gpsimd", ucc_i[dt * 128:(dt + 1) * 128, ch * T:(ch + 1) * T],
                   ba(LBa["ub"], T), sSt)
            kb.wait("vector", sSt)
    kb.wait("gpsimd", sSt)
    kb.cc("AllGather", ALU.bypass, ucc_i, ug, sC[2])
    kb.fence()

    kb.phase("P5_wx")
    TC2 = 1024
    LBb = lay(BA_COLS, wx=16 * 544, uc=2 * 16 * TC2, st1=512)
    LFb = lay(FA_COLS, st2=512)
    for ci in range(16):
        kb.dma("gpsimd", ba(LBb["wx"] + ci * 544, 544),
               wxT[ci * 128:(ci + 1) * 128, :], sCast)
    kb.wait("gpsimd", sCast)
    for ch in range(8):
        par = ch % 2
        ucb = lambda ci: ba(LBb["uc"] + par * 16 * TC2 + ci * TC2, TC2)
        w = [(sPE, kb.val(sPE))] if ch >= 2 else [(sC[2], 1)]
        kb.wait("sync", sLd)
        for ci in range(16):
            kb.dma("sync", ucb(ci), ug[ci * 128:(ci + 1) * 128,
                                       ch * TC2:(ch + 1) * TC2], sLd,
                   waits=w if ci == 0 else ())
        kb.wait("sync", sLd)
        kb.wait("tensor", sLd)
        if ch == 0:
            kb.wait("tensor", sCast, kb.val(sCast))
        for t2 in range(2):
            for co in range(4):
                for ci in range(16):
                    kb.mm(pA[:], ba(LBb["wx"] + ci * 544 + co * 128, 128),
                          ucb(ci)[:, t2 * 512:(t2 + 1) * 512],
                          ci == 0, ci == 15, inc=sPE if ci == 15 else None)
                kb.wait("vector", sPE)
                kb.wait("vector", sSt, kb.val(sSt))
                kb.v("tensor_copy", ba(LBb["st1"], 512), pA[:], inc=sDVE)
                kb.wait("tensor", sDVE)
                kb.wait("gpsimd", sDVE)
                kb.wait("gpsimd", sSt)
                kb.dma("gpsimd",
                       dincc_i[co * 128:(co + 1) * 128,
                               ch * TC2 + t2 * 512:ch * TC2 + (t2 + 1) * 512],
                       ba(LBb["st1"], 512), sSt)
            for ci in range(16):
                kb.mm(pR[0:32, :], ba(LBb["wx"] + ci * 544 + 512, 32),
                      ucb(ci)[:, t2 * 512:(t2 + 1) * 512],
                      ci == 0, ci == 15, inc=sPE if ci == 15 else None)
            kb.wait("vector", sPE)
            kb.wait("vector", sSt, kb.val(sSt))
            kb.v("tensor_copy", FA[0:32, LFb["st2"]:LFb["st2"] + 512],
                 pR[0:32, :], inc=sDVE)
            kb.wait("tensor", sDVE)
            kb.wait("gpsimd", sDVE)
            kb.wait("gpsimd", sSt)
            kb.dma("gpsimd",
                   bcf[:, ch * TC2 + t2 * 512:ch * TC2 + (t2 + 1) * 512],
                   FA[0:32, LFb["st2"]:LFb["st2"] + 512], sSt)
    kb.wait("gpsimd", sSt)
    kb.cc("AllGather", ALU.bypass, dincc_i, ding, sC[3])
    kb.fence()

    kb.phase("P6_scan")
    TC = 1024
    LFc = lay(FA_COLS, delta=4 * TC, du=4 * TC, yac=4 * TC, uc=4 * TC,
              bb=2 * TC, cbr=2 * TC, dA=2 * TC, dBu=2 * TC, sseq=TC, pst=TC)
    LFc["zc"] = LFc["delta"]
    LFc["s1"] = LFc["pst"]
    LFc["s2"] = LFc["pst"] + 512
    LBc = lay(BA_COLS, dinc=16 * TC, wdt=16 * 512, wo=4 * D, y2b=4 * TC)
    kb.dma("sync", bdt[:], bdt_c, sLd)
    kb.dma("sync", alog[:], alog_c, sLd)
    kb.dma("sync", dco[:], dcol, sLd)
    kb.wait("sync", sLd)
    for ci in range(16):
        kb.dma("gpsimd", ba(LBc["wdt"] + ci * 512, 512),
               wdtT[ci * 128:(ci + 1) * 128, :], sCast)
    for ci in range(4):
        kb.dma("gpsimd", ba(LBc["wo"] + ci * D, D),
               woT[ci * 128:(ci + 1) * 128, :], sCast)
    kb.wait("gpsimd", sCast)
    wld = kb.val(sCast)
    kb.wait("scalar", sLd)
    kb.act(ac[:], alog[:], AF.Exp, inc=sACT)
    kb.wait("vector", sACT)
    kb.v("tensor_scalar_mul", ac[:], ac[:], -1.0, inc=sDVE)
    kb.v("memset", stc[:], 0.0, inc=sDVE)

    scan_done = []
    pool_done = {}
    dve_done = {}
    bcast_mm_done = {}

    for ch in range(8):
        dcb = lambda ci: ba(LBc["dinc"] + ci * TC, TC)
        ucb = lambda dt: fa(LFc["uc"] + dt * TC, TC)
        w = [(sPE, kb.val(sPE)), (sDVE, kb.val(sDVE))] if ch >= 1 else [(sC[3], 1)]
        kb.wait("sync", sLd)
        for ci in range(16):
            kb.dma("sync", dcb(ci), ding[ci * 128:(ci + 1) * 128,
                                         ch * TC:(ch + 1) * TC], sLd,
                   waits=w if ci == 0 else ())
        for dt in range(4):
            kb.dma("sync", ucb(dt), uf32[dt * 128:(dt + 1) * 128,
                                         ch * TC:(ch + 1) * TC], sLd)
        kb.wait("sync", sLd)
        ld_ch = kb.val(sLd)
        kb.wait("tensor", sLd, ld_ch)
        if ch == 0:
            kb.wait("tensor", sCast, wld)
        kb.wait("scalar", sSt, kb.val(sSt))
        for dt in range(4):
            for t2 in range(2):
                pp = (pA, pB)[t2]
                for ci in range(16):
                    kb.mm(pp[:], ba(LBc["wdt"] + ci * 512 + dt * 128, 128),
                          dcb(ci)[:, t2 * 512:(t2 + 1) * 512],
                          ci == 0, ci == 15, inc=sPE if ci == 15 else None)
                kb.wait("scalar", sPE)
                kb.wait("scalar", sDVE)
                do = LFc["delta"] + dt * TC + t2 * 512
                kb.act(fa(LFc["s1"], 512), pp[:], AF.Abs,
                       bias=bdt[:, dt:dt + 1], inc=sACT)
                kb.act(fa(LFc["s1"], 512), fa(LFc["s1"], 512), AF.Exp,
                       scale=-1.0, inc=sACT)
                kb.act(fa(LFc["s1"], 512), fa(LFc["s1"], 512), AF.Ln,
                       bias=1.0, inc=sACT)
                kb.act(fa(LFc["s2"], 512), pp[:], AF.Relu,
                       bias=bdt[:, dt:dt + 1], inc=sACT)
                kb.wait("vector", sACT)
                kb.v("tensor_add", fa(do, 512), fa(LFc["s1"], 512),
                     fa(LFc["s2"], 512), inc=sDVE)
                kb.wait("tensor", sDVE)
            kb.wait("vector", sLd, ld_ch)
            kb.v("tensor_mul", fa(LFc["du"] + dt * TC, TC),
                 fa(LFc["delta"] + dt * TC, TC), ucb(dt), inc=sDVE)
        kb.v("memset", fa(LFc["yac"], 4 * TC), 0.0, inc=sDVE)
        du_done = kb.val(sDVE)
        for n in range(DS):
            np_ = n % 2
            pn = n - 1
            prev = (ch, pn) if pn >= 0 else ((ch - 1, DS + pn) if ch >= 1
                                             else None)
            wbr = []
            if prev in pool_done:
                wbr.append((sCast, pool_done[prev]))
            if prev in dve_done:
                wbr.append((sDVE, dve_done[prev]))
            kb.wait("sync", sLd)
            kb.dma("sync", fa(LFc["bb"] + np_ * TC, TC),
                   bcf[n:n + 1, ch * TC:(ch + 1) * TC].broadcast_to((128, TC)),
                   sLd, waits=wbr)
            kb.dma("sync", fa(LFc["cbr"] + np_ * TC, TC),
                   bcf[16 + n:17 + n, ch * TC:(ch + 1) * TC].broadcast_to((128, TC)),
                   sLd)
            kb.wait("sync", sLd)
            bcast_done_v = kb.val(sLd)
            for dt in range(4):
                k = len(scan_done)
                p = k % 2
                dsl = fa(LFc["delta"] + dt * TC, TC)
                gv = max(scan_done[k - 2] if k >= 2 else 0, du_done)
                kb.wait("scalar", sDVE, gv)
                kb.wait("gpsimd", sDVE, gv)
                if dt == 0:
                    kb.wait("gpsimd", sLd, bcast_done_v)
                    kb.wait("vector", sLd, bcast_done_v)
                kb.act(fa(LFc["dA"] + p * TC, TC), dsl, AF.Exp,
                       scale=ac[:, dt * 16 + n:dt * 16 + n + 1], inc=sACT)
                kb.g("tensor_mul", fa(LFc["dBu"] + p * TC, TC),
                     fa(LFc["du"] + dt * TC, TC), fa(LFc["bb"] + np_ * TC, TC),
                     inc=sCast)
                kb.wait("vector", sACT)
                kb.wait("vector", sCast)
                kb.op("vector", lambda e, o=fa(LFc["sseq"], TC),
                      a=fa(LFc["dA"] + p * TC, TC),
                      b=fa(LFc["dBu"] + p * TC, TC),
                      i=stc[:, dt * 16 + n:dt * 16 + n + 1]:
                      e.tensor_tensor_scan(o, a, b, i, ALU.mult, ALU.add),
                      inc=sDVE)
                scan_done.append(kb.val(sDVE))
                kb.v("tensor_copy", stc[:, dt * 16 + n:dt * 16 + n + 1],
                     fa(LFc["sseq"] + TC - 1, 1), inc=sDVE)
                kb.v("tensor_mul", fa(LFc["sseq"], TC), fa(LFc["sseq"], TC),
                     fa(LFc["cbr"] + np_ * TC, TC), inc=sDVE)
                kb.v("tensor_add", fa(LFc["yac"] + dt * TC, TC),
                     fa(LFc["yac"] + dt * TC, TC), fa(LFc["sseq"], TC),
                     inc=sDVE)
            pool_done[(ch, n)] = kb.val(sCast)
            dve_done[(ch, n)] = kb.val(sDVE)
        kb.wait("sync", sACT)
        kb.wait("sync", sDVE)
        for dt in range(4):
            kb.dma("sync", fa(LFc["zc"] + dt * TC, TC),
                   zf[dt * 128:(dt + 1) * 128, ch * TC:(ch + 1) * TC], sLd)
        kb.wait("sync", sLd)
        kb.wait("scalar", sLd)
        kb.wait("scalar", sDVE)
        kb.wait("vector", sPE)
        for dt in range(4):
            ysl = fa(LFc["yac"] + dt * TC, TC)
            zsl = fa(LFc["zc"] + dt * TC, TC)
            kb.stt(ysl, ucb(dt), dco[:, dt:dt + 1], ysl,
                   ALU.mult, ALU.add, inc=sDVE)
            kb.wait("scalar", sDVE)
            kb.act(zsl, zsl, AF.Silu, inc=sACT)
            kb.wait("vector", sACT)
            kb.v("tensor_mul", ysl, ysl, zsl, inc=sDVE)
            kb.v("tensor_copy", ba(LBc["y2b"] + dt * TC, TC), ysl, inc=sDVE)
        kb.wait("tensor", sDVE)
        kb.wait("tensor", sACT)
        for tt in range(8):
            for coh in range(2):
                pp = pC[:, coh * 512:coh * 512 + 512]
                for dt in range(4):
                    kb.mm(pp, ba(LBc["y2b"] + dt * TC + tt * 128, 128),
                          ba(LBc["wo"] + dt * D + coh * 512, 512),
                          dt == 0, dt == 3,
                          inc=sPE if dt == 3 else None)
            kb.wait("vector", sPE)
            kb.wait("vector", sSt, kb.val(sSt))
            kb.v("tensor_copy", fa(LFc["pst"], 512), pC[:, 0:512], inc=sDVE)
            kb.v("tensor_copy", fa(LFc["pst"] + 512, 512), pC[:, 512:1024],
                 inc=sDVE)
            kb.wait("tensor", sDVE)
            kb.wait("scalar", sDVE)
            kb.dma("scalar",
                   pocc_i[ch * TC + tt * 128:ch * TC + (tt + 1) * 128, :],
                   fa(LFc["pst"], TC), sSt)
    kb.wait("gpsimd", sSt)
    kb.cc("ReduceScatter", ALU.add, pocc_i, pors, sC[4])
    kb.fence()

    kb.phase("P7_lns")
    LF7 = lay(FA_COLS, xr=16 * D, gbc=2 * D, pl=2 * D, sq=D, s1=512, s2=512,
              xt=D)
    LB7 = lay(BA_COLS, x3h=8 * 1024, stage=D)
    lr = FA[0:1, LF7["pl"]:LF7["pl"] + 2 * D]
    kb.dma("sync", lr[:, 0:D], lng, sLd)
    kb.dma("sync", lr[:, D:2 * D], lnb, sLd)
    kb.dma("sync", sc_sb[:, 0:3], scol, sLd)
    kb.wait("sync", sLd)
    kb.wait("vector", sLd)
    kb.v("tensor_scalar_mul", sc_sb[:, 0:2], sc_sb[:, 0:2], 2.0, inc=sDVE)
    kb.wait("tensor", sLd)
    kb.wait("tensor", sDVE)
    for j in range(4):
        kb.mm(pR[:], ones1[:, :], lr[:, j * 512:(j + 1) * 512],
              True, True, inc=sPE)
        kb.wait("vector", sPE)
        kb.v("tensor_copy", fa(LF7["gbc"] + j * 512, 512), pR[:], inc=sDVE)
        kb.wait("tensor", sDVE)

    for i in range(16):
        kb.wait("sync", sLd)
        kb.wait("sync", sDVE)
        kb.dma("sync", fa(LF7["pl"], D), pors[i * 128:(i + 1) * 128, :], sLd,
               waits=[(sC[4], 1)] if i == 0 else ())
        kb.dma("sync", fa(LF7["xr"] + i * D, D), x2f[i * 128:(i + 1) * 128, :],
               sLd)
        kb.wait("sync", sLd)
        kb.wait("vector", sLd)
        m = fa(LF7["pl"], D)
        kb.v("tensor_add", m, m, fa(LF7["xr"] + i * D, D), inc=sDVE)
        emit_layernorm_tm(kb, lambda _i: m, fa(LF7["sq"], D), cols, 1, D, sems)
        kb.wait("vector", sACT)
        kb.v("tensor_mul", m, m, fa(LF7["gbc"], D), inc=sDVE)
        kb.v("tensor_add", m, m, fa(LF7["gbc"] + D, D), inc=sDVE)
        kb.v("tensor_add", m, fa(LF7["xr"] + i * D, D), m, inc=sDVE)
        emit_layernorm_tm(kb, lambda _i: m, fa(LF7["sq"], D), cols, 1, D, sems)
        kb.wait("gpsimd", sDVE)
        kb.wait("gpsimd", sACT)
        kb.wait("gpsimd", sSt)
        kb.dma("gpsimd", x3fh[i * 128:(i + 1) * 128, :], m, sSt)
        kb.wait("vector", sACT)
        kb.v("tensor_copy", fa(LF7["xr"] + i * D, D), m, inc=sDVE)
        kb.v("tensor_copy", ba(LB7["stage"], D), m, inc=sDVE)
        kb.wait("tensor", sDVE)
        for ci in range(8):
            kb.trans(pT[:, 0:128], ba(LB7["stage"] + ci * 128, 128), idb[:],
                     inc=sPE)
            kb.wait("vector", sPE)
            kb.v("tensor_copy",
                 ba(LB7["x3h"] + ci * 1024 + (i % 8) * 128, 128), pT[:, 0:128],
                 inc=sDVE)
            kb.wait("tensor", sDVE)
        if i % 8 == 7:
            half = i // 8
            kb.wait("gpsimd", sDVE)
            kb.wait("gpsimd", sSt)
            for ci in range(8):
                kb.dma("gpsimd", x3bh[ci * 128:(ci + 1) * 128,
                                      half * 1024:(half + 1) * 1024],
                       ba(LB7["x3h"] + ci * 1024, 1024), sSt)
            kb.wait("gpsimd", sSt)
            kb.wait("vector", sSt)
    kb.fence()

    kb.phase("P8_ffn")
    LB8 = lay(BA_COLS, h2=32 * 1024, wst=13312)
    F8_h1 = 8192
    F8_x3 = 40960
    LF8 = lay(4096, sq=1024, s1=512, s2=512, xt=1024)
    h1 = lambda o, n: FAB[:, F8_h1 + o:F8_h1 + o + n]
    x3c = lambda o, n: FAB[:, F8_x3 + o:F8_x3 + o + n]

    kb.phase("P8_ffn")

    HT = 1024
    psl = [pA[:], pB[:], pC[:, 0:512], pC[:, 512:1024], pKV[:], pR[:]]

    for half in range(2):
        h0 = half * HT
        kb.wait("sync", sSt, kb.val(sSt))
        kb.wait("sync", sPE)
        for ci in range(8):
            kb.dma("sync", x3c(ci * HT, HT),
                   x3bh[ci * 128:(ci + 1) * 128, h0:h0 + HT], sLd)
        kb.wait("sync", sLd)
        kb.wait("tensor", sLd)

        co = 0
        while co < 32:
            ng = min(3, 32 - co)
            kb.wait("sync", sPE)
            kb.wait("sync", sLd)
            for ci in range(8):
                kb.dma("sync", ba(LB8["wst"] + ci * ng * 128, ng * 128),
                       w0dh[ci * 128:(ci + 1) * 128, co * 128:(co + ng) * 128],
                       sLd)
            kb.wait("sync", sLd)
            kb.wait("tensor", sLd)
            for ci in range(8):
                for g in range(ng):
                    for t2 in range(2):
                        kb.mm(psl[g * 2 + t2],
                              ba(LB8["wst"] + ci * ng * 128 + g * 128, 128),
                              x3c(ci * HT + t2 * 512, 512),
                              ci == 0, ci == 7,
                              inc=sPE if (ci == 7 and g == ng - 1 and t2 == 1)
                              else None)
            kb.wait("scalar", sPE)
            kb.wait("scalar", sDVE)
            for g in range(ng):
                for t2 in range(2):
                    kb.wait("scalar", sDVE)
                    kb.act(FA[:, LF8["s1"]:LF8["s1"] + 512],
                           psl[g * 2 + t2],
                           AF.Relu, scale=sc_sb[:, 0:1], inc=sACT)
                    kb.wait("vector", sACT)
                    kb.v("tensor_copy",
                         h1((co + g) * HT + t2 * 512, 512),
                         FA[:, LF8["s1"]:LF8["s1"] + 512], inc=sDVE)
            kb.wait("tensor", sDVE)
            co += ng

        co = 0
        while co < 32:
            ng = min(3, 32 - co)
            kb.wait("sync", sPE)
            kb.wait("sync", sLd)
            for ci in range(32):
                kb.dma("sync",
                       ba(LB8["wst"] + ci * ng * 128, ng * 128),
                       w1dh[ci * 128:(ci + 1) * 128, co * 128:(co + ng) * 128],
                       sLd)
            kb.wait("sync", sLd)
            kb.wait("tensor", sLd)
            for ci in range(32):
                for g in range(ng):
                    for t2 in range(2):
                        kb.mm(psl[g * 2 + t2],
                              ba(LB8["wst"] + ci * ng * 128 + g * 128, 128),
                              h1(ci * HT + t2 * 512, 512),
                              ci == 0, ci == 31,
                              inc=sPE if (ci == 31 and g == ng - 1 and t2 == 1)
                              else None)
            kb.wait("scalar", sPE)
            kb.wait("scalar", sDVE)
            for g in range(ng):
                for t2 in range(2):
                    kb.wait("scalar", sDVE)
                    kb.act(FA[:, LF8["s1"]:LF8["s1"] + 512],
                           psl[g * 2 + t2],
                           AF.Relu, scale=sc_sb[:, 1:2], inc=sACT)
                    kb.wait("vector", sACT)
                    kb.v("tensor_copy",
                         ba(LB8["h2"] + (co + g) * HT + t2 * 512, 512),
                         FA[:, LF8["s1"]:LF8["s1"] + 512], inc=sDVE)
            kb.wait("tensor", sDVE)
            co += ng

        for ttg in range(0, 8, 3):
            ntt = min(3, 8 - ttg)
            for cig in range(0, 32, 11):
                nci = min(11, 32 - cig)
                kb.wait("sync", sPE)
                kb.wait("sync", sLd)
                for cl in range(nci):
                    kb.dma("sync", ba(LB8["wst"] + cl * D, D),
                           w2dh[(cig + cl) * 128:(cig + cl + 1) * 128, :], sLd)
                kb.wait("sync", sLd)
                kb.wait("tensor", sLd)
                for cl in range(nci):
                    ci = cig + cl
                    for tt in range(ntt):
                        for coh in range(2):
                            kb.mm(psl[tt * 2 + coh],
                                  ba(LB8["h2"] + ci * HT + (ttg + tt) * 128, 128),
                                  ba(LB8["wst"] + cl * D + coh * 512, 512),
                                  ci == 0, ci == 31,
                                  inc=sPE if (cl == nci - 1 and tt == ntt - 1
                                              and coh == 1) else None)
            kb.wait("scalar", sPE)
            for tt in range(ntt):
                gi = half * 8 + ttg + tt
                kb.wait("sync", sDVE)
                kb.wait("sync", sSt, kb.val(sSt))
                kb.dma("sync", FA[:, LF8["xt"]:LF8["xt"] + D],
                       x3fh[gi * 128:(gi + 1) * 128, :], sLd)
                kb.wait("sync", sLd)
                kb.wait("scalar", sDVE)
                for coh in range(2):
                    kb.act(FA[:, (LF8["s1"] if coh == 0 else LF8["s2"]):
                              (LF8["s1"] if coh == 0 else LF8["s2"]) + 512],
                           psl[tt * 2 + coh],
                           AF.Identity, scale=sc_sb[:, 2:3], inc=sACT)
                kb.wait("vector", sACT)
                kb.wait("vector", sLd)
                kb.v("tensor_add", FA[:, LF8["xt"]:LF8["xt"] + 512],
                     FA[:, LF8["xt"]:LF8["xt"] + 512],
                     FA[:, LF8["s1"]:LF8["s1"] + 512], inc=sDVE)
                kb.op("vector", lambda e, o=FA[:, LF8["xt"] + 512:LF8["xt"] + 1024],
                      a=FA[:, LF8["xt"] + 512:LF8["xt"] + 1024],
                      b=FA[:, LF8["s2"]:LF8["s2"] + 512]:
                      e.tensor_add(o, a, b), inc=sDVE)
                emit_layernorm_tm(kb, lambda _i: FA[:, LF8["xt"]:LF8["xt"] + D],
                                  FA[:, LF8["sq"]:LF8["sq"] + D], cols, 1, D,
                                  sems)
                kb.wait("gpsimd", sACT)
                kb.wait("gpsimd", sSt)
                kb.dma("gpsimd", xo[gi * 128:(gi + 1) * 128, :],
                       FA[:, LF8["xt"]:LF8["xt"] + D], sSt)
            kb.wait("tensor", sDVE)
            kb.wait("tensor", sACT)
    kb.op("gpsimd", lambda e: e.memset(cols[0:1, 0:1], 0.0),
          waits=[(sSt, kb.val(sSt))], selfwait=False)

    kb.emit()
    return kb


def build_moe_g(kb, drams, mem, psums, sems, consts):
    (gwT, eb_r, w13br_t, w2br_t, xsort, osort, x2f) = drams
    WX = 8 * HID
    FA, BA, FAB, LF = mem
    pA, pB, pC, pKV, pR, pTF, pT = psums
    sLd, sCast, sPE, sDVE, sACT, sSt, sC = sems
    (ones1, idf, idb, cols, ltri, iot, sidxu, rsel, gsc, kcon,
     sel4) = consts
    fa = lambda o, n: FA[:, o:o + n]

    GS = LF["gsm"]
    GB, BBC, SQ2, GEM, XR = (LF["gb"], LF["bbc"], LF["sq"], LF["gem"],
                             LF["xr"])
    gcol = FA[:, GS:GS + 8]
    key = FA[:, GS + 8:GS + 16]
    msk = FA[:, GS + 16:GS + 24]
    ebb = FA[:, GS + 24:GS + 32]
    neg = FA[:, GS + 32:GS + 40]
    gtm = FA[:, GS + 40:GS + 48]
    ebr = FA[0:1, GS + 48:GS + 56]

    gw_f = FA[:, GS + 64:GS + 64 + 64]
    kb.wait("sync", sLd)
    kb.dma("sync", ebr, eb_r, sLd)
    for ci in range(8):
        kb.dma("sync", FA[:, GS + 64 + ci * NE:GS + 64 + (ci + 1) * NE],
               gwT[ci * 128:(ci + 1) * 128, :], sLd)
    kb.wait("sync", sLd)
    kb.v("memset", neg[:], -1e30, inc=sDVE)
    kb.wait("tensor", sLd)
    kb.wait("tensor", sDVE)
    kb.mm(pR[:, 0:8], ones1[:, :], ebr, True, True, inc=sPE)
    kb.wait("vector", sPE)
    kb.v("tensor_copy", ebb[:], pR[:, 0:8], inc=sDVE)
    kb.wait("tensor", sDVE)
    for tt in range(16):
        for ci in range(8):
            kb.trans(pTF[:], FA[:, XR + tt * D + ci * 128:XR + tt * D + ci * 128 + 128],
                     idf[:], inc=sPE)
            kb.wait("vector", sPE)
            kb.op("vector", lambda e, o=FA[:, LF["x1cf"] + ci * 128:LF["x1cf"] + (ci + 1) * 128],
                  p=pTF[:]: e.tensor_copy(o, p), inc=sDVE)
            kb.wait("tensor", sDVE)
        for ci in range(8):
            kb.mm(pR[:, 0:8],
                  FA[:, LF["x1cf"] + ci * 128:LF["x1cf"] + (ci + 1) * 128],
                  gw_f[:, ci * NE:(ci + 1) * NE],
                  ci == 0, ci == 7, inc=sPE if ci == 7 else None)
        kb.wait("vector", sPE)
        kb.op("vector", lambda e, o=gcol[:, 0:1], p=pR[:, 0:8]:
              e.reduce_max(o, p, AX.X), inc=sDVE)
        kb.v("tensor_scalar_mul", gcol[:, 1:2], gcol[:, 0:1], -1.0, inc=sDVE)
        kb.wait("scalar", sDVE)
        kb.op("scalar", lambda e, o=gtm[:], p=pR[:, 0:8], b=gcol[:, 1:2]:
              e.activation(o, p, AF.Exp, bias=b), inc=sACT)
        kb.wait("vector", sACT)
        kb.wait("tensor", sACT)
        kb.op("vector", lambda e, o=gcol[:, 2:3], i=gtm[:]:
              e.reduce_sum(o, i, AX.X), inc=sDVE)
        kb.v("reciprocal", gcol[:, 3:4], gcol[:, 2:3], inc=sDVE)
        kb.op("vector", lambda e, o=gtm[:], i=gtm[:], m=gcol[:, 3:4]:
              e.tensor_scalar_mul(o, i, m), inc=sDVE)
        kb.v("tensor_add", key[:], gtm[:], ebb[:], inc=sDVE)
        kb.op("vector", lambda e, o=gcol[:, 4:5], i=key[:]:
              e.reduce_max(o, i, AX.X), inc=sDVE)
        kb.op("vector", lambda e, o=msk[:], i=key[:], m=gcol[:, 4:5]:
              e.tensor_single_scalar(o, i, m, ALU.is_equal), inc=sDVE)
        kb.v("tensor_scalar_mul", msk[:], msk[:], -1e30, inc=sDVE)
        kb.v("tensor_add", msk[:], msk[:], key[:], inc=sDVE)
        kb.op("vector", lambda e, o=gcol[:, 5:6], i=msk[:]:
              e.reduce_max(o, i, AX.X), inc=sDVE)
        kb.op("vector", lambda e, o=key[:], i=key[:], m=gcol[:, 5:6]:
              e.tensor_single_scalar(o, i, m, ALU.is_ge), inc=sDVE)
        kb.v("tensor_mul", gtm[:], gtm[:], key[:], inc=sDVE)
        kb.wait("tensor", sDVE)
        kb.trans(pTF[0:8, :], gtm[:], idf[:], inc=sPE)
        kb.wait("vector", sPE)
        kb.op("vector",
              lambda e, o=FA[0:8, GEM + tt * 128:GEM + (tt + 1) * 128],
              p=pTF[0:8, :]: e.tensor_copy(o, p), inc=sDVE)
        kb.wait("tensor", sDVE)

    if CUT == 1:
        return
    kb.wait("vector", sACT)
    kb.v("memset", BA[:, 0:1024], 0.0, inc=sDVE)
    kb.wait("scalar", sDVE)
    for j in range(48):
        kb.dma("scalar", xsort[j * 128:(j + 1) * 128, :], BA[:, 0:1024], sSt)
    kb.wait("scalar", sACT)
    for i in range(16):
        kb.dma("scalar", x2f[i * 128:(i + 1) * 128, :],
               fa(XR + i * D, D), sSt)
    spill_done = kb.val(sSt)

    HT2 = 1024
    cnt = FA[0:8, GS:GS + 1]
    pc_ = FA[0:8, GS + 1:GS + 2]
    off = FA[0:8, GS + 2:GS + 3]
    om1 = FA[0:8, GS + 3:GS + 4]
    o81 = FA[0:8, GS + 4:GS + 5]
    carry = FA[0:8, GS + 5:GS + 6]
    kb.v("memset", o81, 1.0, inc=sDVE)
    kb.v("memset", FA[0:8, GB + HT2:GB + 2 * HT2], 1.0, inc=sDVE)
    for h in range(2):
        gemH = FA[0:8, GEM + h * HT2:GEM + (h + 1) * HT2]
        posH = FA[0:8, BBC + h * HT2:BBC + (h + 1) * HT2]
        mH = FA[0:8, GB:GB + HT2]
        kb.v("tensor_single_scalar", mH, gemH, 0.0, ALU.is_gt, inc=sDVE)
        init = 0.0 if h == 0 else carry
        kb.op("vector", lambda e, o=posH, a=FA[0:8, GB + HT2:GB + 2 * HT2],
              b=mH, i=init:
              e.tensor_tensor_scan(o, a, b, i, ALU.mult, ALU.add), inc=sDVE)
        if h == 0:
            kb.v("tensor_copy", carry, FA[0:8, BBC + HT2 - 1:BBC + HT2],
                 inc=sDVE)
    kb.v("tensor_copy", cnt, FA[0:8, BBC + 2 * HT2 - 1:BBC + 2 * HT2],
         inc=sDVE)
    kb.v("tensor_single_scalar", FA[0:8, GS + 8:GS + 16], iot[0:8, 0:8],
         cnt, ALU.is_lt, inc=sDVE)
    kb.v("reduce_sum", pc_, FA[0:8, GS + 8:GS + 16], AX.X, inc=sDVE)
    kb.v("tensor_scalar_mul", pc_, pc_, 256.0, inc=sDVE)
    kb.wait("tensor", sDVE)
    kb.wait("tensor", sACT)
    kb.mm(pR[0:8, 0:1], ltri[0:8, 0:8], pc_, True, True, inc=sPE)
    kb.wait("vector", sPE)
    kb.v("tensor_copy", off, pR[0:8, 0:1], inc=sDVE)
    kb.v("tensor_scalar_sub", om1, off, 1.0, inc=sDVE)
    kb.v("tensor_single_scalar", FA[0:8, BBC:BBC + 2 * HT2],
         FA[0:8, BBC:BBC + 2 * HT2], om1, ALU.add, inc=sDVE)
    kb.v("tensor_single_scalar", FA[0:8, GS + 8:GS + 32], iot[0:8, 16:40],
         off, ALU.is_ge, inc=sDVE)
    kb.wait("tensor", sDVE)
    kb.mm(pR[0:1, 0:24], o81, FA[0:8, GS + 8:GS + 32], True, True, inc=sPE)
    kb.wait("vector", sPE)
    kb.v("tensor_copy", FA[0:1, GS + 48:GS + 72], pR[0:1, 0:24], inc=sDVE)
    kb.v("tensor_scalar_sub", FA[0:1, GS + 48:GS + 72],
         FA[0:1, GS + 48:GS + 72], 1.0, inc=sDVE)
    kb.v("memset", FA[0:1, GS + 88:GS + 89], -1.0, inc=sDVE)
    kb.v("tensor_copy", FA[0:1, GS + 89:GS + 112], FA[0:1, GS + 48:GS + 71],
         inc=sDVE)
    kb.op("vector", lambda e, o=FA[0:1, GS + 88:GS + 112],
          a=FA[0:1, GS + 48:GS + 72], b=FA[0:1, GS + 88:GS + 112]:
          e.tensor_tensor(o, a, b, ALU.is_equal), inc=sDVE)
    kb.v("tensor_scalar_mul", FA[0:1, GS + 8:GS + 32],
         FA[0:1, GS + 48:GS + 72], float(128 * 2 * WX), inc=sDVE)
    kb.stt(FA[0:1, GS + 8:GS + 32], FA[0:1, GS + 88:GS + 112], 67108864.0,
           FA[0:1, GS + 8:GS + 32], ALU.mult, ALU.add, inc=sDVE)
    kb.v("tensor_copy", rsel[0:1, 0:24], FA[0:1, GS + 8:GS + 32], inc=sDVE)
    kb.v("tensor_scalar_mul", FA[0:1, GS + 8:GS + 32],
         FA[0:1, GS + 48:GS + 72], float(128 * WX), inc=sDVE)
    kb.stt(FA[0:1, GS + 8:GS + 32], FA[0:1, GS + 88:GS + 112], 33554432.0,
           FA[0:1, GS + 8:GS + 32], ALU.mult, ALU.add, inc=sDVE)
    kb.v("tensor_copy", rsel[0:1, 24:48], FA[0:1, GS + 8:GS + 32],
         inc=sDVE)

    for h in range(2):
        gemH = FA[0:8, GEM + h * HT2:GEM + (h + 1) * HT2]
        pposH = FA[0:8, BBC + h * HT2:BBC + (h + 1) * HT2]
        mH = FA[0:8, GB:GB + HT2]
        cumH = FA[0:8, GB + HT2:GB + 2 * HT2]
        t1H = FA[0:8, SQ2:SQ2 + HT2]
        kb.wait("vector", sPE)
        kb.v("tensor_single_scalar", mH, gemH, 0.0, ALU.is_gt, inc=sDVE)
        for q in range(2):
            kb.wait("tensor", sDVE)
            kb.mm(pR[0:8, 0:512], ltri[0:8, 8:16],
                  FA[0:8, GB + q * 512:GB + (q + 1) * 512], True, True,
                  inc=sPE)
            kb.wait("vector", sPE)
            kb.v("tensor_copy",
                 FA[0:8, GB + HT2 + q * 512:GB + HT2 + (q + 1) * 512],
                 pR[0:8, 0:512], inc=sDVE)
        kb.v("tensor_single_scalar", t1H, cumH, kcon[0:8, 4:5],
             ALU.is_equal, inc=sDVE)
        kb.v("tensor_mul", t1H, t1H, mH, inc=sDVE)
        kb.v("tensor_single_scalar", cumH, cumH, kcon[0:8, 3:4],
             ALU.is_equal, inc=sDVE)
        kb.v("tensor_mul", cumH, cumH, mH, inc=sDVE)
        for r, (a_, b_) in enumerate(((cumH, pposH), (t1H, pposH),
                                      (cumH, gemH), (t1H, gemH))):
            kb.wait("vector", sPE)
            kb.op("vector", lambda e, o=mH, a=a_, b=b_:
                  e.tensor_mul(o, a, b), inc=sDVE)
            kb.wait("tensor", sDVE)
            for q in range(2):
                kb.mm(pC[:, q * 512:(q + 1) * 512],
                      sel4[0:8, r * 128:(r + 1) * 128],
                      FA[0:8, GB + q * 512:GB + (q + 1) * 512],
                      r == 0, r == 3, inc=sPE if q == 1 else None)
        kb.wait("vector", sPE)
        kb.v("tensor_copy", FA[:, GEM + 2 * HT2:GEM + 3 * HT2], pC[:, 0:1024],
             inc=sDVE)
        for kk in range(8):
            kcoli = h * 8 + kk
            kb.wait("tensor", sDVE)
            kb.trans(pTF[:], FA[:, GEM + 2 * HT2 + kk * 128:GEM + 2 * HT2 + (kk + 1) * 128],
                     idf[:], inc=sPE)
            kb.wait("vector", sPE)
            kb.v("tensor_copy", sidxu[:, kcoli:kcoli + 1], pTF[:, 0:1],
                 inc=sDVE)
            kb.v("tensor_copy", sidxu[:, 16 + kcoli:17 + kcoli],
                 pTF[:, 32:33], inc=sDVE)
            kb.v("tensor_copy", gsc[:, kcoli:kcoli + 1], pTF[:, 64:65],
                 inc=sDVE)
            kb.v("tensor_copy", gsc[:, 16 + kcoli:17 + kcoli], pTF[:, 96:97],
                 inc=sDVE)
        kb.wait("tensor", sDVE)

    kb.wait("gpsimd", sSt, spill_done)
    kb.wait("gpsimd", sDVE)
    wreg0 = {}
    for k in range(16):
        kb.wait("scalar", sSt, kb.val(sSt))
        kb.wait("scalar", sDVE)
        kb.act(BA[:, 0:1024], fa(XR + k * D, D), AF.Identity,
               scale=gsc[:, k:k + 1], inc=sACT)
        kb.act(BA[:, 1024:2048], fa(XR + k * D, D), AF.Identity,
               scale=gsc[:, 16 + k:17 + k], inc=sACT)
        kb.wait("gpsimd", sACT)
        for (six0, src) in ((k, BA[:, 0:1024]), (16 + k, BA[:, 1024:2048])):
            def scat_x(e, c=six0, s=src):
                if "x" not in wreg0:
                    wreg0["x"] = e.register("xbnd").__enter__()
                    e.reg_mov(wreg0["x"], 24 * 256 - 1)
                e.indirect_dma_start(
                    out=xsort,
                    out_offset=bass.IndirectOffsetOnAxis(
                        ap=sidxu[:, c:c + 1], axis=0),
                    in_=s, in_offset=None, bounds_check=wreg0["x"],
                    oob_is_err=False).then_inc(sSt, 16)
            kb.op_raw("gpsimd", scat_x, incs={sSt: 16})

    if CUT == 2:
        return
    if DBG:
        dbg_sidxu = kb.nc.dram_tensor("dbg_sidxu", [128, 32],
                                      mybir.dt.uint32,
                                      kind="ExternalOutput").ap()
        dbg_gsc = kb.nc.dram_tensor("dbg_gsc", [128, 32], F32,
                                    kind="ExternalOutput").ap()
        dbg_rsel = kb.nc.dram_tensor("dbg_rsel", [1, 48], mybir.dt.int32,
                                     kind="ExternalOutput").ap()
        dbg_cols = kb.nc.dram_tensor("dbg_cols", [8, 8], F32,
                                     kind="ExternalOutput").ap()
        dbg_gem = kb.nc.dram_tensor("dbg_gem", [8, T], F32,
                                    kind="ExternalOutput").ap()
        dbg_xs = kb.nc.dram_tensor("dbg_xs", [24 * 256, D], BF16,
                                   kind="ExternalOutput").ap()
        dbg_x1 = kb.nc.dram_tensor("dbg_x1", [T, D], F32,
                                   kind="ExternalOutput").ap()
        dbg_os = kb.nc.dram_tensor("dbg_os", [24 * 256, D], F32,
                                   kind="ExternalOutput").ap()
        dbg_x2 = kb.nc.dram_tensor("dbg_x2", [T, D], F32,
                                   kind="ExternalOutput").ap()
        kb.wait("scalar", sDVE)
        kb.wait("scalar", sSt, kb.val(sSt))
        kb.dma("scalar", dbg_sidxu[:, 0:32], sidxu[:, 0:32], sSt)
        kb.dma("scalar", dbg_gsc, gsc[:], sSt)
        kb.dma("scalar", dbg_rsel, rsel[0:1, :], sSt)
        kb.dma("scalar", dbg_cols, FA[0:8, GS:GS + 8], sSt)
        kb.dma("scalar", dbg_gem, FA[0:8, GEM:GEM + T], sSt)
        kb.wait("scalar", sSt)

    FB_xt, FB_xs, FB_h, FB_hT = 22528, 23552, 25600, 28416
    OST = 16384
    S1F = LF["s1"]
    HCC = ((0, 512), (512, 1024), (1024, 1536), (1536, 2048), (2048, 2560),
           (2560, 2816))
    mm_prev = kb.val(sPE)
    if DBG:
        kb.wait("scalar", sSt, kb.val(sSt))
        for jj in range(48):
            kb.dma("scalar", dbg_xs[jj * 128:(jj + 1) * 128, :],
                   xsort[jj * 128:(jj + 1) * 128, :], sSt)
        kb.dma("scalar", dbg_x1, x2f, sSt)
        kb.wait("scalar", sSt)
    silu_done = {}
    xtr_done = [0, 0]
    wreg = {}
    pat13 = [[2 * WX, 128], [WX, 2], [1, WX]]
    pat2 = [[WX, 128], [1, WX]]
    for j in range(24):
        par = j % 2
        if j == 0:
            kb.wait("sync", sSt, kb.val(sSt))
            kb.wait("sync", sACT)
            kb.wait("sync", sDVE)
            kb.wait("sync", sC[5], 1)
            kb.wait("sync", sC[6], 1)
        kb.wait("sync", sPE, mm_prev)

        def wload13(e, jj=j):
            if "s" not in wreg:
                wreg["s"] = e.register("wsel_s").__enter__()
            r = wreg["s"]
            e.reg_load(r, rsel[0:1, jj:jj + 1])
            e.dma_start(out=BA[:, 0:2 * WX], in_=bass.AP(w13br_t, r, pat13),
                        bounds_check="skip_entire_dma").then_inc(sLd, 16)
        if CUT != 3:
            kb.op_raw("sync", wload13, incs={sLd: 16})
        kb.wait("scalar", sPE, mm_prev)
        kb.wait("scalar", sDVE)

        def wload2(e, jj=j):
            if "a" not in wreg:
                wreg["a"] = e.register("wsel_a").__enter__()
            r = wreg["a"]
            e.reg_load(r, rsel[0:1, 24 + jj:25 + jj])
            e.dma_start(out=FAB[:, 0:22528], in_=bass.AP(w2br_t, r, pat2),
                        bounds_check="skip_entire_dma").then_inc(sLd, 16)
        if CUT != 3:
            kb.op_raw("scalar", wload2, incs={sLd: 16})
        wld_j = kb.val(sLd)

        for st_ in range(2):
            sp = (2 * j + st_) % 2
            kb.wait("sync", sPE, xtr_done[sp])
            kb.dma("sync", FAB[:, FB_xs + sp * 1024:FB_xs + (sp + 1) * 1024],
                   xsort[j * 256 + st_ * 128:j * 256 + (st_ + 1) * 128, :],
                   sLd)
            kb.wait("sync", sLd)
            ldx = kb.val(sLd)
            dvev = []
            kb.wait("tensor", sLd, ldx)
            tslots = [pA.bitcast(BF16), pB.bitcast(BF16),
                      pKV.bitcast(BF16), pR.bitcast(BF16)]
            kb.wait("tensor", sDVE)
            kb.wait("tensor", sACT)
            for ci in range(8):
                ph = ci % 4
                if ci >= 4:
                    kb.wait("tensor", sDVE, dvev[ci - 4])
                kb.trans(tslots[ph][:, 0:128],
                         FAB[:, FB_xs + sp * 1024 + ci * 128:FB_xs + sp * 1024 + (ci + 1) * 128],
                         idb[:], inc=sPE)
                kb.wait("vector", sPE)
                if ci == 0:
                    kb.wait("vector", sPE, mm_prev)
                kb.v("tensor_copy",
                     FAB[:, FB_xt + ci * 128:FB_xt + (ci + 1) * 128],
                     tslots[ph][:, 0:128], inc=sDVE)
                dvev.append(kb.val(sDVE))
            xtr_done[sp] = kb.val(sPE)

            kb.wait("tensor", sLd, wld_j)
            kb.wait("tensor", sDVE)
            for hci, (h0, h1) in enumerate(HCC):
                wdt = h1 - h0
                pw1 = (pA, pKV)[hci % 2]
                pw3 = (pB, pR)[hci % 2]
                if (j, st_, hci - 2) in silu_done:
                    kb.wait("tensor", sDVE, silu_done[(j, st_, hci - 2)])
                for ci in range(8):
                    kb.mm(pw1[:, 0:wdt],
                          FAB[:, FB_xt + ci * 128:FB_xt + (ci + 1) * 128],
                          BA[:, ci * HID + h0:ci * HID + h1],
                          ci == 0, ci == 7, inc=sPE if ci == 7 else None)
                for ci in range(8):
                    kb.mm(pw3[:, 0:wdt],
                          FAB[:, FB_xt + ci * 128:FB_xt + (ci + 1) * 128],
                          BA[:, WX + ci * HID + h0:WX + ci * HID + h1],
                          ci == 0, ci == 7, inc=sPE if ci == 7 else None)
                kb.wait("scalar", sPE)
                kb.wait("scalar", sDVE)
                kb.act(FA[:, S1F:S1F + wdt], pw1[:, 0:wdt], AF.Silu, inc=sACT)
                kb.wait("vector", sACT)
                kb.op("vector", lambda e, o=FAB[:, FB_h + h0:FB_h + h1],
                      a=FA[:, S1F:S1F + wdt], b=pw3[:, 0:wdt]:
                      e.tensor_mul(o, a, b), inc=sDVE)
                silu_done[(j, st_, hci)] = kb.val(sDVE)
            actv = []
            kb.wait("tensor", sDVE)
            kb.wait("tensor", sACT)
            for kkk in range(22):
                ph = kkk % 4
                if kkk >= 4:
                    kb.wait("tensor", sACT, actv[kkk - 4])
                kb.trans(tslots[ph][:, 0:128],
                         FAB[:, FB_h + kkk * 128:FB_h + (kkk + 1) * 128],
                         idb[:], inc=sPE)
                kb.wait("scalar", sPE)
                kb.act(FAB[:, FB_hT + kkk * 128:FB_hT + (kkk + 1) * 128],
                       tslots[ph][:, 0:128], AF.Identity, inc=sACT)
                actv.append(kb.val(sACT))
            kb.wait("tensor", sACT)
            kb.wait("tensor", sDVE)
            for co in range(2):
                for kkk in range(22):
                    kb.mm(pC[:, co * 512:(co + 1) * 512],
                          FAB[:, FB_hT + kkk * 128:FB_hT + (kkk + 1) * 128],
                          FAB[:, kkk * 1024 + co * 512:kkk * 1024 + co * 512 + 512],
                          kkk == 0, kkk == 21,
                          inc=sPE if kkk == 21 else None)
            kb.wait("vector", sPE)
            kb.wait("vector", sSt, kb.val(sSt))
            kb.v("tensor_copy", fa(OST, 512), pC[:, 0:512], inc=sDVE)
            kb.v("tensor_copy", fa(OST + 512, 512), pC[:, 512:1024], inc=sDVE)
            kb.wait("scalar", sDVE)
            kb.dma("scalar",
                   osort[j * 256 + st_ * 128:j * 256 + (st_ + 1) * 128, :],
                   fa(OST, 1024), sSt)
            mm_prev = kb.val(sPE)

    if DBG:
        kb.wait("scalar", sSt, kb.val(sSt))
        kb.dma("scalar", dbg_os, osort, sSt)
        kb.wait("scalar", sSt)
    kb.fence()
    for i in range(16):
        kb.wait("sync", sLd)
        kb.wait("sync", sDVE)
        kb.dma("sync", fa(XR + i * D, D), x2f[i * 128:(i + 1) * 128, :], sLd)
        kb.wait("gpsimd", sLd)
        kb.wait("gpsimd", sDVE)
        kb.op_raw("gpsimd",
                  lambda e, c=i:
                  e.indirect_dma_start(
                      out=FA[:, OST:OST + 1024], out_offset=None,
                      in_=osort,
                      in_offset=bass.IndirectOffsetOnAxis(
                          ap=sidxu[:, c:c + 1], axis=0)).then_inc(sLd, 16),
                  incs={sLd: 16})
        kb.op_raw("gpsimd",
                  lambda e, c=16 + i:
                  e.indirect_dma_start(
                      out=FA[:, OST + 1024:OST + 2048], out_offset=None,
                      in_=osort,
                      in_offset=bass.IndirectOffsetOnAxis(
                          ap=sidxu[:, c:c + 1], axis=0)).then_inc(sLd, 16),
                  incs={sLd: 16})
        kb.wait("vector", sLd)
        kb.v("tensor_add", fa(XR + i * D, D), fa(XR + i * D, D),
             fa(OST, 1024), inc=sDVE)
        kb.v("tensor_add", fa(XR + i * D, D), fa(XR + i * D, D),
             fa(OST + 1024, 1024), inc=sDVE)
        if DBG:
            kb.wait("gpsimd", sDVE)
            kb.dma("gpsimd", dbg_x2[i * 128:(i + 1) * 128, :],
                   fa(XR + i * D, D), sSt)
    kb.wait("vector", sDVE)



_CACHE = {}


def _get(name, builder):
    if name not in _CACHE:
        _CACHE[name] = builder().nc
    return _CACHE[name]


def _sel4():
    s = np.zeros((8, 512), np.float32)
    for r, rr in enumerate((0, 32, 64, 96)):
        s[:, r * 128 + rr] = 1.0
    return s


def _col4(v):
    return np.ascontiguousarray(np.asarray(v, np.float32).reshape(4, 128).T)


def _in_maps(inputs):
    x = np.asarray(inputs["L0_attn_x"])
    w_in = np.asarray(inputs["L1_attn_W_in"], np.float32)
    cw = np.asarray(inputs["L1_attn_conv_w"], np.float32)
    cb = np.asarray(inputs["L1_attn_conv_b"], np.float32)
    w_x = np.asarray(inputs["L1_attn_W_x"], np.float32)
    w_dt = np.asarray(inputs["L1_attn_W_dt"], np.float32)
    b_dt = np.asarray(inputs["L1_attn_b_dt"], np.float32)
    alog = np.asarray(inputs["L1_attn_A_log"], np.float32)
    dd = np.asarray(inputs["L1_attn_D"], np.float32)
    w_out = np.asarray(inputs["L1_attn_W_out"], np.float32)
    k0 = np.asarray(inputs["L1_ffn_k0"]).view(np.uint8)
    k1 = np.ascontiguousarray(np.asarray(inputs["L1_ffn_k1"]).view(np.uint8))
    k2 = np.ascontiguousarray(np.asarray(inputs["L1_ffn_k2"]).view(np.uint8))
    s0 = np.asarray(inputs["L1_ffn_s0"], np.float32).reshape(())
    s1 = np.asarray(inputs["L1_ffn_s1"], np.float32).reshape(())
    s2 = np.asarray(inputs["L1_ffn_s2"], np.float32).reshape(())
    scol = np.tile(np.stack([s0, s1, s2]).reshape(1, 3), (128, 1))
    base = {
        "wq": np.ascontiguousarray(inputs["L0_attn_Wq"], np.float32),
        "wk": np.ascontiguousarray(inputs["L0_attn_Wk"], np.float32),
        "wv": np.ascontiguousarray(inputs["L0_attn_Wv"], np.float32),
        "wp": np.ascontiguousarray(inputs["L0_attn_Wp"], np.float32),
        "bq_c": np.ascontiguousarray(
            np.asarray(inputs["L0_attn_bq"], np.float32).reshape(8, 128).T),
        "bk_r": np.asarray(inputs["L0_attn_bk"], np.float32).reshape(1, D),
        "bv_r": np.asarray(inputs["L0_attn_bv"], np.float32).reshape(1, D),
        "bp_r": np.asarray(inputs["L0_attn_bp"], np.float32).reshape(1, D),
        "id128b": np.eye(128, dtype=np.float32).astype(BF),
        "idf": np.eye(128, dtype=np.float32),
        "ckon": np.tile(np.array([list(CONSTS)], np.float32), (128, 1)),
        "ones_d": np.ones((1, 128), np.float32),
        "gwT": np.ascontiguousarray(
            np.asarray(inputs["L0_ffn_gate_w"], np.float32).T),
        "eb_r": np.asarray(
            inputs["L0_ffn_expert_bias"], np.float32).reshape(1, NE),
        "ltri2": np.concatenate(
            [np.triu(np.ones((8, 8), np.float32), 1),
             np.triu(np.ones((8, 8), np.float32), 0)], axis=1),
        "iotas": np.concatenate(
            [np.tile(np.arange(16, dtype=np.float32) * 256, (8, 1)),
             np.tile(np.arange(40, dtype=np.float32) * 256, (8, 1))], axis=1),

        "sel4_d": _sel4(),
        "lng": np.asarray(inputs["L1_attn_ln_g"], np.float32).reshape(1, D),
        "lnb": np.asarray(inputs["L1_attn_ln_b"], np.float32).reshape(1, D),
        "k0t": np.ascontiguousarray(k0[:D]),
        "k0b": np.ascontiguousarray(k0[D:]),
        "k1": k1, "k2": k2,
        "scol": np.ascontiguousarray(scol),
    }
    w1f = np.asarray(inputs["L0_ffn_w1"], np.float32)
    w2f = np.asarray(inputs["L0_ffn_w2"], np.float32)
    w3f = np.asarray(inputs["L0_ffn_w3"], np.float32)
    maps = []
    for c in range(NCORES):
        b, s = c // 4, c % 4
        sl = slice(s * 512, (s + 1) * 512)
        m = dict(base)
        m["w1own"] = np.ascontiguousarray(w1f[c])
        m["w2own"] = np.ascontiguousarray(w2f[c])
        m["w3own"] = np.ascontiguousarray(w3f[c])
        m["xk"] = np.ascontiguousarray(
            np.asarray(x[b, s * T:(s + 1) * T], dtype=np.float32))
        m["winx"] = np.ascontiguousarray(w_in[sl, :].T)
        m["winz"] = np.ascontiguousarray(
            w_in[2048 + s * 512:2048 + (s + 1) * 512, :].T)
        m["cw"] = np.ascontiguousarray(
            cw[sl, 0, :].reshape(4, 128, 4).transpose(1, 0, 2).reshape(128, 16))
        m["cb"] = _col4(cb[sl])
        wsl = np.concatenate([w_x[sl], w_x[2048:2080]], axis=0)
        m["wxT"] = np.ascontiguousarray(wsl.T)
        m["wdtT"] = np.ascontiguousarray(w_dt[sl, :].T)
        m["bdt_c"] = _col4(b_dt[sl])
        m["alog_c"] = np.ascontiguousarray(
            alog[sl, :].reshape(4, 128, 16).transpose(1, 0, 2).reshape(128, 64))
        m["dcol"] = _col4(dd[sl])
        m["woT"] = np.ascontiguousarray(w_out[:, sl].T)
        maps.append(m)
    return maps


def kernel(**inputs):
    nc = _get("fused", build_fused)
    res = run_bass_kernel_spmd(nc, _in_maps(inputs),
                               core_ids=list(range(NCORES))).results
    out = np.empty((2, TB, D), np.float32)
    for c in range(NCORES):
        b, s = c // 4, c % 4
        out[b, s * T:(s + 1) * T] = res[c]["xo"]
    return out


# version 10 (final)
# speedup vs baseline: 1.0249x; 1.0144x over previous
import contextlib

import numpy as np
import ml_dtypes

import concourse.bass as bass
import concourse.mybir as mybir
from concourse.bass_utils import run_bass_kernel_spmd

F32 = mybir.dt.float32
BF16 = mybir.dt.bfloat16
U8 = mybir.dt.uint8
AF = mybir.ActivationFunctionType
ALU = mybir.AluOpType
AX = mybir.AxisListType
BF = ml_dtypes.bfloat16

CONSTS = (1e-5, -0.5, -1.0, 1.0, 2.0, 0.5, 3.0, 4.0)
DBG = False
CUT = 0

NCORES = 8
T = 2048
TB = 8192
D = 1024
NH = 16
NE = 8
HID = 2816
DI = 2048
DIC = 512
DS = 16

GROUPS = [[0, 1, 2, 3], [4, 5, 6, 7]]

ENGINES = ("sync", "tensor", "vector", "scalar", "gpsimd")


class KB:

    def __init__(self):
        self.nc = bass.Bass("TRN2", target_bir_lowering=False, debug=False)
        self.stack = contextlib.ExitStack()
        self.plan = []
        self.semv = {}
        self._pend = {e: [] for e in ENGINES}
        self._phase = "P0"

    def phase(self, name):
        self._phase = name

    def sem(self, name):
        h = self.stack.enter_context(self.nc.semaphore(name))
        self.semv[h] = 0
        return h

    def sbuf(self, name, shape, dt):
        return self.stack.enter_context(self.nc.sbuf_tensor(name, shape, dt))

    def psum(self, name, shape, dt=F32):
        return self.stack.enter_context(self.nc.psum_tensor(name, shape, dt))

    def op(self, eng, fn, waits=(), inc=None, n=1, selfwait=None):
        w = list(self._pend[eng]) + list(waits)
        self._pend[eng] = []
        if selfwait is None:
            selfwait = eng in ("vector", "scalar", "gpsimd")
        if selfwait and inc is not None and self.semv.get(inc, 0) > 0:
            w.append((inc, self.semv[inc]))
        incs = []
        if inc is not None:
            self.semv[inc] += n
            incs = [(inc, n)]
        self.plan.append((eng, fn, w, incs, self._phase))

    def wait(self, eng, sem, val=None):
        self._pend[eng].append((sem, self.semv[sem] if val is None else val))

    def val(self, s):
        return self.semv[s]

    def fence(self):
        snap = [(s, v) for s, v in self.semv.items() if v > 0]
        for e in ENGINES:
            self._pend[e] = self._pend[e] + list(snap)

    def dma(self, q, out, in_, sem, waits=(), **kw):
        self.op(q, lambda e, o=out, i=in_, kw=kw: e.dma_start(out=o, in_=i, **kw),
                waits, inc=sem, n=16, selfwait=False)

    def cc(self, kind, op_, ins, outs, sem, waits=(), groups=None):
        g = GROUPS if groups is None else groups
        self.op("gpsimd",
                lambda e, k=kind, o=op_, i=ins, u=outs, g=g:
                e.collective_compute(k, o, replica_groups=g,
                                     ins=[i.opt()], outs=[u.opt()]),
                waits, inc=sem, n=1, selfwait=False)

    def op_raw(self, eng, fn, waits=(), incs=None):
        w = list(self._pend[eng]) + list(waits)
        self._pend[eng] = []
        for s, n in (incs or {}).items():
            self.semv[s] += n
        self.plan.append((eng, fn, w, [], self._phase))

    def mm(self, out, lhsT, rhs, start, stop, waits=(), inc=None):
        self.op("tensor",
                lambda e, o=out, l=lhsT, r=rhs, st=start, sp=stop:
                e.matmul(o, l, r, start=st, stop=sp),
                waits, inc=inc)

    def trans(self, out, in_, ident, waits=(), inc=None):
        self.op("tensor",
                lambda e, o=out, i=in_, d=ident: e.transpose(o, i, d),
                waits, inc=inc)

    def act(self, out, in_, func, bias=0.0, scale=1.0, waits=(), inc=None,
            accum_out=None):
        self.op("scalar",
                lambda e, o=out, i=in_, f=func, b=bias, s=scale, a=accum_out:
                e.activation(o, i, f, bias=b, scale=s, accum_out=a),
                waits, inc=inc)

    def v(self, fname, *args, waits=(), inc=None):
        self.op("vector",
                lambda e, fn=fname, a=args: getattr(e, fn)(*a),
                waits, inc=inc)

    def g(self, fname, *args, waits=(), inc=None):
        self.op("gpsimd",
                lambda e, fn=fname, a=args: getattr(e, fn)(*a),
                waits, inc=inc)

    def stt(self, out, in0, scalar, in1, op0, op1, waits=(), inc=None):
        self.op("vector",
                lambda e, o=out, a=in0, s=scalar, b=in1, o0=op0, o1=op1:
                e.scalar_tensor_tensor(o, a, s, b, o0, o1),
                waits, inc=inc)

    def gstt(self, out, in0, scalar, in1, op0, op1, waits=(), inc=None):
        self.op("gpsimd",
                lambda e, o=out, a=in0, s=scalar, b=in1, o0=op0, o1=op1:
                e.scalar_tensor_tensor(o, a, s, b, o0, o1),
                waits, inc=inc)

    def emit(self):
        plan = self.plan
        with self.nc.Block() as block:
            for engname in ENGINES:
                def body(eng, engname=engname):
                    for (en, fn, waits, incs, ph) in plan:
                        if en != engname:
                            continue
                        with self.nc.named_scope(ph):
                            for (s, vv) in waits:
                                eng.wait_ge(s, vv)
                            inst = fn(eng)
                            for (s, a) in incs:
                                assert inst is not None, "inc on op, no inst"
                                inst.then_inc(s, a)
                getattr(block, engname)(body)
        return self.nc


def lay(total, **sizes):
    off, out = 0, {}
    for k, n in sizes.items():
        out[k] = off
        off += n
    assert off <= total, f"layout overflow: {off} > {total}"
    return out


FA_COLS = 26752
BA_COLS = 46080



def emit_layernorm_tm(kb, s_tiles, sq_scr, cols, n_tiles, DW, sems,
                      gain_col=None, bias_col=None):
    sDVE, sACT = sems
    inv = 1.0 / DW
    kb.wait("scalar", sDVE)
    kb.wait("vector", sACT)
    if hasattr(kb, "kcon_ld"):
        kb.wait("scalar", kb.sLd_h, kb.kcon_ld)
    for i in range(n_tiles):
        s = s_tiles(i)
        kb.act(sq_scr, s, AF.Square, accum_out=cols[:, 1:2], inc=sACT)
        kb.v("reduce_sum", cols[:, 0:1], s, AX.X, inc=sDVE)
        kb.wait("vector", sACT)
        kb.v("tensor_scalar_mul", cols[:, 2:3], cols[:, 0:1], inv, inc=sDVE)
        kb.v("tensor_scalar_mul", cols[:, 3:4], cols[:, 1:2], inv, inc=sDVE)
        kb.v("tensor_mul", cols[:, 6:7], cols[:, 2:3], cols[:, 2:3], inc=sDVE)
        kb.v("tensor_sub", cols[:, 3:4], cols[:, 3:4], cols[:, 6:7], inc=sDVE)
        kb.wait("scalar", sDVE)
        kb.act(cols[:, 7:8], cols[:, 3:4], AF.Ln, bias=1e-5, inc=sACT)
        kb.act(cols[:, 4:5], cols[:, 7:8], AF.Exp, scale=-0.5, inc=sACT)
        kb.wait("vector", sACT)
        kb.v("tensor_mul", cols[:, 5:6], cols[:, 2:3], cols[:, 4:5], inc=sDVE)
        kb.v("tensor_scalar_mul", cols[:, 5:6], cols[:, 5:6], -1.0, inc=sDVE)
        kb.wait("scalar", sDVE)
        kb.act(s, s, AF.Identity, scale=cols[:, 4:5], bias=cols[:, 5:6], inc=sACT)
        if gain_col is not None:
            kb.act(s, s, AF.Identity, scale=gain_col, bias=bias_col, inc=sACT)
        kb.wait("vector", sACT)


def emit_tm2cm(kb, src_tile, ba, stage_off, dst_fn, n_tiles, idb, sems):
    sPE, sDVE, sACT = sems
    for i in range(n_tiles):
        kb.wait("vector", sACT)
        kb.v("tensor_copy", ba(stage_off, D), src_tile(i), inc=sDVE)
        kb.wait("tensor", sDVE)
        for ci in range(8):
            kb.trans(kb.pT[:, 0:128], ba(stage_off + ci * 128, 128), idb[:],
                     inc=sPE)
            kb.wait("vector", sPE)
            kb.v("tensor_copy", dst_fn(ci, i), kb.pT[:, 0:128], inc=sDVE)
            kb.wait("tensor", sDVE)



def build_fused():
    kb = KB()
    dram = lambda n, s, dt, k="ExternalInput": kb.nc.dram_tensor(n, s, dt, kind=k).ap()

    xk = dram("xk", [T, D], F32)
    wq = dram("wq", [D, D], F32)
    wk = dram("wk", [D, D], F32)
    wv = dram("wv", [D, D], F32)
    wp = dram("wp", [D, D], F32)
    bq_c = dram("bq_c", [128, 8], F32)
    idf_d = dram("idf", [128, 128], F32)
    ckon_d = dram("ckon", [128, 8], F32)
    bk_r = dram("bk_r", [1, D], F32)
    bv_r = dram("bv_r", [1, D], F32)
    bp_r = dram("bp_r", [1, D], F32)
    id128b = dram("id128b", [128, 128], BF16)
    ones_d = dram("ones_d", [1, 128], F32)
    gwT = dram("gwT", [D, NE], F32)
    eb_r = dram("eb_r", [1, NE], F32)
    w1own = dram("w1own", [D, HID], F32)
    w2own = dram("w2own", [HID, D], F32)
    w3own = dram("w3own", [D, HID], F32)
    ltri2 = dram("ltri2", [8, 16], F32)
    iotas = dram("iotas", [8, 56], F32)
    sel4_d = dram("sel4_d", [8, 512], F32)
    winx = dram("winx", [D, DIC], F32)
    winz = dram("winz", [D, DIC], F32)
    cw = dram("cw", [128, 16], F32)
    cb = dram("cb", [128, 4], F32)
    wxT = dram("wxT", [DI, 544], F32)
    wdtT = dram("wdtT", [DI, DIC], F32)
    bdt_c = dram("bdt_c", [128, 4], F32)
    alog_c = dram("alog_c", [128, 64], F32)
    dcol = dram("dcol", [128, 4], F32)
    woT = dram("woT", [DIC, D], F32)
    lng = dram("lng", [1, D], F32)
    lnb = dram("lnb", [1, D], F32)
    k0t = dram("k0t", [D, 4096], U8)
    k0b = dram("k0b", [D, 4096], U8)
    k1 = dram("k1", [4096, 4096], U8)
    k2 = dram("k2", [4096, D], U8)
    scol = dram("scol", [128, 3], F32)

    WX = 8 * HID
    w13loc = dram("w13loc", [128, 2 * WX], BF16, "Internal")
    w2loc = dram("w2loc", [128, WX], BF16, "Internal")
    w13br_t = kb.nc.dram_tensor("w13br", [NE * 128, 2 * WX], BF16,
                                kind="Internal")
    w2br_t = kb.nc.dram_tensor("w2br", [NE * 128, WX], BF16, kind="Internal")
    w13br, w2br = w13br_t.ap(), w2br_t.ap()
    xsort = dram("xsort", [24 * 256, D], BF16, "Internal")
    osort = dram("osort", [24 * 256, D], F32, "Internal")
    x2f = dram("x2f", [T, D], F32, "Internal")
    zf = dram("zf", [DIC, TB], F32, "Internal")
    uf32 = dram("uf32", [DIC, TB], F32, "Internal")
    bcf = dram("bcf", [32, TB], F32, "Internal")
    x3fh = dram("x3fh", [T, D], F32, "Internal")
    x3bh = dram("x3bh", [D, T], BF16, "Internal")
    w0dh = dram("w0dh", [D, 4096], BF16, "Internal")
    w1dh = dram("w1dh", [4096, 4096], BF16, "Internal")
    w2dh = dram("w2dh", [4096, D], BF16, "Internal")
    kvcc_i = dram("kvcc_i", [128, 512], F32, "Internal")
    kvcc_o = dram("kvcc_o", [128, 512], F32, "Internal")
    x2cc_i = dram("x2cc_i", [D, T], BF16, "Internal")
    x2g = dram("x2g", [4 * D, T], BF16, "Internal")
    ucc_i = dram("ucc_i", [DIC, TB], BF16, "Internal")
    ug = dram("ug", [DI, TB], BF16, "Internal")
    dincc_i = dram("dincc_i", [DIC, TB], BF16, "Internal")
    ding = dram("ding", [DI, TB], BF16, "Internal")
    pocc_i = dram("pocc_i", [TB, D], F32, "Internal")
    pors1 = dram("pors1", [T // 2, D], F32, "Internal")
    pors2 = dram("pors2", [T // 2, D], F32, "Internal")

    xo = dram("xo", [T, D], F32, "ExternalOutput")

    sLd = kb.sem("sLd")
    kb.sLd_h = sLd
    sCast = kb.sem("sCast")
    sPE = kb.sem("sPE")
    sDVE = kb.sem("sDVE")
    sACT = kb.sem("sACT")
    sSt = kb.sem("sSt")
    sC = [kb.sem(f"sC{i}") for i in range(8)]
    sems = (sDVE, sACT)

    bq_cols = kb.sbuf("bq_cols", [128, 8], F32)
    ones1 = kb.sbuf("ones1", [1, 128], F32)
    idb = kb.sbuf("idb", [128, 128], BF16)
    idf = kb.sbuf("idf_sb", [128, 128], F32)
    cols = kb.sbuf("cols", [128, 8], F32)
    kcon = kb.sbuf("kcon", [128, 8], F32)
    cw_sb = kb.sbuf("cw_sb", [128, 16], F32)
    cb_sb = kb.sbuf("cb_sb", [128, 4], F32)
    bdt = kb.sbuf("bdt", [128, 4], F32)
    alog = kb.sbuf("alog", [128, 64], F32)
    dco = kb.sbuf("dco", [128, 4], F32)
    ac = kb.sbuf("ac", [128, 64], F32)
    stc = kb.sbuf("stc", [128, 64], F32)
    brows = kb.sbuf("brows", [1, 2 * 1024], F32)
    sc_sb = kb.sbuf("sc_sb", [128, 4], F32)
    ltri = kb.sbuf("ltri", [8, 16], F32)
    iot = kb.sbuf("iot", [8, 56], F32)
    sidxu = kb.sbuf("sidxu", [128, 32], mybir.dt.uint32)
    rsel = kb.sbuf("rsel", [1, 48], mybir.dt.int32)
    gsc = kb.sbuf("gsc", [128, 32], F32)
    sel4 = kb.sbuf("sel4", [8, 512], F32)

    FA = kb.sbuf("FA", [128, FA_COLS], F32)
    BA = kb.sbuf("BA", [128, BA_COLS], BF16)
    KVB = 41984
    FAB = FA.bitcast(BF16)
    fa = lambda o, n: FA[:, o:o + n]
    ba = lambda o, n: BA[:, o:o + n]
    fab = lambda o, n: FAB[:, o:o + n]

    pA = kb.psum("pA", [128, 512])
    pB = kb.psum("pB", [128, 512])
    pC = kb.psum("pC", [128, 1024])
    pKV = kb.psum("pKV", [128, 512])
    pR = kb.psum("pR", [128, 512])
    pTF = kb.psum("pTF", [128, 128])
    pT = kb.psum("pT", [128, 256], BF16)
    kb.pT = pT

    kb.phase("P0_wconv")
    G8 = [list(range(8))]
    kb.dma("gpsimd", BA[:, 0:WX],
           w1own.rearrange("(c p) j -> p c j", p=128), sCast)
    kb.dma("gpsimd", BA[:, WX:2 * WX],
           w3own.rearrange("(c p) j -> p c j", p=128), sCast)
    kb.wait("gpsimd", sCast)
    kb.dma("gpsimd", w13loc, BA[:, 0:2 * WX], sCast)
    kb.wait("gpsimd", sCast)
    kb.cc("AllGather", ALU.bypass, w13loc, w13br, sC[5], groups=G8)
    kb.dma("gpsimd", BA[:, 0:WX],
           w2own.rearrange("(c p) j -> p c j", p=128), sCast)
    kb.wait("gpsimd", sCast)
    kb.dma("gpsimd", w2loc, BA[:, 0:WX], sCast)
    kb.wait("gpsimd", sCast)
    kb.cc("AllGather", ALU.bypass, w2loc, w2br, sC[6], groups=G8)
    conv_done = kb.val(sCast)

    kb.phase("P1_attn")
    LF = lay(FA_COLS, xr=16 * D, bbc=3 * D, brow=3 * D, sq=D,
             s1=512, s2=512, gb=T, gsm=128)
    LF["gem"] = LF["brow"]
    LF["x1cf"] = LF["s1"]

    kb.dma("sync", kcon[:], ckon_d, sLd)
    for j, cv in enumerate(CONSTS):
        kb.nc.const_aps.register(F32, cv, kcon[:, j:j + 1])
    kb.dma("sync", bq_cols[:], bq_c, sLd)
    kb.dma("sync", FA[0:1, LF["brow"]:LF["brow"] + D], bk_r, sLd)
    kb.dma("sync", FA[0:1, LF["brow"] + D:LF["brow"] + 2 * D], bv_r, sLd)
    kb.dma("sync", FA[0:1, LF["brow"] + 2 * D:LF["brow"] + 3 * D], bp_r, sLd)
    kb.dma("sync", idb[:], id128b, sLd)
    kb.dma("sync", idf[:], idf_d, sLd)
    kb.dma("sync", ones1[:], ones_d, sLd)
    kb.kcon_ld = kb.val(sLd)

    xk_t = xk.rearrange("(n p) d -> n p d", p=128)
    for i in range(16):
        kb.dma("sync", fa(LF["xr"] + i * D, D), xk_t[i], sLd)

    kb.wait("tensor", sLd, kb.kcon_ld)
    for j in range(6):
        kb.mm(pR[:], ones1[:, :],
              FA[0:1, LF["brow"] + j * 512:LF["brow"] + (j + 1) * 512],
              True, True, inc=sPE)
        kb.wait("vector", sPE)
        kb.v("tensor_copy", fa(LF["bbc"] + j * 512, 512), pR[:], inc=sDVE)
        kb.wait("tensor", sDVE)

    LB = lay(BA_COLS, xq=8 * T, wk=8 * D, wv=8 * D, k1=4 * D, vb=4 * D,
             stage=D)
    LQ = {"xq": LB["xq"], "wq": LB["wk"], "q1": LB["wv"]}
    kb.wait("vector", sLd)
    kb.wait("vector", sCast, conv_done)
    emit_tm2cm(kb, lambda i: fa(LF["xr"] + i * D, D), ba, LB["stage"],
               lambda ci, i: ba(LB["xq"] + ci * T + i * 128, 128),
               16, idb, (sPE, sDVE, sACT))

    for wsrc, off in ((wk, LB["wk"]), (wv, LB["wv"])):
        for ci in range(8):
            kb.dma("gpsimd", ba(off + ci * D, D),
                   wsrc[ci * 128:(ci + 1) * 128, :], sCast)
    kb.wait("tensor", sCast)
    kb.wait("tensor", sDVE)
    kb.v("memset", fa(LF["gb"], 512), 0.0, inc=sDVE)

    for g in range(16):
        for coh in range(2):
            for ci in range(8):
                kb.mm(pA[:], ba(LB["xq"] + ci * T + g * 128, 128),
                      ba(LB["wk"] + ci * D + coh * 512, 512),
                      ci == 0, ci == 7, inc=sPE if ci == 7 else None)
            for ci in range(8):
                kb.mm(pB[:], ba(LB["xq"] + ci * T + g * 128, 128),
                      ba(LB["wv"] + ci * D + coh * 512, 512),
                      ci == 0, ci == 7, inc=sPE if ci == 7 else None)
            kb.wait("vector", sPE)
            s1 = fa(LF["s1"], 512)
            s2 = fa(LF["s2"], 512)
            kb.stt(s1, pA[:], 1.0, fa(LF["bbc"] + coh * 512, 512),
                   ALU.mult, ALU.add, inc=sDVE)
            kb.stt(s2, pB[:], 1.0, fa(LF["bbc"] + D + coh * 512, 512),
                   ALU.mult, ALU.add, inc=sDVE)
            kb.wait("tensor", sDVE)
            kb.v("tensor_copy",
                 ba(LB["vb"] + (g % 4) * D + coh * 512, 512), s2, inc=sDVE)
            kb.wait("scalar", sDVE)
            kb.act(fa(LF["sq"], 512), s1, AF.Relu, inc=sACT)
            kb.wait("vector", sACT)
            kb.v("tensor_scalar_min", s1, s1, 0.0, inc=sDVE)
            kb.wait("scalar", sDVE)
            kb.act(s1, s1, AF.Exp, inc=sACT)
            kb.wait("vector", sACT)
            kb.v("tensor_add", s1, s1, fa(LF["sq"], 512), inc=sDVE)
            kb.v("tensor_copy",
                 ba(LB["k1"] + (g % 4) * D + coh * 512, 512), s1, inc=sDVE)
        if g % 4 == 3:
            kb.wait("tensor", sDVE)
            for hp in range(8):
                for hh in range(2):
                    for t4 in range(4):
                        h = hp * 2 + hh
                        kb.mm(pKV[hh * 64:hh * 64 + 64, 0:64],
                              ba(LB["k1"] + t4 * D + h * 64, 64),
                              ba(LB["vb"] + t4 * D + h * 64, 64),
                              t4 == 0, t4 == 3,
                              inc=sPE if (t4 == 3 and hh == 1) else None)
                kb.wait("vector", sPE)
                kb.op("vector", lambda e, o=FA[:, LF["gb"] + hp * 64:LF["gb"] + hp * 64 + 64],
                      a=FA[:, LF["gb"] + hp * 64:LF["gb"] + hp * 64 + 64],
                      b=pKV[:, 0:64]: e.tensor_add(o, a, b),
                      waits=[(sDVE, kb.val(sDVE))], inc=sDVE)
                kb.wait("tensor", sDVE)

    kb.wait("gpsimd", sDVE)
    kb.dma("gpsimd", kvcc_i, fa(LF["gb"], 512), sSt)
    kb.wait("gpsimd", sSt)
    kb.cc("AllReduce", ALU.add, kvcc_i, kvcc_o, sC[0])
    kb.wait("sync", sC[0], 1)
    kb.dma("sync", fa(LF["gb"], 512), kvcc_o, sLd)
    kb.wait("vector", sLd)
    kb.v("memset", BA[:, KVB:KVB + 1024], 0.0, inc=sDVE)
    for h in range(NH):
        prow = (h % 2) * 64
        hp = h // 2
        dro = (h % 2) * 64
        kb.v("tensor_copy",
             BA[dro:dro + 64, KVB + hp * 128 + dro:KVB + hp * 128 + dro + 64],
             FA[prow:prow + 64, LF["gb"] + hp * 64:LF["gb"] + hp * 64 + 64],
             inc=sDVE)

    kv_done = kb.val(sPE)
    for ci in range(8):
        w = [(sPE, kv_done)] if ci == 0 else []
        kb.dma("gpsimd", ba(LQ["wq"] + ci * D, D),
               wq[ci * 128:(ci + 1) * 128, :], sCast, waits=w)
    kb.wait("tensor", sCast)
    for co in range(8):
        for tc in range(4):
            for ci in range(8):
                kb.mm(pA[:], ba(LQ["wq"] + ci * D + co * 128, 128),
                      ba(LQ["xq"] + ci * T + tc * 512, 512),
                      ci == 0, ci == 7, inc=sPE if ci == 7 else None)
            kb.wait("scalar", sPE)
            kb.wait("scalar", sDVE)
            kb.act(fa(LF["s1"], 512), pA[:], AF.Identity,
                   bias=bq_cols[:, co:co + 1], inc=sACT)
            kb.act(fa(LF["s2"], 512), pA[:], AF.Relu,
                   bias=bq_cols[:, co:co + 1], inc=sACT)
            kb.wait("vector", sACT)
            kb.v("tensor_scalar_min", fa(LF["s1"], 512), fa(LF["s1"], 512),
                 0.0, inc=sDVE)
            kb.wait("scalar", sDVE)
            kb.act(fa(LF["s1"], 512), fa(LF["s1"], 512), AF.Exp, inc=sACT)
            kb.wait("vector", sACT)
            kb.v("tensor_add", fa(LF["s1"], 512), fa(LF["s1"], 512),
                 fa(LF["s2"], 512), inc=sDVE)
            kb.v("tensor_copy", ba(LQ["q1"] + co * T + tc * 512, 512),
                 fa(LF["s1"], 512), inc=sDVE)
            kb.wait("tensor", sDVE)

    for ci in range(8):
        w = [(sPE, kb.val(sPE))] if ci == 0 else []
        kb.dma("gpsimd", ba(LQ["wq"] + ci * D, D),
               wp[ci * 128:(ci + 1) * 128, :], sCast, waits=w)
    for hp in range(8):
        for tc in range(4):
            kb.mm(pA[:], BA[:, KVB + hp * 128:KVB + (hp + 1) * 128],
                  ba(LQ["q1"] + hp * T + tc * 512, 512), True, True, inc=sPE)
            kb.wait("vector", sPE)
            kb.v("tensor_copy", ba(LQ["q1"] + hp * T + tc * 512, 512), pA[:],
                 inc=sDVE)
            kb.wait("tensor", sDVE)
    kb.wait("tensor", sCast)
    for tt in range(16):
        for coh in range(2):
            for ci in range(8):
                kb.mm(pC[:, coh * 512:(coh + 1) * 512],
                      ba(LQ["q1"] + ci * T + tt * 128, 128),
                      ba(LQ["wq"] + ci * D + coh * 512, 512),
                      ci == 0, ci == 7, inc=sPE if ci == 7 else None)
        kb.wait("vector", sPE)
        for coh in range(2):
            xo_ = LF["xr"] + tt * D + coh * 512
            kb.stt(fa(LF["s1"], 512), pC[:, coh * 512:(coh + 1) * 512], 1.0,
                   fa(LF["bbc"] + 2 * D + coh * 512, 512), ALU.mult, ALU.add,
                   inc=sDVE)
            kb.v("tensor_add", fa(xo_, 512), fa(xo_, 512), fa(LF["s1"], 512),
                 inc=sDVE)
        kb.wait("tensor", sDVE)
    emit_layernorm_tm(kb, lambda i: fa(LF["xr"] + i * D, D),
                      fa(LF["sq"], D), cols, 16, D, sems)

    kb.phase("P2_moe")
    kb.wait("sync", sLd)
    kb.dma("sync", ltri[:], ltri2, sLd)
    kb.dma("sync", iot[:], iotas, sLd)
    kb.dma("sync", sel4[:], sel4_d, sLd)
    kb.wait("sync", sLd)
    build_moe_g(kb, (gwT, eb_r, w13br_t, w2br_t, xsort, osort, x2f),
                (FA, BA, FAB, LF), (pA, pB, pC, pKV, pR, pTF, pT),
                (sLd, sCast, sPE, sDVE, sACT, sSt, sC),
                (ones1, idf, idb, cols, ltri, iot, sidxu, rsel, gsc,
                 kcon, sel4))
    kb.fence()
    emit_layernorm_tm(kb, lambda i: fa(LF["xr"] + i * D, D),
                      fa(LF["sq"], D), cols, 16, D, sems)

    kb.phase("P3_x2gather")
    L3x = lay(BA_COLS, x2cb=8 * T, stage=D)
    emit_tm2cm(kb, lambda i: fa(LF["xr"] + i * D, D), ba, L3x["stage"],
               lambda ci, i: ba(L3x["x2cb"] + ci * T + i * 128, 128),
               16, idb, (sPE, sDVE, sACT))
    for i in range(16):
        kb.dma("gpsimd", x2f[i * 128:(i + 1) * 128, :],
               fa(LF["xr"] + i * D, D), sSt)
    for ci in range(8):
        kb.dma("gpsimd", x2cc_i[ci * 128:(ci + 1) * 128, :],
               ba(L3x["x2cb"] + ci * T, T), sSt)
    kb.wait("gpsimd", sSt)
    kb.wait("gpsimd", sDVE)
    kb.cc("AllGather", ALU.bypass, x2cc_i, x2g, sC[1])
    kb.fence()

    kb.phase("P4_win_conv")
    LFa = lay(FA_COLS, xs=4 * (T + 3), ua=T, us=T, zs=T)
    LBa = lay(BA_COLS, wx=8 * 512, wz=8 * 512, xc=2 * 8 * T, ub=T)
    kb.dma("sync", cw_sb[:], cw, sLd)
    kb.dma("sync", cb_sb[:], cb, sLd)
    for ci in range(8):
        kb.dma("gpsimd", ba(LBa["wx"] + ci * 512, 512),
               winx[ci * 128:(ci + 1) * 128, :], sCast)
        kb.dma("gpsimd", ba(LBa["wz"] + ci * 512, 512),
               winz[ci * 128:(ci + 1) * 128, :], sCast)
    kb.wait("gpsimd", sCast)
    for dt in range(4):
        kb.v("memset", fa(LFa["xs"] + dt * (T + 3), 3), 0.0, inc=sDVE)

    for ch in range(4):
        par = ch % 2
        xcb = lambda ci: ba(LBa["xc"] + par * 8 * T + ci * T, T)
        w = [(sPE, kb.val(sPE))] if ch >= 2 else [(sC[1], 1)]
        kb.wait("sync", sLd)
        for ci in range(8):
            kb.dma("sync", xcb(ci),
                   x2g[ch * D + ci * 128:ch * D + (ci + 1) * 128, :], sLd,
                   waits=w if ci == 0 else ())
        kb.wait("sync", sLd)
        kb.wait("tensor", sLd)
        if ch == 0:
            kb.wait("tensor", sCast, kb.val(sCast))
        for dt in range(4):
            xso = LFa["xs"] + dt * (T + 3)
            for t4 in range(4):
                for ci in range(8):
                    kb.mm(pA[:], ba(LBa["wx"] + ci * 512 + dt * 128, 128),
                          xcb(ci)[:, t4 * 512:(t4 + 1) * 512],
                          ci == 0, ci == 7, inc=sPE if ci == 7 else None)
                for ci in range(8):
                    kb.mm(pB[:], ba(LBa["wz"] + ci * 512 + dt * 128, 128),
                          xcb(ci)[:, t4 * 512:(t4 + 1) * 512],
                          ci == 0, ci == 7, inc=sPE if ci == 7 else None)
                kb.wait("vector", sPE)
                kb.v("tensor_copy", FA[:, xso + 3 + t4 * 512:xso + 3 + (t4 + 1) * 512],
                     pA[:], inc=sDVE)
                kb.v("tensor_copy", fa(LFa["zs"] + t4 * 512, 512), pB[:],
                     inc=sDVE)
                kb.wait("tensor", sDVE)
            kb.wait("gpsimd", sDVE)
            kb.wait("gpsimd", sSt)
            kb.dma("gpsimd", zf[dt * 128:(dt + 1) * 128, ch * T:(ch + 1) * T],
                   fa(LFa["zs"], T), sSt)
            kb.op("vector", lambda e, o=fa(LFa["ua"], T), i=FA[:, xso:xso + T],
                  s=cw_sb[:, dt * 4:dt * 4 + 1]:
                  e.tensor_scalar_mul(o, i, s), inc=sDVE)
            for j in range(1, 4):
                kb.stt(fa(LFa["ua"], T), FA[:, xso + j:xso + j + T],
                       cw_sb[:, dt * 4 + j:dt * 4 + j + 1], fa(LFa["ua"], T),
                       ALU.mult, ALU.add, inc=sDVE)
            kb.v("tensor_copy", FA[:, xso:xso + 3],
                 FA[:, xso + T:xso + T + 3], inc=sDVE)
            kb.wait("scalar", sDVE)
            kb.wait("scalar", sSt, kb.val(sSt))
            kb.act(fa(LFa["us"], T), fa(LFa["ua"], T), AF.Silu,
                   bias=cb_sb[:, dt:dt + 1], inc=sACT)
            kb.wait("vector", sACT)
            kb.v("tensor_copy", ba(LBa["ub"], T), fa(LFa["us"], T), inc=sDVE)
            kb.wait("gpsimd", sDVE)
            kb.wait("gpsimd", sACT)
            kb.wait("gpsimd", sSt)
            kb.dma("gpsimd", uf32[dt * 128:(dt + 1) * 128, ch * T:(ch + 1) * T],
                   fa(LFa["us"], T), sSt)
            kb.dma("gpsimd", ucc_i[dt * 128:(dt + 1) * 128, ch * T:(ch + 1) * T],
                   ba(LBa["ub"], T), sSt)
            kb.wait("vector", sSt)
    kb.wait("gpsimd", sSt)
    kb.cc("AllGather", ALU.bypass, ucc_i, ug, sC[2])
    kb.fence()

    kb.phase("P5_wx")
    TC2 = 1024
    LBb = lay(BA_COLS, wx=16 * 544, uc=2 * 16 * TC2, st1=512)
    LFb = lay(FA_COLS, st2=512)
    for ci in range(16):
        kb.dma("gpsimd", ba(LBb["wx"] + ci * 544, 544),
               wxT[ci * 128:(ci + 1) * 128, :], sCast)
    kb.wait("gpsimd", sCast)
    for ch in range(8):
        par = ch % 2
        ucb = lambda ci: ba(LBb["uc"] + par * 16 * TC2 + ci * TC2, TC2)
        w = [(sPE, kb.val(sPE))] if ch >= 2 else [(sC[2], 1)]
        kb.wait("sync", sLd)
        for ci in range(16):
            kb.dma("sync", ucb(ci), ug[ci * 128:(ci + 1) * 128,
                                       ch * TC2:(ch + 1) * TC2], sLd,
                   waits=w if ci == 0 else ())
        kb.wait("sync", sLd)
        kb.wait("tensor", sLd)
        if ch == 0:
            kb.wait("tensor", sCast, kb.val(sCast))
        for t2 in range(2):
            for co in range(4):
                for ci in range(16):
                    kb.mm(pA[:], ba(LBb["wx"] + ci * 544 + co * 128, 128),
                          ucb(ci)[:, t2 * 512:(t2 + 1) * 512],
                          ci == 0, ci == 15, inc=sPE if ci == 15 else None)
                kb.wait("vector", sPE)
                kb.wait("vector", sSt, kb.val(sSt))
                kb.v("tensor_copy", ba(LBb["st1"], 512), pA[:], inc=sDVE)
                kb.wait("tensor", sDVE)
                kb.wait("gpsimd", sDVE)
                kb.wait("gpsimd", sSt)
                kb.dma("gpsimd",
                       dincc_i[co * 128:(co + 1) * 128,
                               ch * TC2 + t2 * 512:ch * TC2 + (t2 + 1) * 512],
                       ba(LBb["st1"], 512), sSt)
            for ci in range(16):
                kb.mm(pR[0:32, :], ba(LBb["wx"] + ci * 544 + 512, 32),
                      ucb(ci)[:, t2 * 512:(t2 + 1) * 512],
                      ci == 0, ci == 15, inc=sPE if ci == 15 else None)
            kb.wait("vector", sPE)
            kb.wait("vector", sSt, kb.val(sSt))
            kb.v("tensor_copy", FA[0:32, LFb["st2"]:LFb["st2"] + 512],
                 pR[0:32, :], inc=sDVE)
            kb.wait("tensor", sDVE)
            kb.wait("gpsimd", sDVE)
            kb.wait("gpsimd", sSt)
            kb.dma("gpsimd",
                   bcf[:, ch * TC2 + t2 * 512:ch * TC2 + (t2 + 1) * 512],
                   FA[0:32, LFb["st2"]:LFb["st2"] + 512], sSt)
    kb.wait("gpsimd", sSt)
    kb.cc("AllGather", ALU.bypass, dincc_i, ding, sC[3])
    kb.fence()

    kb.phase("P6_scan")
    TC = 1024
    LFc = lay(FA_COLS, delta=4 * TC, du=4 * TC, yac=4 * TC, uc=4 * TC,
              bb=2 * TC, cbr=2 * TC, dA=2 * TC, dBu=2 * TC, sseq=TC, pst=TC)
    LFc["zc"] = LFc["delta"]
    LFc["s1"] = LFc["pst"]
    LFc["s2"] = LFc["pst"] + 512
    LBc = lay(BA_COLS, dinc=16 * TC, wdt=16 * 512, wo=4 * D, y2b=4 * TC)
    kb.dma("sync", bdt[:], bdt_c, sLd)
    kb.dma("sync", alog[:], alog_c, sLd)
    kb.dma("sync", dco[:], dcol, sLd)
    kb.wait("sync", sLd)
    for ci in range(16):
        kb.dma("gpsimd", ba(LBc["wdt"] + ci * 512, 512),
               wdtT[ci * 128:(ci + 1) * 128, :], sCast)
    for ci in range(4):
        kb.dma("gpsimd", ba(LBc["wo"] + ci * D, D),
               woT[ci * 128:(ci + 1) * 128, :], sCast)
    kb.wait("gpsimd", sCast)
    wld = kb.val(sCast)
    kb.wait("scalar", sLd)
    kb.act(ac[:], alog[:], AF.Exp, inc=sACT)
    kb.wait("vector", sACT)
    kb.v("tensor_scalar_mul", ac[:], ac[:], -1.0, inc=sDVE)
    kb.v("memset", stc[:], 0.0, inc=sDVE)

    scan_done = []
    pool_done = {}
    dve_done = {}
    bcast_mm_done = {}

    for ch in range(8):
        dcb = lambda ci: ba(LBc["dinc"] + ci * TC, TC)
        ucb = lambda dt: fa(LFc["uc"] + dt * TC, TC)
        w = [(sPE, kb.val(sPE)), (sDVE, kb.val(sDVE))] if ch >= 1 else [(sC[3], 1)]
        kb.wait("sync", sLd)
        for ci in range(16):
            kb.dma("sync", dcb(ci), ding[ci * 128:(ci + 1) * 128,
                                         ch * TC:(ch + 1) * TC], sLd,
                   waits=w if ci == 0 else ())
        for dt in range(4):
            kb.dma("sync", ucb(dt), uf32[dt * 128:(dt + 1) * 128,
                                         ch * TC:(ch + 1) * TC], sLd)
        kb.wait("sync", sLd)
        ld_ch = kb.val(sLd)
        kb.wait("tensor", sLd, ld_ch)
        if ch == 0:
            kb.wait("tensor", sCast, wld)
        kb.wait("scalar", sSt, kb.val(sSt))
        for dt in range(4):
            for t2 in range(2):
                pp = (pA, pB)[t2]
                for ci in range(16):
                    kb.mm(pp[:], ba(LBc["wdt"] + ci * 512 + dt * 128, 128),
                          dcb(ci)[:, t2 * 512:(t2 + 1) * 512],
                          ci == 0, ci == 15, inc=sPE if ci == 15 else None)
                kb.wait("scalar", sPE)
                kb.wait("scalar", sDVE)
                do = LFc["delta"] + dt * TC + t2 * 512
                kb.act(fa(LFc["s1"], 512), pp[:], AF.Abs,
                       bias=bdt[:, dt:dt + 1], inc=sACT)
                kb.act(fa(LFc["s1"], 512), fa(LFc["s1"], 512), AF.Exp,
                       scale=-1.0, inc=sACT)
                kb.act(fa(LFc["s1"], 512), fa(LFc["s1"], 512), AF.Ln,
                       bias=1.0, inc=sACT)
                kb.act(fa(LFc["s2"], 512), pp[:], AF.Relu,
                       bias=bdt[:, dt:dt + 1], inc=sACT)
                kb.wait("vector", sACT)
                kb.v("tensor_add", fa(do, 512), fa(LFc["s1"], 512),
                     fa(LFc["s2"], 512), inc=sDVE)
                kb.wait("tensor", sDVE)
            kb.wait("vector", sLd, ld_ch)
            kb.v("tensor_mul", fa(LFc["du"] + dt * TC, TC),
                 fa(LFc["delta"] + dt * TC, TC), ucb(dt), inc=sDVE)
        kb.v("memset", fa(LFc["yac"], 4 * TC), 0.0, inc=sDVE)
        du_done = kb.val(sDVE)
        for n in range(DS):
            np_ = n % 2
            pn = n - 1
            prev = (ch, pn) if pn >= 0 else ((ch - 1, DS + pn) if ch >= 1
                                             else None)
            wbr = []
            if prev in pool_done:
                wbr.append((sCast, pool_done[prev]))
            if prev in dve_done:
                wbr.append((sDVE, dve_done[prev]))
            kb.wait("sync", sLd)
            kb.dma("sync", fa(LFc["bb"] + np_ * TC, TC),
                   bcf[n:n + 1, ch * TC:(ch + 1) * TC].broadcast_to((128, TC)),
                   sLd, waits=wbr)
            kb.dma("sync", fa(LFc["cbr"] + np_ * TC, TC),
                   bcf[16 + n:17 + n, ch * TC:(ch + 1) * TC].broadcast_to((128, TC)),
                   sLd)
            kb.wait("sync", sLd)
            bcast_done_v = kb.val(sLd)
            for dt in range(4):
                k = len(scan_done)
                p = k % 2
                dsl = fa(LFc["delta"] + dt * TC, TC)
                gv = max(scan_done[k - 2] if k >= 2 else 0, du_done)
                kb.wait("scalar", sDVE, gv)
                kb.wait("gpsimd", sDVE, gv)
                if dt == 0:
                    kb.wait("gpsimd", sLd, bcast_done_v)
                    kb.wait("vector", sLd, bcast_done_v)
                kb.act(fa(LFc["dA"] + p * TC, TC), dsl, AF.Exp,
                       scale=ac[:, dt * 16 + n:dt * 16 + n + 1], inc=sACT)
                kb.g("tensor_mul", fa(LFc["dBu"] + p * TC, TC),
                     fa(LFc["du"] + dt * TC, TC), fa(LFc["bb"] + np_ * TC, TC),
                     inc=sCast)
                kb.wait("vector", sACT)
                kb.wait("vector", sCast)
                kb.op("vector", lambda e, o=fa(LFc["sseq"], TC),
                      a=fa(LFc["dA"] + p * TC, TC),
                      b=fa(LFc["dBu"] + p * TC, TC),
                      i=stc[:, dt * 16 + n:dt * 16 + n + 1]:
                      e.tensor_tensor_scan(o, a, b, i, ALU.mult, ALU.add),
                      inc=sDVE)
                scan_done.append(kb.val(sDVE))
                kb.v("tensor_copy", stc[:, dt * 16 + n:dt * 16 + n + 1],
                     fa(LFc["sseq"] + TC - 1, 1), inc=sDVE)
                kb.v("tensor_mul", fa(LFc["sseq"], TC), fa(LFc["sseq"], TC),
                     fa(LFc["cbr"] + np_ * TC, TC), inc=sDVE)
                kb.v("tensor_add", fa(LFc["yac"] + dt * TC, TC),
                     fa(LFc["yac"] + dt * TC, TC), fa(LFc["sseq"], TC),
                     inc=sDVE)
            pool_done[(ch, n)] = kb.val(sCast)
            dve_done[(ch, n)] = kb.val(sDVE)
        kb.wait("sync", sACT)
        kb.wait("sync", sDVE)
        for dt in range(4):
            kb.dma("sync", fa(LFc["zc"] + dt * TC, TC),
                   zf[dt * 128:(dt + 1) * 128, ch * TC:(ch + 1) * TC], sLd)
        kb.wait("sync", sLd)
        kb.wait("scalar", sLd)
        kb.wait("scalar", sDVE)
        kb.wait("vector", sPE)
        for dt in range(4):
            ysl = fa(LFc["yac"] + dt * TC, TC)
            zsl = fa(LFc["zc"] + dt * TC, TC)
            kb.stt(ysl, ucb(dt), dco[:, dt:dt + 1], ysl,
                   ALU.mult, ALU.add, inc=sDVE)
            kb.wait("scalar", sDVE)
            kb.act(zsl, zsl, AF.Silu, inc=sACT)
            kb.wait("vector", sACT)
            kb.v("tensor_mul", ysl, ysl, zsl, inc=sDVE)
            kb.v("tensor_copy", ba(LBc["y2b"] + dt * TC, TC), ysl, inc=sDVE)
        kb.wait("tensor", sDVE)
        kb.wait("tensor", sACT)
        for tt in range(8):
            for coh in range(2):
                pp = pC[:, coh * 512:coh * 512 + 512]
                for dt in range(4):
                    kb.mm(pp, ba(LBc["y2b"] + dt * TC + tt * 128, 128),
                          ba(LBc["wo"] + dt * D + coh * 512, 512),
                          dt == 0, dt == 3,
                          inc=sPE if dt == 3 else None)
            kb.wait("vector", sPE)
            kb.wait("vector", sSt, kb.val(sSt))
            kb.v("tensor_copy", fa(LFc["pst"], 512), pC[:, 0:512], inc=sDVE)
            kb.v("tensor_copy", fa(LFc["pst"] + 512, 512), pC[:, 512:1024],
                 inc=sDVE)
            kb.wait("tensor", sDVE)
            kb.wait("scalar", sDVE)
            kb.dma("scalar",
                   pocc_i[ch * TC + tt * 128:ch * TC + (tt + 1) * 128, :],
                   fa(LFc["pst"], TC), sSt)
        if ch == 3:
            kb.wait("gpsimd", sSt)
            kb.cc("ReduceScatter", ALU.add, pocc_i[0:4 * TC, :], pors1,
                  sC[4])
    kb.wait("gpsimd", sSt)
    kb.cc("ReduceScatter", ALU.add, pocc_i[4 * TC:8 * TC, :], pors2, sC[7])
    kb.fence()

    kb.phase("P7_lns")
    LF7 = lay(FA_COLS, xr=16 * D, gbc=2 * D, pl=2 * D, sq=D, s1=512, s2=512,
              xt=D)
    LB7 = lay(BA_COLS, x3h=8 * 1024, stage=D)
    lr = FA[0:1, LF7["pl"]:LF7["pl"] + 2 * D]
    kb.dma("sync", lr[:, 0:D], lng, sLd)
    kb.dma("sync", lr[:, D:2 * D], lnb, sLd)
    kb.dma("sync", sc_sb[:, 0:3], scol, sLd)
    kb.wait("sync", sLd)
    kb.wait("vector", sLd)
    kb.v("tensor_scalar_mul", sc_sb[:, 0:2], sc_sb[:, 0:2], 2.0, inc=sDVE)
    kb.wait("tensor", sLd)
    kb.wait("tensor", sDVE)
    for j in range(4):
        kb.mm(pR[:], ones1[:, :], lr[:, j * 512:(j + 1) * 512],
              True, True, inc=sPE)
        kb.wait("vector", sPE)
        kb.v("tensor_copy", fa(LF7["gbc"] + j * 512, 512), pR[:], inc=sDVE)
        kb.wait("tensor", sDVE)

    for i in range(16):
        kb.wait("sync", sLd)
        kb.wait("sync", sDVE)
        psrc = (pors1[i * 128:(i + 1) * 128, :] if i < 8 else
                pors2[(i - 8) * 128:(i - 7) * 128, :])
        pw = [(sC[4], 1)] if i == 0 else ([(sC[7], 1)] if i == 8 else [])
        kb.dma("sync", fa(LF7["pl"], D), psrc, sLd, waits=pw)
        kb.dma("sync", fa(LF7["xr"] + i * D, D), x2f[i * 128:(i + 1) * 128, :],
               sLd)
        kb.wait("sync", sLd)
        kb.wait("vector", sLd)
        m = fa(LF7["pl"], D)
        kb.v("tensor_add", m, m, fa(LF7["xr"] + i * D, D), inc=sDVE)
        emit_layernorm_tm(kb, lambda _i: m, fa(LF7["sq"], D), cols, 1, D, sems)
        kb.wait("vector", sACT)
        kb.v("tensor_mul", m, m, fa(LF7["gbc"], D), inc=sDVE)
        kb.v("tensor_add", m, m, fa(LF7["gbc"] + D, D), inc=sDVE)
        kb.v("tensor_add", m, fa(LF7["xr"] + i * D, D), m, inc=sDVE)
        emit_layernorm_tm(kb, lambda _i: m, fa(LF7["sq"], D), cols, 1, D, sems)
        kb.wait("gpsimd", sDVE)
        kb.wait("gpsimd", sACT)
        kb.wait("gpsimd", sSt)
        kb.dma("gpsimd", x3fh[i * 128:(i + 1) * 128, :], m, sSt)
        kb.wait("vector", sACT)
        kb.v("tensor_copy", fa(LF7["xr"] + i * D, D), m, inc=sDVE)
        kb.v("tensor_copy", ba(LB7["stage"], D), m, inc=sDVE)
        kb.wait("tensor", sDVE)
        for ci in range(8):
            kb.trans(pT[:, 0:128], ba(LB7["stage"] + ci * 128, 128), idb[:],
                     inc=sPE)
            kb.wait("vector", sPE)
            kb.v("tensor_copy",
                 ba(LB7["x3h"] + ci * 1024 + (i % 8) * 128, 128), pT[:, 0:128],
                 inc=sDVE)
            kb.wait("tensor", sDVE)
        if i % 8 == 7:
            half = i // 8
            kb.wait("gpsimd", sDVE)
            kb.wait("gpsimd", sSt)
            for ci in range(8):
                kb.dma("gpsimd", x3bh[ci * 128:(ci + 1) * 128,
                                      half * 1024:(half + 1) * 1024],
                       ba(LB7["x3h"] + ci * 1024, 1024), sSt)
            kb.wait("gpsimd", sSt)
            kb.wait("vector", sSt)
    kb.fence()

    kb.phase("P8_ffn")
    LB8 = lay(BA_COLS, h2=32 * 1024, wst=13312)
    F8_h1 = 8192
    F8_x3 = 40960
    LF8 = lay(4096, sq=1024, s1=512, s2=512, xt=1024)
    h1 = lambda o, n: FAB[:, F8_h1 + o:F8_h1 + o + n]
    x3c = lambda o, n: FAB[:, F8_x3 + o:F8_x3 + o + n]

    def build_w(src_u8, dst, rows, ncols, second=None):
        for r in range(rows // 128):
            kb.wait("gpsimd", sSt)
            kb.wait("gpsimd", sCast)
            kb.dma("gpsimd", ba(LB8["wst"], ncols),
                   src_u8[r * 128:(r + 1) * 128, :], sSt)
            if second is not None:
                kb.dma("gpsimd", ba(LB8["h2"], ncols),
                       second[r * 128:(r + 1) * 128, :], sSt)
            kb.wait("gpsimd", sSt)
            if second is None:
                kb.g("tensor_scalar", ba(LB8["wst"], ncols),
                     ba(LB8["wst"], ncols), 2.0, -1.0, ALU.mult, ALU.add,
                     inc=sCast)
            else:
                kb.g("tensor_sub", ba(LB8["wst"], ncols), ba(LB8["wst"], ncols),
                     ba(LB8["h2"], ncols), inc=sCast)
                kb.g("tensor_scalar_mul", ba(LB8["wst"], ncols),
                     ba(LB8["wst"], ncols), 2.0, inc=sCast)
            kb.wait("gpsimd", sCast)
            kb.dma("gpsimd", dst[r * 128:(r + 1) * 128, :],
                   ba(LB8["wst"], ncols), sSt)

    kb.phase("P8a_buildw")
    build_w(k0t, w0dh, D, 4096, second=k0b)
    build_w(k1, w1dh, 4096, 4096)
    build_w(k2, w2dh, 4096, D)
    kb.phase("P8_ffn")
    kb.fence()

    HT = 1024
    psl = [pA[:], pB[:], pC[:, 0:512], pC[:, 512:1024], pKV[:], pR[:]]

    for half in range(2):
        h0 = half * HT
        kb.wait("sync", sSt, kb.val(sSt))
        kb.wait("sync", sPE)
        for ci in range(8):
            kb.dma("sync", x3c(ci * HT, HT),
                   x3bh[ci * 128:(ci + 1) * 128, h0:h0 + HT], sLd)
        kb.wait("sync", sLd)
        kb.wait("tensor", sLd)

        co = 0
        while co < 32:
            ng = min(3, 32 - co)
            kb.wait("sync", sPE)
            kb.wait("sync", sLd)
            for ci in range(8):
                kb.dma("sync", ba(LB8["wst"] + ci * ng * 128, ng * 128),
                       w0dh[ci * 128:(ci + 1) * 128, co * 128:(co + ng) * 128],
                       sLd)
            kb.wait("sync", sLd)
            kb.wait("tensor", sLd)
            for ci in range(8):
                for g in range(ng):
                    for t2 in range(2):
                        kb.mm(psl[g * 2 + t2],
                              ba(LB8["wst"] + ci * ng * 128 + g * 128, 128),
                              x3c(ci * HT + t2 * 512, 512),
                              ci == 0, ci == 7,
                              inc=sPE if (ci == 7 and g == ng - 1 and t2 == 1)
                              else None)
            kb.wait("scalar", sPE)
            kb.wait("scalar", sDVE)
            for g in range(ng):
                for t2 in range(2):
                    kb.wait("scalar", sDVE)
                    kb.act(FA[:, LF8["s1"]:LF8["s1"] + 512],
                           psl[g * 2 + t2],
                           AF.Relu, scale=sc_sb[:, 0:1], inc=sACT)
                    kb.wait("vector", sACT)
                    kb.v("tensor_copy",
                         h1((co + g) * HT + t2 * 512, 512),
                         FA[:, LF8["s1"]:LF8["s1"] + 512], inc=sDVE)
            kb.wait("tensor", sDVE)
            co += ng

        co = 0
        while co < 32:
            ng = min(3, 32 - co)
            kb.wait("sync", sPE)
            kb.wait("sync", sLd)
            for ci in range(32):
                kb.dma("sync",
                       ba(LB8["wst"] + ci * ng * 128, ng * 128),
                       w1dh[ci * 128:(ci + 1) * 128, co * 128:(co + ng) * 128],
                       sLd)
            kb.wait("sync", sLd)
            kb.wait("tensor", sLd)
            for ci in range(32):
                for g in range(ng):
                    for t2 in range(2):
                        kb.mm(psl[g * 2 + t2],
                              ba(LB8["wst"] + ci * ng * 128 + g * 128, 128),
                              h1(ci * HT + t2 * 512, 512),
                              ci == 0, ci == 31,
                              inc=sPE if (ci == 31 and g == ng - 1 and t2 == 1)
                              else None)
            kb.wait("scalar", sPE)
            kb.wait("scalar", sDVE)
            for g in range(ng):
                for t2 in range(2):
                    kb.wait("scalar", sDVE)
                    kb.act(FA[:, LF8["s1"]:LF8["s1"] + 512],
                           psl[g * 2 + t2],
                           AF.Relu, scale=sc_sb[:, 1:2], inc=sACT)
                    kb.wait("vector", sACT)
                    kb.v("tensor_copy",
                         ba(LB8["h2"] + (co + g) * HT + t2 * 512, 512),
                         FA[:, LF8["s1"]:LF8["s1"] + 512], inc=sDVE)
            kb.wait("tensor", sDVE)
            co += ng

        for ttg in range(0, 8, 3):
            ntt = min(3, 8 - ttg)
            for cig in range(0, 32, 11):
                nci = min(11, 32 - cig)
                kb.wait("sync", sPE)
                kb.wait("sync", sLd)
                for cl in range(nci):
                    kb.dma("sync", ba(LB8["wst"] + cl * D, D),
                           w2dh[(cig + cl) * 128:(cig + cl + 1) * 128, :], sLd)
                kb.wait("sync", sLd)
                kb.wait("tensor", sLd)
                for cl in range(nci):
                    ci = cig + cl
                    for tt in range(ntt):
                        for coh in range(2):
                            kb.mm(psl[tt * 2 + coh],
                                  ba(LB8["h2"] + ci * HT + (ttg + tt) * 128, 128),
                                  ba(LB8["wst"] + cl * D + coh * 512, 512),
                                  ci == 0, ci == 31,
                                  inc=sPE if (cl == nci - 1 and tt == ntt - 1
                                              and coh == 1) else None)
            kb.wait("scalar", sPE)
            for tt in range(ntt):
                gi = half * 8 + ttg + tt
                kb.wait("sync", sDVE)
                kb.wait("sync", sSt, kb.val(sSt))
                kb.dma("sync", FA[:, LF8["xt"]:LF8["xt"] + D],
                       x3fh[gi * 128:(gi + 1) * 128, :], sLd)
                kb.wait("sync", sLd)
                kb.wait("scalar", sDVE)
                for coh in range(2):
                    kb.act(FA[:, (LF8["s1"] if coh == 0 else LF8["s2"]):
                              (LF8["s1"] if coh == 0 else LF8["s2"]) + 512],
                           psl[tt * 2 + coh],
                           AF.Identity, scale=sc_sb[:, 2:3], inc=sACT)
                kb.wait("vector", sACT)
                kb.wait("vector", sLd)
                kb.v("tensor_add", FA[:, LF8["xt"]:LF8["xt"] + 512],
                     FA[:, LF8["xt"]:LF8["xt"] + 512],
                     FA[:, LF8["s1"]:LF8["s1"] + 512], inc=sDVE)
                kb.op("vector", lambda e, o=FA[:, LF8["xt"] + 512:LF8["xt"] + 1024],
                      a=FA[:, LF8["xt"] + 512:LF8["xt"] + 1024],
                      b=FA[:, LF8["s2"]:LF8["s2"] + 512]:
                      e.tensor_add(o, a, b), inc=sDVE)
                emit_layernorm_tm(kb, lambda _i: FA[:, LF8["xt"]:LF8["xt"] + D],
                                  FA[:, LF8["sq"]:LF8["sq"] + D], cols, 1, D,
                                  sems)
                kb.wait("gpsimd", sACT)
                kb.wait("gpsimd", sSt)
                kb.dma("gpsimd", xo[gi * 128:(gi + 1) * 128, :],
                       FA[:, LF8["xt"]:LF8["xt"] + D], sSt)
            kb.wait("tensor", sDVE)
            kb.wait("tensor", sACT)
    kb.op("gpsimd", lambda e: e.memset(cols[0:1, 0:1], 0.0),
          waits=[(sSt, kb.val(sSt))], selfwait=False)

    kb.emit()
    return kb


def build_moe_g(kb, drams, mem, psums, sems, consts):
    (gwT, eb_r, w13br_t, w2br_t, xsort, osort, x2f) = drams
    WX = 8 * HID
    FA, BA, FAB, LF = mem
    pA, pB, pC, pKV, pR, pTF, pT = psums
    sLd, sCast, sPE, sDVE, sACT, sSt, sC = sems
    (ones1, idf, idb, cols, ltri, iot, sidxu, rsel, gsc, kcon,
     sel4) = consts
    fa = lambda o, n: FA[:, o:o + n]

    GS = LF["gsm"]
    GB, BBC, SQ2, GEM, XR = (LF["gb"], LF["bbc"], LF["sq"], LF["gem"],
                             LF["xr"])
    gcol = FA[:, GS:GS + 8]
    key = FA[:, GS + 8:GS + 16]
    msk = FA[:, GS + 16:GS + 24]
    ebb = FA[:, GS + 24:GS + 32]
    neg = FA[:, GS + 32:GS + 40]
    gtm = FA[:, GS + 40:GS + 48]
    ebr = FA[0:1, GS + 48:GS + 56]

    gw_f = FA[:, GS + 64:GS + 64 + 64]
    kb.wait("sync", sLd)
    kb.dma("sync", ebr, eb_r, sLd)
    for ci in range(8):
        kb.dma("sync", FA[:, GS + 64 + ci * NE:GS + 64 + (ci + 1) * NE],
               gwT[ci * 128:(ci + 1) * 128, :], sLd)
    kb.wait("sync", sLd)
    kb.v("memset", neg[:], -1e30, inc=sDVE)
    kb.wait("tensor", sLd)
    kb.wait("tensor", sDVE)
    kb.mm(pR[:, 0:8], ones1[:, :], ebr, True, True, inc=sPE)
    kb.wait("vector", sPE)
    kb.v("tensor_copy", ebb[:], pR[:, 0:8], inc=sDVE)
    kb.wait("tensor", sDVE)
    for tt in range(16):
        for ci in range(8):
            kb.trans(pTF[:], FA[:, XR + tt * D + ci * 128:XR + tt * D + ci * 128 + 128],
                     idf[:], inc=sPE)
            kb.wait("vector", sPE)
            kb.op("vector", lambda e, o=FA[:, LF["x1cf"] + ci * 128:LF["x1cf"] + (ci + 1) * 128],
                  p=pTF[:]: e.tensor_copy(o, p), inc=sDVE)
            kb.wait("tensor", sDVE)
        for ci in range(8):
            kb.mm(pR[:, 0:8],
                  FA[:, LF["x1cf"] + ci * 128:LF["x1cf"] + (ci + 1) * 128],
                  gw_f[:, ci * NE:(ci + 1) * NE],
                  ci == 0, ci == 7, inc=sPE if ci == 7 else None)
        kb.wait("vector", sPE)
        kb.op("vector", lambda e, o=gcol[:, 0:1], p=pR[:, 0:8]:
              e.reduce_max(o, p, AX.X), inc=sDVE)
        kb.v("tensor_scalar_mul", gcol[:, 1:2], gcol[:, 0:1], -1.0, inc=sDVE)
        kb.wait("scalar", sDVE)
        kb.op("scalar", lambda e, o=gtm[:], p=pR[:, 0:8], b=gcol[:, 1:2]:
              e.activation(o, p, AF.Exp, bias=b), inc=sACT)
        kb.wait("vector", sACT)
        kb.wait("tensor", sACT)
        kb.op("vector", lambda e, o=gcol[:, 2:3], i=gtm[:]:
              e.reduce_sum(o, i, AX.X), inc=sDVE)
        kb.v("reciprocal", gcol[:, 3:4], gcol[:, 2:3], inc=sDVE)
        kb.op("vector", lambda e, o=gtm[:], i=gtm[:], m=gcol[:, 3:4]:
              e.tensor_scalar_mul(o, i, m), inc=sDVE)
        kb.v("tensor_add", key[:], gtm[:], ebb[:], inc=sDVE)
        kb.op("vector", lambda e, o=gcol[:, 4:5], i=key[:]:
              e.reduce_max(o, i, AX.X), inc=sDVE)
        kb.op("vector", lambda e, o=msk[:], i=key[:], m=gcol[:, 4:5]:
              e.tensor_single_scalar(o, i, m, ALU.is_equal), inc=sDVE)
        kb.v("tensor_scalar_mul", msk[:], msk[:], -1e30, inc=sDVE)
        kb.v("tensor_add", msk[:], msk[:], key[:], inc=sDVE)
        kb.op("vector", lambda e, o=gcol[:, 5:6], i=msk[:]:
              e.reduce_max(o, i, AX.X), inc=sDVE)
        kb.op("vector", lambda e, o=key[:], i=key[:], m=gcol[:, 5:6]:
              e.tensor_single_scalar(o, i, m, ALU.is_ge), inc=sDVE)
        kb.v("tensor_mul", gtm[:], gtm[:], key[:], inc=sDVE)
        kb.wait("tensor", sDVE)
        kb.trans(pTF[0:8, :], gtm[:], idf[:], inc=sPE)
        kb.wait("vector", sPE)
        kb.op("vector",
              lambda e, o=FA[0:8, GEM + tt * 128:GEM + (tt + 1) * 128],
              p=pTF[0:8, :]: e.tensor_copy(o, p), inc=sDVE)
        kb.wait("tensor", sDVE)

    if CUT == 1:
        return
    kb.wait("vector", sACT)
    kb.v("memset", BA[:, 0:1024], 0.0, inc=sDVE)
    kb.wait("scalar", sDVE)
    for j in range(48):
        kb.dma("scalar", xsort[j * 128:(j + 1) * 128, :], BA[:, 0:1024], sSt)
    kb.wait("scalar", sACT)
    for i in range(16):
        kb.dma("scalar", x2f[i * 128:(i + 1) * 128, :],
               fa(XR + i * D, D), sSt)
    spill_done = kb.val(sSt)

    HT2 = 1024
    cnt = FA[0:8, GS:GS + 1]
    pc_ = FA[0:8, GS + 1:GS + 2]
    off = FA[0:8, GS + 2:GS + 3]
    om1 = FA[0:8, GS + 3:GS + 4]
    o81 = FA[0:8, GS + 4:GS + 5]
    carry = FA[0:8, GS + 5:GS + 6]
    kb.v("memset", o81, 1.0, inc=sDVE)
    kb.v("memset", FA[0:8, GB + HT2:GB + 2 * HT2], 1.0, inc=sDVE)
    for h in range(2):
        gemH = FA[0:8, GEM + h * HT2:GEM + (h + 1) * HT2]
        posH = FA[0:8, BBC + h * HT2:BBC + (h + 1) * HT2]
        mH = FA[0:8, GB:GB + HT2]
        kb.v("tensor_single_scalar", mH, gemH, 0.0, ALU.is_gt, inc=sDVE)
        init = 0.0 if h == 0 else carry
        kb.op("vector", lambda e, o=posH, a=FA[0:8, GB + HT2:GB + 2 * HT2],
              b=mH, i=init:
              e.tensor_tensor_scan(o, a, b, i, ALU.mult, ALU.add), inc=sDVE)
        if h == 0:
            kb.v("tensor_copy", carry, FA[0:8, BBC + HT2 - 1:BBC + HT2],
                 inc=sDVE)
    kb.v("tensor_copy", cnt, FA[0:8, BBC + 2 * HT2 - 1:BBC + 2 * HT2],
         inc=sDVE)
    kb.v("tensor_single_scalar", FA[0:8, GS + 8:GS + 16], iot[0:8, 0:8],
         cnt, ALU.is_lt, inc=sDVE)
    kb.v("reduce_sum", pc_, FA[0:8, GS + 8:GS + 16], AX.X, inc=sDVE)
    kb.v("tensor_scalar_mul", pc_, pc_, 256.0, inc=sDVE)
    kb.wait("tensor", sDVE)
    kb.wait("tensor", sACT)
    kb.mm(pR[0:8, 0:1], ltri[0:8, 0:8], pc_, True, True, inc=sPE)
    kb.wait("vector", sPE)
    kb.v("tensor_copy", off, pR[0:8, 0:1], inc=sDVE)
    kb.v("tensor_scalar_sub", om1, off, 1.0, inc=sDVE)
    kb.v("tensor_single_scalar", FA[0:8, BBC:BBC + 2 * HT2],
         FA[0:8, BBC:BBC + 2 * HT2], om1, ALU.add, inc=sDVE)
    kb.v("tensor_single_scalar", FA[0:8, GS + 8:GS + 32], iot[0:8, 16:40],
         off, ALU.is_ge, inc=sDVE)
    kb.wait("tensor", sDVE)
    kb.mm(pR[0:1, 0:24], o81, FA[0:8, GS + 8:GS + 32], True, True, inc=sPE)
    kb.wait("vector", sPE)
    kb.v("tensor_copy", FA[0:1, GS + 48:GS + 72], pR[0:1, 0:24], inc=sDVE)
    kb.v("tensor_scalar_sub", FA[0:1, GS + 48:GS + 72],
         FA[0:1, GS + 48:GS + 72], 1.0, inc=sDVE)
    kb.v("memset", FA[0:1, GS + 88:GS + 89], -1.0, inc=sDVE)
    kb.v("tensor_copy", FA[0:1, GS + 89:GS + 112], FA[0:1, GS + 48:GS + 71],
         inc=sDVE)
    kb.op("vector", lambda e, o=FA[0:1, GS + 88:GS + 112],
          a=FA[0:1, GS + 48:GS + 72], b=FA[0:1, GS + 88:GS + 112]:
          e.tensor_tensor(o, a, b, ALU.is_equal), inc=sDVE)
    kb.v("tensor_scalar_mul", FA[0:1, GS + 8:GS + 32],
         FA[0:1, GS + 48:GS + 72], float(128 * 2 * WX), inc=sDVE)
    kb.stt(FA[0:1, GS + 8:GS + 32], FA[0:1, GS + 88:GS + 112], 67108864.0,
           FA[0:1, GS + 8:GS + 32], ALU.mult, ALU.add, inc=sDVE)
    kb.v("tensor_copy", rsel[0:1, 0:24], FA[0:1, GS + 8:GS + 32], inc=sDVE)
    kb.v("tensor_scalar_mul", FA[0:1, GS + 8:GS + 32],
         FA[0:1, GS + 48:GS + 72], float(128 * WX), inc=sDVE)
    kb.stt(FA[0:1, GS + 8:GS + 32], FA[0:1, GS + 88:GS + 112], 33554432.0,
           FA[0:1, GS + 8:GS + 32], ALU.mult, ALU.add, inc=sDVE)
    kb.v("tensor_copy", rsel[0:1, 24:48], FA[0:1, GS + 8:GS + 32],
         inc=sDVE)

    for h in range(2):
        gemH = FA[0:8, GEM + h * HT2:GEM + (h + 1) * HT2]
        pposH = FA[0:8, BBC + h * HT2:BBC + (h + 1) * HT2]
        mH = FA[0:8, GB:GB + HT2]
        cumH = FA[0:8, GB + HT2:GB + 2 * HT2]
        t1H = FA[0:8, SQ2:SQ2 + HT2]
        kb.wait("vector", sPE)
        kb.v("tensor_single_scalar", mH, gemH, 0.0, ALU.is_gt, inc=sDVE)
        for q in range(2):
            kb.wait("tensor", sDVE)
            kb.mm(pR[0:8, 0:512], ltri[0:8, 8:16],
                  FA[0:8, GB + q * 512:GB + (q + 1) * 512], True, True,
                  inc=sPE)
            kb.wait("vector", sPE)
            kb.v("tensor_copy",
                 FA[0:8, GB + HT2 + q * 512:GB + HT2 + (q + 1) * 512],
                 pR[0:8, 0:512], inc=sDVE)
        kb.v("tensor_single_scalar", t1H, cumH, kcon[0:8, 4:5],
             ALU.is_equal, inc=sDVE)
        kb.v("tensor_mul", t1H, t1H, mH, inc=sDVE)
        kb.v("tensor_single_scalar", cumH, cumH, kcon[0:8, 3:4],
             ALU.is_equal, inc=sDVE)
        kb.v("tensor_mul", cumH, cumH, mH, inc=sDVE)
        for r, (a_, b_) in enumerate(((cumH, pposH), (t1H, pposH),
                                      (cumH, gemH), (t1H, gemH))):
            kb.wait("vector", sPE)
            kb.op("vector", lambda e, o=mH, a=a_, b=b_:
                  e.tensor_mul(o, a, b), inc=sDVE)
            kb.wait("tensor", sDVE)
            for q in range(2):
                kb.mm(pC[:, q * 512:(q + 1) * 512],
                      sel4[0:8, r * 128:(r + 1) * 128],
                      FA[0:8, GB + q * 512:GB + (q + 1) * 512],
                      r == 0, r == 3, inc=sPE if q == 1 else None)
        kb.wait("vector", sPE)
        kb.v("tensor_copy", FA[:, GEM + 2 * HT2:GEM + 3 * HT2], pC[:, 0:1024],
             inc=sDVE)
        for kk in range(8):
            kcoli = h * 8 + kk
            kb.wait("tensor", sDVE)
            kb.trans(pTF[:], FA[:, GEM + 2 * HT2 + kk * 128:GEM + 2 * HT2 + (kk + 1) * 128],
                     idf[:], inc=sPE)
            kb.wait("vector", sPE)
            kb.v("tensor_copy", sidxu[:, kcoli:kcoli + 1], pTF[:, 0:1],
                 inc=sDVE)
            kb.v("tensor_copy", sidxu[:, 16 + kcoli:17 + kcoli],
                 pTF[:, 32:33], inc=sDVE)
            kb.v("tensor_copy", gsc[:, kcoli:kcoli + 1], pTF[:, 64:65],
                 inc=sDVE)
            kb.v("tensor_copy", gsc[:, 16 + kcoli:17 + kcoli], pTF[:, 96:97],
                 inc=sDVE)
        kb.wait("tensor", sDVE)

    kb.wait("gpsimd", sSt, spill_done)
    kb.wait("gpsimd", sDVE)
    wreg0 = {}
    for k in range(16):
        kb.wait("scalar", sSt, kb.val(sSt))
        kb.wait("scalar", sDVE)
        kb.act(BA[:, 0:1024], fa(XR + k * D, D), AF.Identity,
               scale=gsc[:, k:k + 1], inc=sACT)
        kb.act(BA[:, 1024:2048], fa(XR + k * D, D), AF.Identity,
               scale=gsc[:, 16 + k:17 + k], inc=sACT)
        kb.wait("gpsimd", sACT)
        for (six0, src) in ((k, BA[:, 0:1024]), (16 + k, BA[:, 1024:2048])):
            def scat_x(e, c=six0, s=src):
                if "x" not in wreg0:
                    wreg0["x"] = e.register("xbnd").__enter__()
                    e.reg_mov(wreg0["x"], 24 * 256 - 1)
                e.indirect_dma_start(
                    out=xsort,
                    out_offset=bass.IndirectOffsetOnAxis(
                        ap=sidxu[:, c:c + 1], axis=0),
                    in_=s, in_offset=None, bounds_check=wreg0["x"],
                    oob_is_err=False).then_inc(sSt, 16)
            kb.op_raw("gpsimd", scat_x, incs={sSt: 16})

    if CUT == 2:
        return
    if DBG:
        dbg_sidxu = kb.nc.dram_tensor("dbg_sidxu", [128, 32],
                                      mybir.dt.uint32,
                                      kind="ExternalOutput").ap()
        dbg_gsc = kb.nc.dram_tensor("dbg_gsc", [128, 32], F32,
                                    kind="ExternalOutput").ap()
        dbg_rsel = kb.nc.dram_tensor("dbg_rsel", [1, 48], mybir.dt.int32,
                                     kind="ExternalOutput").ap()
        dbg_cols = kb.nc.dram_tensor("dbg_cols", [8, 8], F32,
                                     kind="ExternalOutput").ap()
        dbg_gem = kb.nc.dram_tensor("dbg_gem", [8, T], F32,
                                    kind="ExternalOutput").ap()
        dbg_xs = kb.nc.dram_tensor("dbg_xs", [24 * 256, D], BF16,
                                   kind="ExternalOutput").ap()
        dbg_x1 = kb.nc.dram_tensor("dbg_x1", [T, D], F32,
                                   kind="ExternalOutput").ap()
        dbg_os = kb.nc.dram_tensor("dbg_os", [24 * 256, D], F32,
                                   kind="ExternalOutput").ap()
        dbg_x2 = kb.nc.dram_tensor("dbg_x2", [T, D], F32,
                                   kind="ExternalOutput").ap()
        kb.wait("scalar", sDVE)
        kb.wait("scalar", sSt, kb.val(sSt))
        kb.dma("scalar", dbg_sidxu[:, 0:32], sidxu[:, 0:32], sSt)
        kb.dma("scalar", dbg_gsc, gsc[:], sSt)
        kb.dma("scalar", dbg_rsel, rsel[0:1, :], sSt)
        kb.dma("scalar", dbg_cols, FA[0:8, GS:GS + 8], sSt)
        kb.dma("scalar", dbg_gem, FA[0:8, GEM:GEM + T], sSt)
        kb.wait("scalar", sSt)

    FB_xt, FB_xs, FB_h, FB_hT = 22528, 23552, 25600, 28416
    OST = 16384
    S1F = LF["s1"]
    HCC = ((0, 512), (512, 1024), (1024, 1536), (1536, 2048), (2048, 2560),
           (2560, 2816))
    mm_prev = kb.val(sPE)
    if DBG:
        kb.wait("scalar", sSt, kb.val(sSt))
        for jj in range(48):
            kb.dma("scalar", dbg_xs[jj * 128:(jj + 1) * 128, :],
                   xsort[jj * 128:(jj + 1) * 128, :], sSt)
        kb.dma("scalar", dbg_x1, x2f, sSt)
        kb.wait("scalar", sSt)
    silu_done = {}
    xtr_done = [0, 0]
    wreg = {}
    pat13 = [[2 * WX, 128], [WX, 2], [1, WX]]
    pat2 = [[WX, 128], [1, WX]]
    for j in range(24):
        par = j % 2
        if j == 0:
            kb.wait("sync", sSt, kb.val(sSt))
            kb.wait("sync", sACT)
            kb.wait("sync", sDVE)
            kb.wait("sync", sC[5], 1)
            kb.wait("sync", sC[6], 1)
        kb.wait("sync", sPE, mm_prev)

        def wload13(e, jj=j):
            if "s" not in wreg:
                wreg["s"] = e.register("wsel_s").__enter__()
            r = wreg["s"]
            e.reg_load(r, rsel[0:1, jj:jj + 1])
            e.dma_start(out=BA[:, 0:2 * WX], in_=bass.AP(w13br_t, r, pat13),
                        bounds_check="skip_entire_dma").then_inc(sLd, 16)
        if CUT != 3:
            kb.op_raw("sync", wload13, incs={sLd: 16})
        kb.wait("scalar", sPE, mm_prev)
        kb.wait("scalar", sDVE)

        def wload2(e, jj=j):
            if "a" not in wreg:
                wreg["a"] = e.register("wsel_a").__enter__()
            r = wreg["a"]
            e.reg_load(r, rsel[0:1, 24 + jj:25 + jj])
            e.dma_start(out=FAB[:, 0:22528], in_=bass.AP(w2br_t, r, pat2),
                        bounds_check="skip_entire_dma").then_inc(sLd, 16)
        if CUT != 3:
            kb.op_raw("scalar", wload2, incs={sLd: 16})
        wld_j = kb.val(sLd)

        for st_ in range(2):
            sp = (2 * j + st_) % 2
            kb.wait("sync", sPE, xtr_done[sp])
            kb.dma("sync", FAB[:, FB_xs + sp * 1024:FB_xs + (sp + 1) * 1024],
                   xsort[j * 256 + st_ * 128:j * 256 + (st_ + 1) * 128, :],
                   sLd)
            kb.wait("sync", sLd)
            ldx = kb.val(sLd)
            dvev = []
            kb.wait("tensor", sLd, ldx)
            tslots = [pA.bitcast(BF16), pB.bitcast(BF16),
                      pKV.bitcast(BF16), pR.bitcast(BF16)]
            kb.wait("tensor", sDVE)
            kb.wait("tensor", sACT)
            for ci in range(8):
                ph = ci % 4
                if ci >= 4:
                    kb.wait("tensor", sDVE, dvev[ci - 4])
                kb.trans(tslots[ph][:, 0:128],
                         FAB[:, FB_xs + sp * 1024 + ci * 128:FB_xs + sp * 1024 + (ci + 1) * 128],
                         idb[:], inc=sPE)
                kb.wait("vector", sPE)
                if ci == 0:
                    kb.wait("vector", sPE, mm_prev)
                kb.v("tensor_copy",
                     FAB[:, FB_xt + ci * 128:FB_xt + (ci + 1) * 128],
                     tslots[ph][:, 0:128], inc=sDVE)
                dvev.append(kb.val(sDVE))
            xtr_done[sp] = kb.val(sPE)

            kb.wait("tensor", sLd, wld_j)
            kb.wait("tensor", sDVE)
            for hci, (h0, h1) in enumerate(HCC):
                wdt = h1 - h0
                pw1 = (pA, pKV)[hci % 2]
                pw3 = (pB, pR)[hci % 2]
                if (j, st_, hci - 2) in silu_done:
                    kb.wait("tensor", sDVE, silu_done[(j, st_, hci - 2)])
                for ci in range(8):
                    kb.mm(pw1[:, 0:wdt],
                          FAB[:, FB_xt + ci * 128:FB_xt + (ci + 1) * 128],
                          BA[:, ci * HID + h0:ci * HID + h1],
                          ci == 0, ci == 7, inc=sPE if ci == 7 else None)
                for ci in range(8):
                    kb.mm(pw3[:, 0:wdt],
                          FAB[:, FB_xt + ci * 128:FB_xt + (ci + 1) * 128],
                          BA[:, WX + ci * HID + h0:WX + ci * HID + h1],
                          ci == 0, ci == 7, inc=sPE if ci == 7 else None)
                kb.wait("scalar", sPE)
                kb.wait("scalar", sDVE)
                kb.act(FA[:, S1F:S1F + wdt], pw1[:, 0:wdt], AF.Silu, inc=sACT)
                kb.wait("vector", sACT)
                kb.op("vector", lambda e, o=FAB[:, FB_h + h0:FB_h + h1],
                      a=FA[:, S1F:S1F + wdt], b=pw3[:, 0:wdt]:
                      e.tensor_mul(o, a, b), inc=sDVE)
                silu_done[(j, st_, hci)] = kb.val(sDVE)
            actv = []
            kb.wait("tensor", sDVE)
            kb.wait("tensor", sACT)
            for kkk in range(22):
                ph = kkk % 4
                if kkk >= 4:
                    kb.wait("tensor", sACT, actv[kkk - 4])
                kb.trans(tslots[ph][:, 0:128],
                         FAB[:, FB_h + kkk * 128:FB_h + (kkk + 1) * 128],
                         idb[:], inc=sPE)
                kb.wait("scalar", sPE)
                kb.act(FAB[:, FB_hT + kkk * 128:FB_hT + (kkk + 1) * 128],
                       tslots[ph][:, 0:128], AF.Identity, inc=sACT)
                actv.append(kb.val(sACT))
            kb.wait("tensor", sACT)
            kb.wait("tensor", sDVE)
            for co in range(2):
                for kkk in range(22):
                    kb.mm(pC[:, co * 512:(co + 1) * 512],
                          FAB[:, FB_hT + kkk * 128:FB_hT + (kkk + 1) * 128],
                          FAB[:, kkk * 1024 + co * 512:kkk * 1024 + co * 512 + 512],
                          kkk == 0, kkk == 21,
                          inc=sPE if kkk == 21 else None)
            kb.wait("vector", sPE)
            kb.wait("vector", sSt, kb.val(sSt))
            kb.v("tensor_copy", fa(OST, 512), pC[:, 0:512], inc=sDVE)
            kb.v("tensor_copy", fa(OST + 512, 512), pC[:, 512:1024], inc=sDVE)
            kb.wait("scalar", sDVE)
            kb.dma("scalar",
                   osort[j * 256 + st_ * 128:j * 256 + (st_ + 1) * 128, :],
                   fa(OST, 1024), sSt)
            mm_prev = kb.val(sPE)

    if DBG:
        kb.wait("scalar", sSt, kb.val(sSt))
        kb.dma("scalar", dbg_os, osort, sSt)
        kb.wait("scalar", sSt)
    kb.fence()
    for i in range(16):
        kb.wait("sync", sLd)
        kb.wait("sync", sDVE)
        kb.dma("sync", fa(XR + i * D, D), x2f[i * 128:(i + 1) * 128, :], sLd)
        kb.wait("gpsimd", sLd)
        kb.wait("gpsimd", sDVE)
        kb.op_raw("gpsimd",
                  lambda e, c=i:
                  e.indirect_dma_start(
                      out=FA[:, OST:OST + 1024], out_offset=None,
                      in_=osort,
                      in_offset=bass.IndirectOffsetOnAxis(
                          ap=sidxu[:, c:c + 1], axis=0)).then_inc(sLd, 16),
                  incs={sLd: 16})
        kb.op_raw("gpsimd",
                  lambda e, c=16 + i:
                  e.indirect_dma_start(
                      out=FA[:, OST + 1024:OST + 2048], out_offset=None,
                      in_=osort,
                      in_offset=bass.IndirectOffsetOnAxis(
                          ap=sidxu[:, c:c + 1], axis=0)).then_inc(sLd, 16),
                  incs={sLd: 16})
        kb.wait("vector", sLd)
        kb.v("tensor_add", fa(XR + i * D, D), fa(XR + i * D, D),
             fa(OST, 1024), inc=sDVE)
        kb.v("tensor_add", fa(XR + i * D, D), fa(XR + i * D, D),
             fa(OST + 1024, 1024), inc=sDVE)
        if DBG:
            kb.wait("gpsimd", sDVE)
            kb.dma("gpsimd", dbg_x2[i * 128:(i + 1) * 128, :],
                   fa(XR + i * D, D), sSt)
    kb.wait("vector", sDVE)



_CACHE = {}


def _get(name, builder):
    if name not in _CACHE:
        _CACHE[name] = builder().nc
    return _CACHE[name]


def _sel4():
    s = np.zeros((8, 512), np.float32)
    for r, rr in enumerate((0, 32, 64, 96)):
        s[:, r * 128 + rr] = 1.0
    return s


def _col4(v):
    return np.ascontiguousarray(np.asarray(v, np.float32).reshape(4, 128).T)


def _in_maps(inputs):
    x = np.asarray(inputs["L0_attn_x"])
    w_in = np.asarray(inputs["L1_attn_W_in"], np.float32)
    cw = np.asarray(inputs["L1_attn_conv_w"], np.float32)
    cb = np.asarray(inputs["L1_attn_conv_b"], np.float32)
    w_x = np.asarray(inputs["L1_attn_W_x"], np.float32)
    w_dt = np.asarray(inputs["L1_attn_W_dt"], np.float32)
    b_dt = np.asarray(inputs["L1_attn_b_dt"], np.float32)
    alog = np.asarray(inputs["L1_attn_A_log"], np.float32)
    dd = np.asarray(inputs["L1_attn_D"], np.float32)
    w_out = np.asarray(inputs["L1_attn_W_out"], np.float32)
    k0 = np.asarray(inputs["L1_ffn_k0"]).view(np.uint8)
    k1 = np.ascontiguousarray(np.asarray(inputs["L1_ffn_k1"]).view(np.uint8))
    k2 = np.ascontiguousarray(np.asarray(inputs["L1_ffn_k2"]).view(np.uint8))
    s0 = np.asarray(inputs["L1_ffn_s0"], np.float32).reshape(())
    s1 = np.asarray(inputs["L1_ffn_s1"], np.float32).reshape(())
    s2 = np.asarray(inputs["L1_ffn_s2"], np.float32).reshape(())
    scol = np.tile(np.stack([s0, s1, s2]).reshape(1, 3), (128, 1))
    base = {
        "wq": np.ascontiguousarray(inputs["L0_attn_Wq"], np.float32),
        "wk": np.ascontiguousarray(inputs["L0_attn_Wk"], np.float32),
        "wv": np.ascontiguousarray(inputs["L0_attn_Wv"], np.float32),
        "wp": np.ascontiguousarray(inputs["L0_attn_Wp"], np.float32),
        "bq_c": np.ascontiguousarray(
            np.asarray(inputs["L0_attn_bq"], np.float32).reshape(8, 128).T),
        "bk_r": np.asarray(inputs["L0_attn_bk"], np.float32).reshape(1, D),
        "bv_r": np.asarray(inputs["L0_attn_bv"], np.float32).reshape(1, D),
        "bp_r": np.asarray(inputs["L0_attn_bp"], np.float32).reshape(1, D),
        "id128b": np.eye(128, dtype=np.float32).astype(BF),
        "idf": np.eye(128, dtype=np.float32),
        "ckon": np.tile(np.array([list(CONSTS)], np.float32), (128, 1)),
        "ones_d": np.ones((1, 128), np.float32),
        "gwT": np.ascontiguousarray(
            np.asarray(inputs["L0_ffn_gate_w"], np.float32).T),
        "eb_r": np.asarray(
            inputs["L0_ffn_expert_bias"], np.float32).reshape(1, NE),
        "ltri2": np.concatenate(
            [np.triu(np.ones((8, 8), np.float32), 1),
             np.triu(np.ones((8, 8), np.float32), 0)], axis=1),
        "iotas": np.concatenate(
            [np.tile(np.arange(16, dtype=np.float32) * 256, (8, 1)),
             np.tile(np.arange(40, dtype=np.float32) * 256, (8, 1))], axis=1),

        "sel4_d": _sel4(),
        "lng": np.asarray(inputs["L1_attn_ln_g"], np.float32).reshape(1, D),
        "lnb": np.asarray(inputs["L1_attn_ln_b"], np.float32).reshape(1, D),
        "k0t": np.ascontiguousarray(k0[:D]),
        "k0b": np.ascontiguousarray(k0[D:]),
        "k1": k1, "k2": k2,
        "scol": np.ascontiguousarray(scol),
    }
    w1f = np.asarray(inputs["L0_ffn_w1"], np.float32)
    w2f = np.asarray(inputs["L0_ffn_w2"], np.float32)
    w3f = np.asarray(inputs["L0_ffn_w3"], np.float32)
    maps = []
    for c in range(NCORES):
        b, s = c // 4, c % 4
        sl = slice(s * 512, (s + 1) * 512)
        m = dict(base)
        m["w1own"] = np.ascontiguousarray(w1f[c])
        m["w2own"] = np.ascontiguousarray(w2f[c])
        m["w3own"] = np.ascontiguousarray(w3f[c])
        m["xk"] = np.ascontiguousarray(
            np.asarray(x[b, s * T:(s + 1) * T], dtype=np.float32))
        m["winx"] = np.ascontiguousarray(w_in[sl, :].T)
        m["winz"] = np.ascontiguousarray(
            w_in[2048 + s * 512:2048 + (s + 1) * 512, :].T)
        m["cw"] = np.ascontiguousarray(
            cw[sl, 0, :].reshape(4, 128, 4).transpose(1, 0, 2).reshape(128, 16))
        m["cb"] = _col4(cb[sl])
        wsl = np.concatenate([w_x[sl], w_x[2048:2080]], axis=0)
        m["wxT"] = np.ascontiguousarray(wsl.T)
        m["wdtT"] = np.ascontiguousarray(w_dt[sl, :].T)
        m["bdt_c"] = _col4(b_dt[sl])
        m["alog_c"] = np.ascontiguousarray(
            alog[sl, :].reshape(4, 128, 16).transpose(1, 0, 2).reshape(128, 64))
        m["dcol"] = _col4(dd[sl])
        m["woT"] = np.ascontiguousarray(w_out[:, sl].T)
        maps.append(m)
    return maps


def kernel(**inputs):
    nc = _get("fused", build_fused)
    res = run_bass_kernel_spmd(nc, _in_maps(inputs),
                               core_ids=list(range(NCORES))).results
    out = np.empty((2, TB, D), np.float32)
    for c in range(NCORES):
        b, s = c // 4, c % 4
        out[b, s * T:(s + 1) * T] = res[c]["xo"]
    return out
